# version 02
# baseline (speedup 1.0000x reference)
import numpy as np
import ml_dtypes
from contextlib import ExitStack

import concourse.bass as bass
import concourse.mybir as mybir
from concourse.bass_utils import run_bass_kernel_spmd

F32 = mybir.dt.float32
BF16 = mybir.dt.bfloat16
AF = mybir.ActivationFunctionType
ALU = mybir.AluOpType
BF = ml_dtypes.bfloat16

D, S, H, HD = 1024, 2048, 16, 64
DT = D // 128
ST = S // 128
NCORE = 8
EPS = 1e-5


def _c(a):
    return np.ascontiguousarray(a)


def host_blocks(WT):
    IN, OUT = WT.shape
    a = WT.reshape(IN // 128, 128, OUT // 128, 128).transpose(2, 1, 0, 3)
    return _c(a.reshape(OUT, IN))


def host_vec(v):
    return _c(np.asarray(v).reshape(-1, 128).T)


def host_pwide(blocks2d):
    nb = blocks2d.shape[0] // 128
    return _c(blocks2d.reshape(nb, 128, -1).transpose(1, 0, 2).reshape(128, -1))


ENG_ATTR = {"t": "tensor", "v": "vector", "s": "scalar", "g": "gpsimd", "y": "sync"}


class Plan:

    def __init__(self, nc, ctx):
        self.nc = nc
        self.ctx = ctx
        self.ops = []
        self.cnt = {}
        self.handles = {}

    def sem(self, name):
        if name not in self.cnt:
            self.cnt[name] = 0
            self.handles[name] = self.ctx.enter_context(self.nc.semaphore(name))
        return name

    def val(self, name):
        return self.cnt.get(name, 0)

    def lag(self, name, k):
        return (name, max(0, self.cnt.get(name, 0) - k))

    def emit(self, eng, fn, wait=(), inc=()):
        waits = []
        for w in wait:
            if isinstance(w, tuple):
                s, v = w
            else:
                s, v = w, self.cnt.get(w, 0)
            self.sem(s)
            if v > 0:
                waits.append((s, v))
        incs = []
        for s, k in inc:
            self.sem(s)
            self.cnt[s] += k
            incs.append((s, k))
        assert len(incs) <= 1, f"hw allows one sync update per instruction: {incs}"
        self.ops.append((eng, fn, waits, incs))

    def gate(self, engines, wait):
        for e in engines:
            self.emit(e, None, wait=wait)

    def flush(self, name=None):
        if not self.ops:
            return
        ops = self.ops
        self.ops = []
        with self.nc.allow_low_precision("bf16 matmul operands / bf16 psum transpose are deliberate"), \
             self.nc.Block() as blk:
            for engname, attr in ENG_ATTR.items():
                eops = [o for o in ops if o[0] == engname]
                if not eops:
                    continue

                def body(eng, eops=eops):
                    for _, fn, waits, incs in eops:
                        for s, v in waits:
                            eng.wait_ge(self.handles[s], v)
                        inst = fn(eng) if fn is not None else None
                        for s, k in incs:
                            if inst is None:
                                inst = eng.nop()
                            inst.then_inc(self.handles[s], k)

                getattr(blk, attr)(body)



def emit_rep(P, rep_ps, rep_ps2, mv, vv, m_rep, r_rep, ones_f32r, pfx):
    for c in range(4):
        cs = slice(c * 512, (c + 1) * 512)
        P.emit("t", lambda e, cs=cs: e.matmul(
            rep_ps[:, :], ones_f32r[0:1, :], mv[0:1, cs], start=True, stop=True),
            wait=["rstd", P.lag("mrv", 0)], inc=[("mrt", 1)])
        P.emit("v", lambda e, cs=cs: e.tensor_copy(m_rep[:, cs], rep_ps[:, :]),
               wait=["mrt"], inc=[("mrv", 1)])
        P.emit("t", lambda e, cs=cs: e.matmul(
            rep_ps2[:, :], ones_f32r[0:1, :], vv[0:1, cs], start=True, stop=True),
            wait=["rstd", P.lag("rrv", 0)], inc=[("mrt", 1)])
        P.emit("v", lambda e, cs=cs: e.tensor_copy(r_rep[:, cs], rep_ps2[:, :]),
               wait=["mrt"], inc=[("rrv", 1)])



def emit_A(nc, P, ctx, xin, xout, debug=False):
    xT = xin
    wq = nc.dram_tensor("A_wq", [D, D], F32, kind="ExternalInput").ap()
    wk = nc.dram_tensor("A_wk", [D, D], F32, kind="ExternalInput").ap()
    wv = nc.dram_tensor("A_wv", [D, D], F32, kind="ExternalInput").ap()
    wo = nc.dram_tensor("A_wo", [D, D], F32, kind="ExternalInput").ap()
    bqv = nc.dram_tensor("A_bqv", [128, DT], F32, kind="ExternalInput").ap()
    bkv = nc.dram_tensor("A_bkv", [128, DT], F32, kind="ExternalInput").ap()
    bvv = nc.dram_tensor("A_bvv", [128, DT], F32, kind="ExternalInput").ap()
    bov = nc.dram_tensor("A_bov", [128, DT], F32, kind="ExternalInput").ap()
    ri = nc.dram_tensor("A_ri", [D, 3], F32, kind="ExternalInput").ap()
    rj = nc.dram_tensor("A_rj", [D, 3], F32, kind="ExternalInput").ap()
    relb = nc.dram_tensor("A_relb", [3, 1], F32, kind="ExternalInput").ap()
    lbT = nc.dram_tensor("A_lbT", [3, H], F32, kind="ExternalInput").ap()
    gw = nc.dram_tensor("A_gw", [D, H], F32, kind="ExternalInput").ap()
    gbv = nc.dram_tensor("A_gbv", [H, 1], F32, kind="ExternalInput").ap()
    x2T = xout
    if debug:
        dbg_q = nc.dram_tensor("A_dbg_q", [D, S], F32, kind="ExternalOutput").ap()
        dbg_k = nc.dram_tensor("A_dbg_k", [D, S], F32, kind="ExternalOutput").ap()
        dbg_at = nc.dram_tensor("A_dbg_at", [D, S], F32, kind="ExternalOutput").ap()

    id128 = nc.inline_tensor(np.eye(128, dtype=BF), "A_id128")
    e16 = np.zeros((16, 48), dtype=BF)
    for h in range(16):
        e16[h, 3 * h:3 * h + 3] = 1.0
    e16sel = nc.inline_tensor(e16, "A_e16sel")

    with ExitStack() as bctx:
        E = bctx.enter_context

        bq_sb = E(nc.sbuf_tensor("A_bq_sb", [128, DT], F32))
        bk_sb = E(nc.sbuf_tensor("A_bk_sb", [128, DT], F32))
        bv_sb = E(nc.sbuf_tensor("A_bv_sb", [128, DT], F32))
        bo_sb = E(nc.sbuf_tensor("A_bo_sb", [128, DT], F32))
        relb_sb = E(nc.sbuf_tensor("A_relb_sb", [3, 1], F32))
        gb_sb = E(nc.sbuf_tensor("A_gb_sb", [H, 1], F32))
        lbT_sb = E(nc.sbuf_tensor("A_lbT_sb", [3, H], F32))
        id_sb = E(nc.sbuf_tensor("A_id_sb", [128, 128], BF16))
        e16_sb = E(nc.sbuf_tensor("A_e16_sb", [16, 48], BF16))
        ones_f32 = E(nc.sbuf_tensor("A_ones_f32", [128, 1], F32))
        ones_bf = E(nc.sbuf_tensor("A_ones_bf", [1, 128], BF16))
        eps_sb = E(nc.sbuf_tensor("A_eps_sb", [1, 1], F32))
        ones_f32r = E(nc.sbuf_tensor("A_ones_f32r", [1, 128], F32))
        aT = E(nc.sbuf_tensor("A_aT", [3, S], BF16))
        cT = E(nc.sbuf_tensor("A_cT", [3, S], BF16))
        gateT = E(nc.sbuf_tensor("A_gateT", [16, S], BF16))

        for nm, sb, ap_ in (("bq", bq_sb, bqv), ("bk", bk_sb, bkv), ("bv", bv_sb, bvv),
                            ("bo", bo_sb, bov), ("rb", relb_sb, relb),
                            ("gb", gb_sb, gbv), ("lb", lbT_sb, lbT)):
            P.emit("y", lambda e, sb=sb, a=ap_: e.dma_start(sb[:, :], a), inc=[("ldm", 16)])
        P.emit("g", lambda e: e.dma_start(id_sb[:, :], id128.ap()), inc=[("ldm", 16)])
        P.emit("g", lambda e: e.dma_start(e16_sb[:, :], e16sel.ap()), inc=[("ldm", 16)])
        LDM = 9 * 16
        P.emit("v", lambda e: e.memset(ones_f32[:, :], 1.0), inc=[("ldv", 1)])
        P.emit("v", lambda e: e.memset(ones_bf[:, :], 1.0), inc=[("ldv", 1)])
        P.emit("v", lambda e: e.memset(eps_sb[:, :], EPS), inc=[("eps", 1)])
        P.emit("v", lambda e: e.memset(ones_f32r[:, :], 1.0), inc=[("ldv", 1)])
        P.emit("v", lambda e: e.tensor_scalar(bk_sb[:, :], bk_sb[:, :], 0.125, None, ALU.mult),
               wait=["ldm"], inc=[("bk8", 1)])

        qT = [E(nc.sbuf_tensor(f"A_qT{t}", [128, S], BF16)) for t in range(DT)]
        kT = [E(nc.sbuf_tensor(f"A_kT{t}", [128, S], BF16)) for t in range(DT)]
        v_aug = [E(nc.sbuf_tensor(f"A_vaug{j}", [128, H * 65], BF16)) for j in range(ST)]

        cx = bctx.enter_context(ExitStack())
        x_ln = [cx.enter_context(nc.sbuf_tensor(f"A_x_ln{t}", [128, S], BF16))
                for t in range(DT)]

        with ExitStack() as c1:
            C = c1.enter_context
            xr = [C(nc.sbuf_tensor(f"A_xr{i}", [128, S], F32)) for i in range(2)]
            sq_sb = C(nc.sbuf_tensor("A_sq", [128, S], F32))
            mv = C(nc.sbuf_tensor("A_mv", [1, S], F32))
            vv = C(nc.sbuf_tensor("A_vv", [1, S], F32))
            m_rep = C(nc.sbuf_tensor("A_m_rep", [128, S], F32))
            r_rep = C(nc.sbuf_tensor("A_r_rep", [128, S], F32))

            with ExitStack() as c1s:
                stat_ps = c1s.enter_context(nc.psum_tensor("A_stat_ps", [1, S], F32))
                stat_ps2 = c1s.enter_context(nc.psum_tensor("A_stat_ps2", [1, S], F32))
                P.gate(["t"], ["ldv"])
                for t in range(DT):
                    P.emit("y", lambda e, t=t, sl=t % 2: e.dma_start(
                        xr[sl][:, :], xT[t * 128:(t + 1) * 128, :]),
                        wait=[P.lag("sxu", 0), P.lag("ssq", 0)], inc=[("ldx", 16)])
                    P.emit("s", lambda e, t=t, sl=t % 2: e.activation(
                        sq_sb[:, :], xr[sl][:, :], AF.Square),
                        wait=["ldx", ("squ", P.val("squ"))],
                        inc=[("ssq", 1)])
                    for c in range(4):
                        P.emit("t", lambda e, t=t, c=c, sl=t % 2: e.matmul(
                            stat_ps[:, c * 512:(c + 1) * 512], ones_f32[:, :],
                            xr[sl][:, c * 512:(c + 1) * 512],
                            start=(t == 0), stop=(t == DT - 1)),
                            wait=["ldx"],
                            inc=[("sxu", 1)] if c == 3 else ())
                    for c in range(4):
                        P.emit("t", lambda e, t=t, c=c: e.matmul(
                            stat_ps2[:, c * 512:(c + 1) * 512], ones_f32[:, :],
                            sq_sb[:, c * 512:(c + 1) * 512],
                            start=(t == 0), stop=(t == DT - 1)),
                            wait=[("ssq", P.val("ssq"))],
                            inc=[("squ", 1)] if c == 3 else ())
                P.emit("v", lambda e: e.tensor_scalar(
                    mv[:, :], stat_ps[:, :], 1.0 / D, None, ALU.mult),
                    wait=["sxu", "squ"], inc=[("lnv", 1)])
                P.emit("v", lambda e: e.tensor_scalar(
                    vv[:, :], stat_ps2[:, :], 1.0 / D, None, ALU.mult),
                    wait=["sxu", "squ"], inc=[("lnv", 1)])
                P.emit("v", lambda e: e.scalar_tensor_tensor(
                    sq_sb[0:1, 0:S], mv[0:1, :], -1.0, mv[0:1, :], ALU.mult, ALU.mult),
                    wait=["lnv"], inc=[("lnv", 1)])
                P.emit("v", lambda e: e.tensor_add(vv[:, :], vv[:, :], sq_sb[0:1, 0:S]),
                       wait=["lnv"], inc=[("lnv", 1)])
                P.emit("s", lambda e: e.activation(
                    vv[:, :], vv[:, :], AF.Sqrt, bias=eps_sb[0:1, 0:1]),
                    wait=["lnv", "eps"], inc=[("lnv", 1)])
                P.emit("v", lambda e: e.reciprocal(vv[:, :], vv[:, :]),
                       wait=["lnv"], inc=[("rstd", 1)])
                P.flush("ph1a")
            with ExitStack() as c1b:
                rep_ps = c1b.enter_context(nc.psum_tensor("A_rep_ps", [128, 512], F32))
                rep_ps2 = c1b.enter_context(nc.psum_tensor("A_rep_ps2", [128, 512], F32))
                emit_rep(P, rep_ps, rep_ps2, mv, vv, m_rep, r_rep, ones_f32r, "A")
                for t in range(DT):
                    P.emit("y", lambda e, t=t, sl=t % 2: e.dma_start(
                        xr[sl][:, :], xT[t * 128:(t + 1) * 128, :]),
                        wait=[P.lag("xsub", 1)], inc=[("ldx", 16)])
                    P.emit("v", lambda e, t=t, sl=t % 2: e.tensor_sub(
                        sq_sb[:, :], xr[sl][:, :], m_rep[:, :]),
                        wait=["mrv", "rrv", "ldx",
                              ("xlnd", P.val("xlnd"))],
                        inc=[("xsub", 1)])
                    P.emit("v", lambda e, t=t: e.tensor_mul(
                        x_ln[t][:, :], sq_sb[:, :], r_rep[:, :]),
                        wait=[("xsub", P.val("xsub"))], inc=[("xlnd", 1)])
                P.flush("ph1b")

        with ExitStack() as c2:
            C = c2.enter_context
            wbuf = [C(nc.sbuf_tensor(f"A_wbuf{i}", [128, D], BF16)) for i in range(2)]
            rwbuf = C(nc.sbuf_tensor("A_rwbuf", [128, 6 * DT], BF16))
            gwbuf = C(nc.sbuf_tensor("A_gwbuf", [128, H * DT], BF16))
            vT_t = C(nc.sbuf_tensor("A_vT_t", [128, S], BF16))
            pp = [C(nc.psum_tensor(f"A_pp{i}", [128, 512], F32)) for i in range(4)]
            tp = C(nc.psum_tensor("A_tp", [128, 128], BF16))
            p3 = C(nc.psum_tensor("A_p3", [3, 512], F32))
            p16 = C(nc.psum_tensor("A_p16", [16, 512], F32))

            P.gate(["s", "v", "g"], ["ldm", "bk8"])

            for t in range(DT):
                P.emit("g", lambda e, t=t: e.dma_start(
                    rwbuf[:, t * 6:t * 6 + 3], ri[t * 128:(t + 1) * 128, :]), inc=[("ldr", 16)])
                P.emit("g", lambda e, t=t: e.dma_start(
                    rwbuf[:, t * 6 + 3:t * 6 + 6], rj[t * 128:(t + 1) * 128, :]), inc=[("ldr", 16)])
                P.emit("g", lambda e, t=t: e.dma_start(
                    gwbuf[:, t * H:(t + 1) * H], gw[t * 128:(t + 1) * 128, :]), inc=[("ldr", 16)])

            for c in range(4):
                cs = slice(c * 512, (c + 1) * 512)
                for t in range(DT):
                    P.emit("t", lambda e, t=t, cs=cs: e.matmul(
                        p3[:, :], rwbuf[:, t * 6:t * 6 + 3], x_ln[t][:, cs],
                        start=(t == 0), stop=(t == DT - 1)),
                        wait=[("ldr", 48 * DT), ("aev", P.val("aev"))],
                        inc=[("limm", 1)] if t == DT - 1 else ())
                P.emit("s", lambda e, cs=cs: e.activation(
                    aT[:, cs], p3[:, :], AF.Exp, bias=relb_sb[:, 0:1]),
                    wait=[("limm", P.val("limm"))], inc=[("aev", 1)])
                for t in range(DT):
                    P.emit("t", lambda e, t=t, cs=cs: e.matmul(
                        p3[:, :], rwbuf[:, t * 6 + 3:t * 6 + 6], x_ln[t][:, cs],
                        start=(t == 0), stop=(t == DT - 1)),
                        wait=[("aev", P.val("aev"))],
                        inc=[("ljmm", 1)] if t == DT - 1 else ())
                P.emit("s", lambda e, cs=cs: e.activation(
                    cT[:, cs], p3[:, :], AF.Exp),
                    wait=[("ljmm", P.val("ljmm"))], inc=[("aev", 1)])
                for t in range(DT):
                    P.emit("t", lambda e, t=t, cs=cs: e.matmul(
                        p16[:, :], gwbuf[:, t * H:(t + 1) * H], x_ln[t][:, cs],
                        start=(t == 0), stop=(t == DT - 1)),
                        wait=[("gev", P.val("gev"))],
                        inc=[("gmm", 1)] if t == DT - 1 else ())
                P.emit("s", lambda e, cs=cs: e.activation(
                    gateT[:, cs], p16[:, :], AF.Sigmoid, bias=gb_sb[:, 0:1]),
                    wait=[("gmm", P.val("gmm"))], inc=[("gev", 1)])

            def proj_dmajor(wdram, bias_sb, scale, out_tiles, ppbase):
                for ob in range(DT):
                    slot = ob % 2
                    P.emit("g", lambda e, ob=ob, slot=slot, w=wdram: e.dma_start(
                        wbuf[slot][:, :], w[ob * 128:(ob + 1) * 128, :]),
                        wait=[P.lag("pev", 4)], inc=[("wld", 16)])
                    for c in range(4):
                        cs = slice(c * 512, (c + 1) * 512)
                        ps = pp[ppbase + (c % 2)]
                        for t in range(DT):
                            P.emit("t", lambda e, slot=slot, t=t, cs=cs, ps=ps: e.matmul(
                                ps[:, :], wbuf[slot][:, t * 128:(t + 1) * 128],
                                x_ln[t][:, cs], start=(t == 0), stop=(t == DT - 1)),
                                wait=[("wld", P.val("wld")), P.lag("pev", 1)],
                                inc=[("pmm", 1)] if t == DT - 1 else ())
                        P.emit("s", lambda e, ob=ob, cs=cs, ps=ps, b=bias_sb, sc=scale,
                               ot=out_tiles: e.activation(
                            ot[ob][:, cs], ps[:, :], AF.Identity,
                            bias=b[:, ob:ob + 1], scale=sc),
                            wait=[("pmm", P.val("pmm"))], inc=[("pev", 1)])

            proj_dmajor(wq, bq_sb, 1.0, qT, 0)
            proj_dmajor(wk, bk_sb, 0.125, kT, 0)

            for j in range(ST):
                P.emit("v", lambda e, j=j: e.memset(v_aug[j][:, :], 1.0),
                       inc=[("vamem", 1)])
            for ob in range(DT):
                slot = ob % 2
                P.emit("g", lambda e, ob=ob, slot=slot: e.dma_start(
                    wbuf[slot][:, :], wv[ob * 128:(ob + 1) * 128, :]),
                    wait=[P.lag("pev", 4)], inc=[("wld", 16)])
                for c in range(4):
                    cs = slice(c * 512, (c + 1) * 512)
                    ps = pp[2 + (c % 2)]
                    for t in range(DT):
                        P.emit("t", lambda e, slot=slot, t=t, cs=cs, ps=ps: e.matmul(
                            ps[:, :], wbuf[slot][:, t * 128:(t + 1) * 128],
                            x_ln[t][:, cs], start=(t == 0), stop=(t == DT - 1)),
                            wait=[("wld", P.val("wld")), P.lag("pev", 1)],
                            inc=[("vmm", 1)] if t == DT - 1 else ())
                    P.emit("s", lambda e, ob=ob, cs=cs, ps=ps: e.activation(
                        vT_t[:, cs], ps[:, :], AF.Identity, bias=bv_sb[:, ob:ob + 1]),
                        wait=[("vmm", P.val("vmm")), ("vtr", P.val("vtr"))],
                        inc=[("pev", 1)])
                for j in range(ST):
                    P.emit("t", lambda e, j=j: e.transpose(
                        tp[:, :], vT_t[:, j * 128:(j + 1) * 128], id_sb[:, :]),
                        wait=[("pev", P.val("pev")), ("vtrc", P.val("vtrc"))],
                        inc=[("vtr", 1)])
                    P.emit("v", lambda e, ob=ob, j=j: e.tensor_copy(
                        v_aug[j][:, (2 * ob) * 65:(2 * ob) * 65 + 64], tp[:, 0:64]),
                        wait=[("vtr", P.val("vtr")), "vamem"], inc=[("vtrc", 1)])
                    P.emit("v", lambda e, ob=ob, j=j: e.tensor_copy(
                        v_aug[j][:, (2 * ob + 1) * 65:(2 * ob + 1) * 65 + 64],
                        tp[:, 64:128]),
                        wait=[("vtr", P.val("vtr")), "vamem"], inc=[("vtrc", 1)])
            P.flush("ph2")

        cx.close()
        attnT = [E(nc.sbuf_tensor(f"A_attnT{t}", [128, S], BF16)) for t in range(DT)]

        with ExitStack() as c3:
            C = c3.enter_context
            r_t = [C(nc.sbuf_tensor(f"A_r{j}", [128, 512], BF16)) for j in range(ST)]
            E2 = [C(nc.sbuf_tensor(f"A_E2_{i}", [128, 1024], BF16)) for i in range(2)]
            Tb = [C(nc.sbuf_tensor(f"A_Tb{i}", [128, 512], BF16)) for i in range(2)]
            qz = [C(nc.sbuf_tensor(f"A_qz{i}", [128, 512], BF16)) for i in range(2)]
            abuf = C(nc.sbuf_tensor("A_abuf", [3, 512], BF16))
            t1buf = C(nc.sbuf_tensor("A_t1buf", [3, 512], BF16))
            rcp = C(nc.sbuf_tensor("A_rcp", [1, 512], BF16))
            rep64 = C(nc.sbuf_tensor("A_rep64", [64, 512], BF16))
            lnD = C(nc.sbuf_tensor("A_lnD", [128, 512], F32))
            avcp = [C(nc.sbuf_tensor(f"A_avcp{i}", [65, 512], F32)) for i in range(2)]
            ps_S = [C(nc.psum_tensor(f"A_ps_S{i}", [128, 1024], F32)) for i in range(2)]
            ps_N = C(nc.psum_tensor("A_ps_N", [128, 512], F32))
            ps_AV = C(nc.psum_tensor("A_ps_AV", [65, 512], F32))
            ps_g3 = C(nc.psum_tensor("A_ps_g3", [3, 512], F32))
            ps_bc = C(nc.psum_tensor("A_ps_bc", [64, 512], F32))

            P.emit("v", lambda e: e.memset(qz[0][:, :], 0.0), inc=[("qzi", 1)])
            P.emit("v", lambda e: e.memset(qz[1][:, :], 0.0), inc=[("qzi", 1)])

            for ic in range(4):
                cs = slice(ic * 512, (ic + 1) * 512)
                for j in range(ST):
                    P.emit("t", lambda e, j=j, cs=cs: e.matmul(
                        ps_N[:, :], cT[:, j * 128:(j + 1) * 128], aT[:, cs],
                        start=True, stop=True),
                        wait=[("rln", P.val("rln")), ("tmv", P.val("tmv"))],
                        inc=[("dmm", 1)])
                    P.emit("s", lambda e: e.activation(lnD[:, :], ps_N[:, :], AF.Ln),
                           wait=[("dmm", P.val("dmm")), ("rex", P.val("rex"))],
                           inc=[("rln", 1)])
                    P.emit("s", lambda e, j=j: e.activation(
                        r_t[j][:, :], lnD[:, :], AF.Exp, scale=-1.0),
                        wait=[("rln", P.val("rln")), ("tmv", P.val("tmv"))],
                        inc=[("rex", 1)])
                for h in range(H):
                    par = h % 2
                    ht = h // 2
                    P.emit("v", lambda e, par=par, ht=ht, cs=cs: e.tensor_copy(
                        qz[par][par * 64:par * 64 + 64, :],
                        qT[ht][par * 64:par * 64 + 64, cs]),
                        wait=["qzi", P.lag("qzu" + str(par), 0)],
                        inc=[("qzc" + str(par), 1)])
                    P.emit("t", lambda e, h=h, cs=cs: e.matmul(
                        ps_g3[:, :], e16_sb[:, 3 * h:3 * h + 3], gateT[:, cs],
                        start=True, stop=True),
                        wait=[("abv", P.val("abv"))], inc=[("g3m", 1)])
                    P.emit("v", lambda e, h=h, cs=cs: e.tensor_scalar(
                        t1buf[:, :], aT[:, cs], lbT_sb[:, h:h + 1], None, ALU.mult),
                        wait=[("abv", P.val("abv"))], inc=[("t1v", 1)])
                    P.emit("v", lambda e: e.tensor_mul(abuf[:, :], t1buf[:, :], ps_g3[:, :]),
                           wait=[("g3m", P.val("g3m")), ("t1v", P.val("t1v"))],
                           inc=[("abv", 1)])
                    for grp in range(8):
                        half = grp % 2
                        sb = ps_S[half]
                        eh = "exh" + str(half)
                        ah = "avh" + str(half)
                        for jj2 in range(2):
                            j = grp * 2 + jj2
                            col = slice(jj2 * 512, (jj2 + 1) * 512)
                            P.emit("t", lambda e, ht=ht, j=j, col=col, sb=sb, par=par: e.matmul(
                                sb[:, col], kT[ht][:, j * 128:(j + 1) * 128],
                                qz[par][:, :], start=True, stop=False),
                                wait=[("qzc" + str(par), P.val("qzc" + str(par))),
                                      (eh, P.val(eh))],
                                inc=[("qzu" + str(par), 1)] if (grp == 7 and jj2 == 1) else ())
                            P.emit("t", lambda e, j=j: e.matmul(
                                ps_N[:, :], cT[:, j * 128:(j + 1) * 128], abuf[:, :],
                                start=True, stop=True),
                                wait=[("abv", P.val("abv")), ("tmv", P.val("tmv")),
                                      ("rln", P.val("rln"))],
                                inc=[("nmm", 1)])
                            P.emit("v", lambda e, j=j, jj2=jj2: e.tensor_mul(
                                Tb[jj2][:, :], ps_N[:, :], r_t[j][:, :]),
                                wait=[("nmm", P.val("nmm")), ("rex", P.val("rex")),
                                      P.lag("idm", 1)],
                                inc=[("tmv", 1)])
                            P.emit("t", lambda e, sb=sb, col=col, jj2=jj2: e.matmul(
                                sb[:, col], id_sb[:, :], Tb[jj2][:, :],
                                start=False, stop=True),
                                wait=[("tmv", P.val("tmv"))], inc=[("idm", 1)])
                        P.emit("s", lambda e, sb=sb, half=half: e.activation(
                            E2[half][:, :], sb[:, :], AF.Exp),
                            wait=[("idm", P.val("idm")), ("ava", P.val("ava"))],
                            inc=[(eh, 1)])
                        for jj2 in range(2):
                            j = grp * 2 + jj2
                            P.emit("t", lambda e, h=h, j=j, jj2=jj2, half=half, grp=grp: e.matmul(
                                ps_AV[:, :], v_aug[j][:, h * 65:h * 65 + 65],
                                E2[half][:, jj2 * 512:(jj2 + 1) * 512],
                                start=(grp == 0 and jj2 == 0),
                                stop=(grp == 7 and jj2 == 1)),
                                wait=[(eh, P.val(eh)), ("avn", P.val("avn"))],
                                inc=[("ava", 1)] if jj2 == 1 else ())
                    P.emit("v", lambda e, sl=h % 2: e.tensor_copy(
                        avcp[sl][:, :], ps_AV[:, :]),
                        wait=[("ava", P.val("ava")), P.lag("anm", 1)], inc=[("avn", 1)])
                    P.emit("v", lambda e, sl=h % 2: e.reciprocal(rcp[:, :], avcp[sl][64:65, :]),
                           wait=[("avn", P.val("avn")), ("bcm", P.val("bcm"))], inc=[("rcv", 1)])
                    P.emit("t", lambda e: e.matmul(
                        ps_bc[:, :], ones_bf[:, 0:64], rcp[:, :], start=True, stop=True),
                        wait=[("rcv", P.val("rcv")), P.lag("rpv", 0)], inc=[("bcm", 1)])
                    P.emit("v", lambda e: e.tensor_copy(rep64[:, :], ps_bc[:, :]),
                           wait=[("bcm", P.val("bcm"))], inc=[("rpv", 1)])
                    P.emit("v", lambda e, par=par, ht=ht, cs=cs, sl=h % 2: e.tensor_mul(
                        attnT[ht][par * 64:par * 64 + 64, cs],
                        avcp[sl][0:64, :], rep64[:, :]),
                        wait=[("rpv", P.val("rpv"))],
                        inc=[("anm", 1)])
            P.flush("ph3")

        with ExitStack() as c4:
            C = c4.enter_context
            wbuf2 = [C(nc.sbuf_tensor(f"A_wbuf2_{i}", [128, D], BF16)) for i in range(2)]
            xc = [C(nc.sbuf_tensor(f"A_xc{i}", [128, 512], F32)) for i in range(4)]
            outst = [C(nc.sbuf_tensor(f"A_outst{i}", [128, 512], F32)) for i in range(2)]
            ppo = [C(nc.psum_tensor(f"A_ppo{i}", [128, 512], F32)) for i in range(2)]

            for ob in range(DT):
                slot = ob % 2
                P.emit("g", lambda e, ob=ob, slot=slot: e.dma_start(
                    wbuf2[slot][:, :], wo[ob * 128:(ob + 1) * 128, :]),
                    wait=[P.lag("oev", 4)], inc=[("owl", 16)])
                for c in range(4):
                    cs = slice(c * 512, (c + 1) * 512)
                    P.emit("y", lambda e, ob=ob, cs=cs, xs=c % 4: e.dma_start(
                        xc[xs][:, :], xT[ob * 128:(ob + 1) * 128, cs]),
                        wait=[P.lag("oev", 2)], inc=[("lxc", 16)])
                    for t in range(DT):
                        P.emit("t", lambda e, slot=slot, t=t, cs=cs, c=c: e.matmul(
                            ppo[c % 2][:, :], wbuf2[slot][:, t * 128:(t + 1) * 128],
                            attnT[t][:, cs], start=(t == 0), stop=(t == DT - 1)),
                            wait=[("owl", P.val("owl")), P.lag("oev", 1)],
                            inc=[("omm", 1)] if t == DT - 1 else ())
                    P.emit("v", lambda e, ob=ob, c=c: e.scalar_tensor_tensor(
                        outst[c % 2][:, :], ppo[c % 2][:, :], bo_sb[:, ob:ob + 1],
                        xc[c % 4][:, :], ALU.add, ALU.add),
                        wait=[("omm", P.val("omm")), ("lxc", P.val("lxc")),
                              P.lag("ost", 16)],
                        inc=[("oev", 1)])
                    P.emit("y", lambda e, ob=ob, cs=cs, c=c: e.dma_start(
                        x2T[ob * 128:(ob + 1) * 128, cs], outst[c % 2][:, :]),
                        wait=[("oev", P.val("oev"))], inc=[("ost", 16)])
            if debug:
                for t in range(DT):
                    P.emit("g", lambda e, t=t: e.dma_start(
                        dbg_q[t * 128:(t + 1) * 128, :], qT[t][:, :]), inc=[("dbg", 16)])
                    P.emit("g", lambda e, t=t: e.dma_start(
                        dbg_k[t * 128:(t + 1) * 128, :], kT[t][:, :]), inc=[("dbg", 16)])
                    P.emit("g", lambda e, t=t: e.dma_start(
                        dbg_at[t * 128:(t + 1) * 128, :], attnT[t][:, :]), inc=[("dbg", 16)])
                P.emit("g", None, wait=[("dbg", P.val("dbg"))])
            P.emit("y", None, wait=[("ost", P.val("ost"))])
            P.flush("ph4")






DH = 4096
DHT = DH // 128


def emit_B(nc, P, ctx, xin, xout, debug=False):
    fw1 = nc.dram_tensor("B_fw1", [DH, D], F32, kind="ExternalInput").ap()
    fw2 = nc.dram_tensor("B_fw2", [DH, D], F32, kind="ExternalInput").ap()
    fh1 = nc.dram_tensor("B_fh1", [DH, DH], F32, kind="ExternalInput").ap()
    fh2 = nc.dram_tensor("B_fh2", [DH, DH], F32, kind="ExternalInput").ap()
    fw3 = nc.dram_tensor("B_fw3", [D, DH], F32, kind="ExternalInput").ap()
    s1v = nc.dram_tensor("B_s1v", [128, DHT], F32, kind="ExternalInput").ap()
    s2v = nc.dram_tensor("B_s2v", [128, DHT], F32, kind="ExternalInput").ap()
    hs1v = nc.dram_tensor("B_hs1v", [128, DHT], F32, kind="ExternalInput").ap()
    hs2v = nc.dram_tensor("B_hs2v", [128, DHT], F32, kind="ExternalInput").ap()
    s3v = nc.dram_tensor("B_s3v", [128, DT], F32, kind="ExternalInput").ap()
    xoT = xout

    with ExitStack() as bctx:
        E = bctx.enter_context

        s1_sb = E(nc.sbuf_tensor("B_s1_sb", [128, DHT], F32))
        s2_sb = E(nc.sbuf_tensor("B_s2_sb", [128, DHT], F32))
        hs1_sb = E(nc.sbuf_tensor("B_hs1_sb", [128, DHT], F32))
        hs2_sb = E(nc.sbuf_tensor("B_hs2_sb", [128, DHT], F32))
        s3_sb = E(nc.sbuf_tensor("B_s3_sb", [128, DT], F32))
        ones_f32 = E(nc.sbuf_tensor("B_ones_f32", [128, 1], F32))
        ones_f32r = E(nc.sbuf_tensor("B_ones_f32r", [1, 128], F32))
        eps_sb = E(nc.sbuf_tensor("B_eps_sb", [1, 1], F32))
        mv = E(nc.sbuf_tensor("B_mv", [1, S], F32))
        vv = E(nc.sbuf_tensor("B_vv", [1, S], F32))
        m_rep = E(nc.sbuf_tensor("B_m_rep", [128, S], F32))
        r_rep = E(nc.sbuf_tensor("B_r_rep", [128, S], F32))

        for nm, sb, ap_ in (("s1", s1_sb, s1v), ("s2", s2_sb, s2v),
                            ("hs1", hs1_sb, hs1v), ("hs2", hs2_sb, hs2v),
                            ("s3", s3_sb, s3v)):
            P.emit("y", lambda e, sb=sb, a=ap_: e.dma_start(sb[:, :], a), inc=[("ldm", 16)])
        LDMB = 5 * 16
        P.emit("v", lambda e: e.memset(ones_f32[:, :], 1.0), inc=[("ldv", 1)])
        P.emit("v", lambda e: e.memset(ones_f32r[:, :], 1.0), inc=[("ldv", 1)])
        P.emit("v", lambda e: e.memset(eps_sb[:, :], EPS), inc=[("ldv", 1)])

        with ExitStack() as c1:
            C = c1.enter_context
            xr = [C(nc.sbuf_tensor(f"B_xr{i}", [128, S], F32)) for i in range(2)]
            sq_sb = C(nc.sbuf_tensor("B_sq", [128, S], F32))
            with ExitStack() as c1s:
                stat_ps = c1s.enter_context(nc.psum_tensor("B_stat_ps", [1, S], F32))
                stat_ps2 = c1s.enter_context(nc.psum_tensor("B_stat_ps2", [1, S], F32))
                P.gate(["t"], ["ldv"])
                for t in range(DT):
                    P.emit("y", lambda e, t=t, sl=t % 2: e.dma_start(
                        xr[sl][:, :], xin[t * 128:(t + 1) * 128, :]),
                        wait=[P.lag("sxu", 0), P.lag("ssq", 0)], inc=[("ldx", 16)])
                    P.emit("s", lambda e, sl=t % 2: e.activation(
                        sq_sb[:, :], xr[sl][:, :], AF.Square),
                        wait=["ldx", ("squ", P.val("squ"))],
                        inc=[("ssq", 1)])
                    for c in range(4):
                        P.emit("t", lambda e, t=t, c=c, sl=t % 2: e.matmul(
                            stat_ps[:, c * 512:(c + 1) * 512], ones_f32[:, :],
                            xr[sl][:, c * 512:(c + 1) * 512],
                            start=(t == 0), stop=(t == DT - 1)),
                            wait=["ldx"],
                            inc=[("sxu", 1)] if c == 3 else ())
                    for c in range(4):
                        P.emit("t", lambda e, t=t, c=c: e.matmul(
                            stat_ps2[:, c * 512:(c + 1) * 512], ones_f32[:, :],
                            sq_sb[:, c * 512:(c + 1) * 512],
                            start=(t == 0), stop=(t == DT - 1)),
                            wait=[("ssq", P.val("ssq"))],
                            inc=[("squ", 1)] if c == 3 else ())
                P.emit("v", lambda e: e.tensor_scalar(
                    mv[:, :], stat_ps[:, :], 1.0 / D, None, ALU.mult),
                    wait=["sxu", "squ"], inc=[("lnv", 1)])
                P.emit("v", lambda e: e.tensor_scalar(
                    vv[:, :], stat_ps2[:, :], 1.0 / D, None, ALU.mult),
                    wait=["sxu", "squ"], inc=[("lnv", 1)])
                P.emit("v", lambda e: e.scalar_tensor_tensor(
                    sq_sb[0:1, 0:S], mv[0:1, :], -1.0, mv[0:1, :], ALU.mult, ALU.mult),
                    wait=["lnv"], inc=[("lnv", 1)])
                P.emit("v", lambda e: e.tensor_add(vv[:, :], vv[:, :], sq_sb[0:1, 0:S]),
                       wait=["lnv"], inc=[("lnv", 1)])
                P.emit("s", lambda e: e.activation(
                    vv[:, :], vv[:, :], AF.Sqrt, bias=eps_sb[0:1, 0:1]),
                    wait=["lnv", "ldv"], inc=[("lnv", 1)])
                P.emit("v", lambda e: e.reciprocal(vv[:, :], vv[:, :]),
                       wait=["lnv"], inc=[("rstd", 1)])
                P.flush("phB1a")
            with ExitStack() as c1b:
                rep_ps = c1b.enter_context(nc.psum_tensor("B_rep_ps", [128, 512], F32))
                rep_ps2 = c1b.enter_context(nc.psum_tensor("B_rep_ps2", [128, 512], F32))
                emit_rep(P, rep_ps, rep_ps2, mv, vv, m_rep, r_rep, ones_f32r, "B")
                P.flush("phB1b")

        with ExitStack() as c2:
            C = c2.enter_context
            x2c = [C(nc.sbuf_tensor(f"B_x2c{t}", [128, 512], F32)) for t in range(DT)]
            xln = [C(nc.sbuf_tensor(f"B_xln{t}", [128, 512], BF16)) for t in range(DT)]
            scr = C(nc.sbuf_tensor("B_scr", [128, 512], F32))
            a_sb = [C(nc.sbuf_tensor(f"B_a_sb{i}", [128, 512], F32)) for i in range(2)]
            hA = [C(nc.sbuf_tensor(f"B_hA{i}", [128, 512], BF16)) for i in range(DHT)]
            hB = [C(nc.sbuf_tensor(f"B_hB{i}", [128, 512], BF16)) for i in range(DHT)]
            wb1 = [C(nc.sbuf_tensor(f"B_wb1_{i}", [128, D], BF16)) for i in range(4)]
            wb4 = [C(nc.sbuf_tensor(f"B_wb4_{i}", [128, DH], BF16)) for i in range(2)]
            outst = [C(nc.sbuf_tensor(f"B_outst{i}", [128, 512], F32)) for i in range(2)]
            pz = [C(nc.psum_tensor(f"B_pz{i}", [128, 512], F32)) for i in range(4)]
            pg = [C(nc.psum_tensor(f"B_pg{i}", [128, 512], F32)) for i in range(2)]

            P.gate(["s", "v", "g"], ["ldm", "rstd"])

            for ch in range(4):
                cs = slice(ch * 512, (ch + 1) * 512)
                for t in range(DT):
                    P.emit("y", lambda e, t=t, cs=cs: e.dma_start(
                        x2c[t][:, :], xin[t * 128:(t + 1) * 128, cs]),
                        wait=[P.lag("bs4", 1)], inc=[("lxc", 16)])
                    P.emit("v", lambda e, t=t, cs=cs: e.tensor_sub(
                        scr[:, :], x2c[t][:, :], m_rep[:, cs]),
                        wait=[("lxc", P.val("lxc")), "mrv", "rrv",
                              ("bxl", P.val("bxl")), ("bz1", P.val("bz1")),
                              ("bz2", P.val("bz2"))],
                        inc=[("bxs", 1)])
                    P.emit("v", lambda e, t=t, cs=cs: e.tensor_mul(
                        xln[t][:, :], scr[:, :], r_rep[:, cs]),
                        wait=[("bxs", P.val("bxs"))], inc=[("bxl", 1)])
                for ob in range(DHT):
                    sl1, sl2 = (ob % 2), 2 + (ob % 2)
                    P.emit("g", lambda e, ob=ob, sl=sl1: e.dma_start(
                        wb1[sl][:, :], fw1[ob * 128:(ob + 1) * 128, :]),
                        wait=[P.lag("bz1", 1)], inc=[("bw1", 16)])
                    P.emit("g", lambda e, ob=ob, sl=sl2: e.dma_start(
                        wb1[sl][:, :], fw2[ob * 128:(ob + 1) * 128, :]),
                        wait=[P.lag("bz2", 1)], inc=[("bw1", 16)])
                    for t in range(DT):
                        P.emit("t", lambda e, ob=ob, sl=sl1, t=t: e.matmul(
                            pz[ob % 2][:, :], wb1[sl][:, t * 128:(t + 1) * 128],
                            xln[t][:, :], start=(t == 0), stop=(t == DT - 1)),
                            wait=[("bw1", P.val("bw1")), ("bxl", P.val("bxl")),
                                  P.lag("bsi", 1)],
                            inc=[("bz1", 1)] if t == DT - 1 else ())
                    for t in range(DT):
                        P.emit("t", lambda e, ob=ob, sl=sl2, t=t: e.matmul(
                            pz[2 + ob % 2][:, :], wb1[sl][:, t * 128:(t + 1) * 128],
                            xln[t][:, :], start=(t == 0), stop=(t == DT - 1)),
                            wait=[("bw1", P.val("bw1")), P.lag("bhv", 1)],
                            inc=[("bz2", 1)] if t == DT - 1 else ())
                    P.emit("s", lambda e, ob=ob, sl=ob % 2: e.activation(
                        a_sb[sl][:, :], pz[ob % 2][:, :], AF.Silu,
                        scale=s1_sb[:, ob:ob + 1]),
                        wait=[("bz1", P.val("bz1")), P.lag("bhv", 1)],
                        inc=[("bsi", 1)])
                    P.emit("v", lambda e, ob=ob, sl=ob % 2: e.scalar_tensor_tensor(
                        hA[ob][:, :], pz[2 + ob % 2][:, :], s2_sb[:, ob:ob + 1],
                        a_sb[sl][:, :], ALU.mult, ALU.mult),
                        wait=[("bz2", P.val("bz2")), ("bsi", P.val("bsi")),
                              ("bgu", P.val("bgu"))],
                        inc=[("bhv", 1)])
                for ob in range(DHT):
                    P.emit("g", lambda e, ob=ob, sl=ob % 2: e.dma_start(
                        wb4[sl][:, :], fh1[ob * 128:(ob + 1) * 128, :]),
                        wait=[P.lag("bgu", 1)], inc=[("bw4", 16)])
                    for t in range(DHT):
                        P.emit("t", lambda e, ob=ob, sl=ob % 2, t=t: e.matmul(
                            pg[ob % 2][:, :], wb4[sl][:, t * 128:(t + 1) * 128],
                            hA[t][:, :], start=(t == 0), stop=(t == DHT - 1)),
                            wait=[("bw4", P.val("bw4")), ("bhv", P.val("bhv")),
                                  P.lag("bg2", 1)],
                            inc=[("bgu", 1)] if t == DHT - 1 else ())
                    P.emit("s", lambda e, ob=ob: e.activation(
                        hB[ob][:, :], pg[ob % 2][:, :], AF.Gelu,
                        scale=hs1_sb[:, ob:ob + 1]),
                        wait=[("bgu", P.val("bgu"))], inc=[("bg2", 1)])
                for ob in range(DHT):
                    P.emit("g", lambda e, ob=ob, sl=ob % 2: e.dma_start(
                        wb4[sl][:, :], fh2[ob * 128:(ob + 1) * 128, :]),
                        wait=[P.lag("bgu", 1)], inc=[("bw4", 16)])
                    for t in range(DHT):
                        P.emit("t", lambda e, ob=ob, sl=ob % 2, t=t: e.matmul(
                            pg[ob % 2][:, :], wb4[sl][:, t * 128:(t + 1) * 128],
                            hB[t][:, :], start=(t == 0), stop=(t == DHT - 1)),
                            wait=[("bw4", P.val("bw4")), ("bg2", P.val("bg2")),
                                  P.lag("bg3", 1)],
                            inc=[("bgu", 1)] if t == DHT - 1 else ())
                    P.emit("s", lambda e, ob=ob: e.activation(
                        hA[ob][:, :], pg[ob % 2][:, :], AF.Gelu,
                        scale=hs2_sb[:, ob:ob + 1]),
                        wait=[("bgu", P.val("bgu"))], inc=[("bg3", 1)])
                for do in range(DT):
                    P.emit("g", lambda e, do=do, sl=do % 2: e.dma_start(
                        wb4[sl][:, :], fw3[do * 128:(do + 1) * 128, :]),
                        wait=[P.lag("bgu", 1)], inc=[("bw4", 16)])
                    for t in range(DHT):
                        P.emit("t", lambda e, do=do, sl=do % 2, t=t: e.matmul(
                            pg[do % 2][:, :], wb4[sl][:, t * 128:(t + 1) * 128],
                            hA[t][:, :], start=(t == 0), stop=(t == DHT - 1)),
                            wait=[("bw4", P.val("bw4")), ("bg3", P.val("bg3")),
                                  P.lag("bs4", 1)],
                            inc=[("bgu", 1)] if t == DHT - 1 else ())
                    P.emit("v", lambda e, do=do: e.scalar_tensor_tensor(
                        outst[do % 2][:, :], pg[do % 2][:, :], s3_sb[:, do:do + 1],
                        x2c[do][:, :], ALU.mult, ALU.add),
                        wait=[("bgu", P.val("bgu")), ("lxc", P.val("lxc")),
                              P.lag("bst", 16)],
                        inc=[("bs4", 1)])
                    P.emit("y", lambda e, do=do, cs=cs, sl=do % 2: e.dma_start(
                        xoT[do * 128:(do + 1) * 128, cs], outst[sl][:, :]),
                        wait=[("bs4", P.val("bs4"))], inc=[("bst", 16)])
            P.emit("y", None, wait=[("bst", P.val("bst"))])
            P.flush("phB2")


def prep_l0_ffn(ins):
    w = {}
    w["fw1"] = host_blocks(_c(ins["L0_ffn_w1_w"].T))
    w["fw2"] = host_blocks(_c(ins["L0_ffn_w2_w"].T))
    w["fh1"] = host_blocks(_c(ins["L0_ffn_h1_w"].T))
    w["fh2"] = host_blocks(_c(ins["L0_ffn_h2_w"].T))
    w["fw3"] = host_blocks(_c(ins["L0_ffn_w3_w"].T))
    w["s1v"] = host_vec(ins["L0_ffn_w1_s"])
    w["s2v"] = host_vec(ins["L0_ffn_w2_s"])
    w["hs1v"] = host_vec(ins["L0_ffn_h1_s"])
    w["hs2v"] = host_vec(ins["L0_ffn_h2_s"])
    w["s3v"] = host_vec(ins["L0_ffn_w3_s"])
    return w





def emit_C(nc, P, ctx, xin, xout, debug=False):
    wq = nc.dram_tensor("C_wq", [D, D], F32, kind="ExternalInput").ap()
    wk = nc.dram_tensor("C_wk", [D, D], F32, kind="ExternalInput").ap()
    wv = nc.dram_tensor("C_wv", [D, D], F32, kind="ExternalInput").ap()
    wo = nc.dram_tensor("C_wo", [D, D], F32, kind="ExternalInput").ap()
    bqv = nc.dram_tensor("C_bqv", [128, DT], F32, kind="ExternalInput").ap()
    bkv = nc.dram_tensor("C_bkv", [128, DT], F32, kind="ExternalInput").ap()
    bvv = nc.dram_tensor("C_bvv", [128, DT], F32, kind="ExternalInput").ap()
    bov = nc.dram_tensor("C_bov", [128, DT], F32, kind="ExternalInput").ap()
    xoT = xout

    id128 = nc.inline_tensor(np.eye(128, dtype=BF), "C_id128")
    band = np.zeros((128, 1024), dtype=BF)
    for rel in range(-1, 3):
        for p in range(128):
            for i in range(256):
                if abs(128 * rel + p - i) <= 64:
                    band[p, (rel + 1) * 256 + i] = 1.0
    bandc = nc.inline_tensor(band, "C_bandc")

    with ExitStack() as bctx:
        E = bctx.enter_context

        bq_sb = E(nc.sbuf_tensor("C_bq_sb", [128, DT], F32))
        bk_sb = E(nc.sbuf_tensor("C_bk_sb", [128, DT], F32))
        bv_sb = E(nc.sbuf_tensor("C_bv_sb", [128, DT], F32))
        bo_sb = E(nc.sbuf_tensor("C_bo_sb", [128, DT], F32))
        id_sb = E(nc.sbuf_tensor("C_id_sb", [128, 128], BF16))
        band_sb = E(nc.sbuf_tensor("C_band_sb", [128, 1024], BF16))
        ones_f32 = E(nc.sbuf_tensor("C_ones_f32", [128, 1], F32))
        ones_f32r = E(nc.sbuf_tensor("C_ones_f32r", [1, 128], F32))
        ones_bf = E(nc.sbuf_tensor("C_ones_bf", [1, 128], BF16))
        eps_sb = E(nc.sbuf_tensor("C_eps_sb", [1, 1], F32))

        for sb, ap_ in ((bq_sb, bqv), (bk_sb, bkv), (bv_sb, bvv), (bo_sb, bov)):
            P.emit("y", lambda e, sb=sb, a=ap_: e.dma_start(sb[:, :], a), inc=[("ldm", 16)])
        P.emit("g", lambda e: e.dma_start(id_sb[:, :], id128.ap()), inc=[("ldm", 16)])
        P.emit("g", lambda e: e.dma_start(band_sb[:, :], bandc.ap()), inc=[("ldm", 16)])
        LDMC = 6 * 16
        P.emit("v", lambda e: e.memset(ones_f32[:, :], 1.0), inc=[("ldv", 1)])
        P.emit("v", lambda e: e.memset(ones_f32r[:, :], 1.0), inc=[("ldv", 1)])
        P.emit("v", lambda e: e.memset(ones_bf[:, :], 1.0), inc=[("ldv", 1)])
        P.emit("v", lambda e: e.memset(eps_sb[:, :], EPS), inc=[("ldv", 1)])
        P.emit("v", lambda e: e.tensor_scalar(bk_sb[:, :], bk_sb[:, :], 0.125, None, ALU.mult),
               wait=["ldm"], inc=[("bk8", 1)])

        qT = [E(nc.sbuf_tensor(f"C_qT{t}", [128, S], BF16)) for t in range(DT)]
        kT = [E(nc.sbuf_tensor(f"C_kT{t}", [128, S], BF16)) for t in range(DT)]
        v_aug = [E(nc.sbuf_tensor(f"C_vaug{j}", [128, H * 65], BF16)) for j in range(ST)]

        cx = bctx.enter_context(ExitStack())
        x_ln = [cx.enter_context(nc.sbuf_tensor(f"C_x_ln{t}", [128, S], BF16))
                for t in range(DT)]

        with ExitStack() as c1:
            C = c1.enter_context
            xr = [C(nc.sbuf_tensor(f"C_xr{i}", [128, S], F32)) for i in range(2)]
            sq_sb = C(nc.sbuf_tensor("C_sq", [128, S], F32))
            mv = C(nc.sbuf_tensor("C_mv", [1, S], F32))
            vv = C(nc.sbuf_tensor("C_vv", [1, S], F32))
            m_rep = C(nc.sbuf_tensor("C_m_rep", [128, S], F32))
            r_rep = C(nc.sbuf_tensor("C_r_rep", [128, S], F32))
            with ExitStack() as c1s:
                stat_ps = c1s.enter_context(nc.psum_tensor("C_stat_ps", [1, S], F32))
                stat_ps2 = c1s.enter_context(nc.psum_tensor("C_stat_ps2", [1, S], F32))
                P.gate(["t"], ["ldv"])
                for t in range(DT):
                    P.emit("y", lambda e, t=t, sl=t % 2: e.dma_start(
                        xr[sl][:, :], xin[t * 128:(t + 1) * 128, :]),
                        wait=[P.lag("sxu", 0), P.lag("ssq", 0)], inc=[("ldx", 16)])
                    P.emit("s", lambda e, sl=t % 2: e.activation(
                        sq_sb[:, :], xr[sl][:, :], AF.Square),
                        wait=["ldx", ("squ", P.val("squ"))],
                        inc=[("ssq", 1)])
                    for c in range(4):
                        P.emit("t", lambda e, t=t, c=c, sl=t % 2: e.matmul(
                            stat_ps[:, c * 512:(c + 1) * 512], ones_f32[:, :],
                            xr[sl][:, c * 512:(c + 1) * 512],
                            start=(t == 0), stop=(t == DT - 1)),
                            wait=["ldx"],
                            inc=[("sxu", 1)] if c == 3 else ())
                    for c in range(4):
                        P.emit("t", lambda e, t=t, c=c: e.matmul(
                            stat_ps2[:, c * 512:(c + 1) * 512], ones_f32[:, :],
                            sq_sb[:, c * 512:(c + 1) * 512],
                            start=(t == 0), stop=(t == DT - 1)),
                            wait=[("ssq", P.val("ssq"))],
                            inc=[("squ", 1)] if c == 3 else ())
                P.emit("v", lambda e: e.tensor_scalar(
                    mv[:, :], stat_ps[:, :], 1.0 / D, None, ALU.mult),
                    wait=["sxu", "squ"], inc=[("lnv", 1)])
                P.emit("v", lambda e: e.tensor_scalar(
                    vv[:, :], stat_ps2[:, :], 1.0 / D, None, ALU.mult),
                    wait=["sxu", "squ"], inc=[("lnv", 1)])
                P.emit("v", lambda e: e.scalar_tensor_tensor(
                    sq_sb[0:1, 0:S], mv[0:1, :], -1.0, mv[0:1, :], ALU.mult, ALU.mult),
                    wait=["lnv"], inc=[("lnv", 1)])
                P.emit("v", lambda e: e.tensor_add(vv[:, :], vv[:, :], sq_sb[0:1, 0:S]),
                       wait=["lnv"], inc=[("lnv", 1)])
                P.emit("s", lambda e: e.activation(
                    vv[:, :], vv[:, :], AF.Sqrt, bias=eps_sb[0:1, 0:1]),
                    wait=["lnv", "ldv"], inc=[("lnv", 1)])
                P.emit("v", lambda e: e.reciprocal(vv[:, :], vv[:, :]),
                       wait=["lnv"], inc=[("rstd", 1)])
                P.flush("phC1a")
            with ExitStack() as c1b:
                rep_ps = c1b.enter_context(nc.psum_tensor("C_rep_ps", [128, 512], F32))
                rep_ps2 = c1b.enter_context(nc.psum_tensor("C_rep_ps2", [128, 512], F32))
                emit_rep(P, rep_ps, rep_ps2, mv, vv, m_rep, r_rep, ones_f32r, "C")
                for t in range(DT):
                    P.emit("y", lambda e, t=t, sl=t % 2: e.dma_start(
                        xr[sl][:, :], xin[t * 128:(t + 1) * 128, :]),
                        wait=[P.lag("xsub", 1)], inc=[("ldx", 16)])
                    P.emit("v", lambda e, t=t, sl=t % 2: e.tensor_sub(
                        sq_sb[:, :], xr[sl][:, :], m_rep[:, :]),
                        wait=["mrv", "rrv", "ldx",
                              ("xlnd", P.val("xlnd"))],
                        inc=[("xsub", 1)])
                    P.emit("v", lambda e, t=t: e.tensor_mul(
                        x_ln[t][:, :], sq_sb[:, :], r_rep[:, :]),
                        wait=[("xsub", P.val("xsub"))], inc=[("xlnd", 1)])
                P.flush("phC1b")

        with ExitStack() as c2:
            C = c2.enter_context
            wbuf = [C(nc.sbuf_tensor(f"C_wbuf{i}", [128, D], BF16)) for i in range(2)]
            vT_t = C(nc.sbuf_tensor("C_vT_t", [128, S], BF16))
            pp = [C(nc.psum_tensor(f"C_pp{i}", [128, 512], F32)) for i in range(4)]
            tp = C(nc.psum_tensor("C_tp", [128, 128], BF16))
            P.gate(["s", "v", "g"], ["ldm", "bk8"])

            def proj_dmajor(wdram, bias_sb, scale, out_tiles, ppbase):
                for ob in range(DT):
                    slot = ob % 2
                    P.emit("g", lambda e, ob=ob, slot=slot, w=wdram: e.dma_start(
                        wbuf[slot][:, :], w[ob * 128:(ob + 1) * 128, :]),
                        wait=[P.lag("pev", 4)], inc=[("wld", 16)])
                    for c in range(4):
                        cs = slice(c * 512, (c + 1) * 512)
                        ps = pp[ppbase + (c % 2)]
                        for t in range(DT):
                            P.emit("t", lambda e, slot=slot, t=t, cs=cs, ps=ps: e.matmul(
                                ps[:, :], wbuf[slot][:, t * 128:(t + 1) * 128],
                                x_ln[t][:, cs], start=(t == 0), stop=(t == DT - 1)),
                                wait=[("wld", P.val("wld")), P.lag("pev", 1),
                                      ("xlnd", DT)],
                                inc=[("pmm", 1)] if t == DT - 1 else ())
                        P.emit("s", lambda e, ob=ob, cs=cs, ps=ps, b=bias_sb, sc=scale,
                               ot=out_tiles: e.activation(
                            ot[ob][:, cs], ps[:, :], AF.Identity,
                            bias=b[:, ob:ob + 1], scale=sc),
                            wait=[("pmm", P.val("pmm"))], inc=[("pev", 1)])

            proj_dmajor(wq, bq_sb, 1.0, qT, 0)
            proj_dmajor(wk, bk_sb, 0.125, kT, 0)

            for j in range(ST):
                P.emit("v", lambda e, j=j: e.memset(v_aug[j][:, :], 1.0),
                       inc=[("vamem", 1)])
            for ob in range(DT):
                slot = ob % 2
                P.emit("g", lambda e, ob=ob, slot=slot: e.dma_start(
                    wbuf[slot][:, :], wv[ob * 128:(ob + 1) * 128, :]),
                    wait=[P.lag("pev", 4)], inc=[("wld", 16)])
                for c in range(4):
                    cs = slice(c * 512, (c + 1) * 512)
                    ps = pp[2 + (c % 2)]
                    for t in range(DT):
                        P.emit("t", lambda e, slot=slot, t=t, cs=cs, ps=ps: e.matmul(
                            ps[:, :], wbuf[slot][:, t * 128:(t + 1) * 128],
                            x_ln[t][:, cs], start=(t == 0), stop=(t == DT - 1)),
                            wait=[("wld", P.val("wld")), P.lag("pev", 1)],
                            inc=[("vmm", 1)] if t == DT - 1 else ())
                    P.emit("s", lambda e, ob=ob, cs=cs, ps=ps: e.activation(
                        vT_t[:, cs], ps[:, :], AF.Identity, bias=bv_sb[:, ob:ob + 1]),
                        wait=[("vmm", P.val("vmm")), ("vtr", P.val("vtr"))],
                        inc=[("pev", 1)])
                for j in range(ST):
                    P.emit("t", lambda e, j=j: e.transpose(
                        tp[:, :], vT_t[:, j * 128:(j + 1) * 128], id_sb[:, :]),
                        wait=[("pev", P.val("pev")), ("vtrc", P.val("vtrc"))],
                        inc=[("vtr", 1)])
                    P.emit("v", lambda e, ob=ob, j=j: e.tensor_copy(
                        v_aug[j][:, (2 * ob) * 65:(2 * ob) * 65 + 64], tp[:, 0:64]),
                        wait=[("vtr", P.val("vtr")), "vamem"], inc=[("vtrc", 1)])
                    P.emit("v", lambda e, ob=ob, j=j: e.tensor_copy(
                        v_aug[j][:, (2 * ob + 1) * 65:(2 * ob + 1) * 65 + 64],
                        tp[:, 64:128]),
                        wait=[("vtr", P.val("vtr")), "vamem"], inc=[("vtrc", 1)])
            P.flush("phC2")

        cx.close()
        attnT = [E(nc.sbuf_tensor(f"C_attnT{t}", [128, S], BF16)) for t in range(DT)]

        with ExitStack() as c3:
            C = c3.enter_context
            Em = [C(nc.sbuf_tensor(f"C_Em{i}", [128, 1024], BF16)) for i in range(2)]
            qz = [C(nc.sbuf_tensor(f"C_qz{i}", [128, 256], BF16)) for i in range(2)]
            rcp = C(nc.sbuf_tensor("C_rcp", [1, 256], BF16))
            rep64 = C(nc.sbuf_tensor("C_rep64", [64, 256], BF16))
            avcp = [C(nc.sbuf_tensor(f"C_avcp{i}", [65, 256], F32)) for i in range(2)]
            ps_S = [C(nc.psum_tensor(f"C_ps_S{i}", [128, 1024], F32)) for i in range(2)]
            ps_AV = C(nc.psum_tensor("C_ps_AV", [65, 256], F32))
            ps_bc = C(nc.psum_tensor("C_ps_bc", [64, 256], F32))

            P.emit("v", lambda e: e.memset(qz[0][:, :], 0.0), inc=[("qzi", 1)])
            P.emit("v", lambda e: e.memset(qz[1][:, :], 0.0), inc=[("qzi", 1)])

            NIC = 8
            for ic in range(NIC):
                cs = slice(ic * 256, (ic + 1) * 256)
                jlist = [jj for jj in range(2 * ic - 1, 2 * ic + 3) if 0 <= jj < ST]
                rel0 = jlist[0] - 2 * ic
                nj = len(jlist)
                for h in range(H):
                    par = h % 2
                    ht = h // 2
                    half = (ic * H + h) % 2
                    sb = ps_S[half]
                    eh = "exh" + str(half)
                    P.emit("v", lambda e, par=par, ht=ht, cs=cs: e.tensor_copy(
                        qz[par][par * 64:par * 64 + 64, :],
                        qT[ht][par * 64:par * 64 + 64, cs]),
                        wait=["qzi", ("qzu" + str(par), P.val("qzu" + str(par)))],
                        inc=[("qzc" + str(par), 1)])
                    for idx, jj in enumerate(jlist):
                        P.emit("t", lambda e, ht=ht, jj=jj, idx=idx, sb=sb, par=par,
                               last=(idx == nj - 1): e.matmul(
                            sb[:, idx * 256:(idx + 1) * 256],
                            kT[ht][:, jj * 128:(jj + 1) * 128],
                            qz[par][:, :], start=True, stop=True),
                            wait=[("qzc" + str(par), P.val("qzc" + str(par))),
                                  (eh, P.val(eh))],
                            inc=[("qzu" + str(par), 1)] if idx == nj - 1 else ())
                    P.emit("s", lambda e, sb=sb, nj=nj, half=half: e.activation(
                        Em[half][:, 0:nj * 256], sb[:, 0:nj * 256], AF.Exp),
                        wait=[("qzu" + str(par), P.val("qzu" + str(par))),
                              ("emv", P.val("emv")), ("ava", P.val("ava"))],
                        inc=[(eh, 1)])
                    P.emit("v", lambda e, nj=nj, rel0=rel0, half=half: e.tensor_mul(
                        Em[half][:, 0:nj * 256], Em[half][:, 0:nj * 256],
                        band_sb[:, (rel0 + 1) * 256:(rel0 + 1 + nj) * 256]),
                        wait=[(eh, P.val(eh))], inc=[("emv", 1)])
                    for idx, jj in enumerate(jlist):
                        P.emit("t", lambda e, h=h, jj=jj, idx=idx, half=half,
                               st=(idx == 0), sp=(idx == nj - 1): e.matmul(
                            ps_AV[:, :], v_aug[jj][:, h * 65:h * 65 + 65],
                            Em[half][:, idx * 256:(idx + 1) * 256],
                            start=st, stop=sp),
                            wait=[("emv", P.val("emv")), ("avn", P.val("avn"))],
                            inc=[("ava", 1)] if idx == nj - 1 else ())
                    P.emit("v", lambda e, sl=h % 2: e.tensor_copy(
                        avcp[sl][:, :], ps_AV[:, :]),
                        wait=[("ava", P.val("ava")), P.lag("anm", 1)], inc=[("avn", 1)])
                    P.emit("v", lambda e, sl=h % 2: e.reciprocal(
                        rcp[:, :], avcp[sl][64:65, :]),
                        wait=[("avn", P.val("avn")), ("bcm", P.val("bcm"))],
                        inc=[("rcv", 1)])
                    P.emit("t", lambda e: e.matmul(
                        ps_bc[:, :], ones_bf[:, 0:64], rcp[:, :], start=True, stop=True),
                        wait=[("rcv", P.val("rcv")), P.lag("rpv", 0)], inc=[("bcm", 1)])
                    P.emit("v", lambda e: e.tensor_copy(rep64[:, :], ps_bc[:, :]),
                           wait=[("bcm", P.val("bcm"))], inc=[("rpv", 1)])
                    P.emit("v", lambda e, par=par, ht=ht, cs=cs, sl=h % 2: e.tensor_mul(
                        attnT[ht][par * 64:par * 64 + 64, cs],
                        avcp[sl][0:64, :], rep64[:, :]),
                        wait=[("rpv", P.val("rpv"))], inc=[("anm", 1)])
            P.flush("phC3")

        with ExitStack() as c4:
            C = c4.enter_context
            wbuf2 = [C(nc.sbuf_tensor(f"C_wbuf2_{i}", [128, D], BF16)) for i in range(2)]
            xc = [C(nc.sbuf_tensor(f"C_xc{i}", [128, 512], F32)) for i in range(4)]
            outst = [C(nc.sbuf_tensor(f"C_outst{i}", [128, 512], F32)) for i in range(2)]
            ppo = [C(nc.psum_tensor(f"C_ppo{i}", [128, 512], F32)) for i in range(2)]
            for ob in range(DT):
                slot = ob % 2
                P.emit("g", lambda e, ob=ob, slot=slot: e.dma_start(
                    wbuf2[slot][:, :], wo[ob * 128:(ob + 1) * 128, :]),
                    wait=[P.lag("oev", 4)], inc=[("owl", 16)])
                for c in range(4):
                    cs = slice(c * 512, (c + 1) * 512)
                    P.emit("y", lambda e, ob=ob, cs=cs, xs=c % 4: e.dma_start(
                        xc[xs][:, :], xin[ob * 128:(ob + 1) * 128, cs]),
                        wait=[P.lag("oev", 2)], inc=[("lxc", 16)])
                    for t in range(DT):
                        P.emit("t", lambda e, slot=slot, t=t, cs=cs, c=c: e.matmul(
                            ppo[c % 2][:, :], wbuf2[slot][:, t * 128:(t + 1) * 128],
                            attnT[t][:, cs], start=(t == 0), stop=(t == DT - 1)),
                            wait=[("owl", P.val("owl")), P.lag("oev", 1)],
                            inc=[("omm", 1)] if t == DT - 1 else ())
                    P.emit("v", lambda e, ob=ob, c=c: e.scalar_tensor_tensor(
                        outst[c % 2][:, :], ppo[c % 2][:, :], bo_sb[:, ob:ob + 1],
                        xc[c % 4][:, :], ALU.add, ALU.add),
                        wait=[("omm", P.val("omm")), ("lxc", P.val("lxc")),
                              P.lag("ost", 16)],
                        inc=[("oev", 1)])
                    P.emit("y", lambda e, ob=ob, cs=cs, c=c: e.dma_start(
                        xoT[ob * 128:(ob + 1) * 128, cs], outst[c % 2][:, :]),
                        wait=[("oev", P.val("oev"))], inc=[("ost", 16)])
            P.emit("y", None, wait=[("ost", P.val("ost"))])
            P.flush("phC4")


def prep_l1_attn(ins):
    w = {}
    w["wq"] = host_blocks(_c(ins["L1_attn_Wq"].T))
    w["wk"] = host_blocks(_c(ins["L1_attn_Wk"].T))
    w["wv"] = host_blocks(_c(ins["L1_attn_Wv"].T))
    w["wo"] = host_blocks(_c(ins["L1_attn_Wo"].T))
    w["bqv"] = host_vec(ins["L1_attn_bq"])
    w["bkv"] = host_vec(ins["L1_attn_bk"])
    w["bvv"] = host_vec(ins["L1_attn_bv"])
    w["bov"] = host_vec(ins["L1_attn_bo"])
    return w





NE = 8
DH2 = 2048
DH2T = DH2 // 128


def emit_D(nc, P, ctx, xin, xout, debug=False):
    wr = nc.dram_tensor("D_wr", [D, NE], F32, kind="ExternalInput").ap()
    brv = nc.dram_tensor("D_brv", [1, NE], F32, kind="ExternalInput").ap()
    w1e = nc.dram_tensor("D_w1e", [NE * 128, DH2T * D], F32, kind="ExternalInput").ap()
    w2e = nc.dram_tensor("D_w2e", [NE * 128, DT * DH2], F32, kind="ExternalInput").ap()
    b1v = nc.dram_tensor("D_b1v", [128, NE * DH2T], F32, kind="ExternalInput").ap()
    b2v = nc.dram_tensor("D_b2v", [NE, D], F32, kind="ExternalInput").ap()
    xoT = xout
    if debug:
        dbg_mk = nc.dram_tensor("D_dbg_mk", [NE, 512], F32, kind="ExternalOutput").ap()
        dbg_mk4 = nc.dram_tensor("D_dbg_mk4", [4 * NE, 512], F32, kind="ExternalOutput").ap()
        dbg_mk5 = nc.dram_tensor("D_dbg_mk5", [4 * NE, 512], F32, kind="ExternalOutput").ap()
        dbg_y0 = nc.dram_tensor("D_dbg_y0", [D, 512], F32, kind="ExternalOutput").ap()
        dbg_h = nc.dram_tensor("D_dbg_h", [DH2, 512], F32, kind="ExternalOutput").ap()
        dbg_ya = nc.dram_tensor("D_dbg_ya", [D, 512], F32, kind="ExternalOutput").ap()
        dbg_xl = nc.dram_tensor("D_dbg_xl", [D, 512], F32, kind="ExternalOutput").ap()
        dbg_lg = nc.dram_tensor("D_dbg_lg", [128, NE], F32, kind="ExternalOutput").ap()
        dbg_mx = nc.dram_tensor("D_dbg_mx", [128, 1], F32, kind="ExternalOutput").ap()
        dbg_ms = nc.dram_tensor("D_dbg_ms", [128, NE], F32, kind="ExternalOutput").ap()
        dbg_mb = nc.dram_tensor("D_dbg_mb", [128, NE], F32, kind="ExternalOutput").ap()
        dbg_r0 = nc.dram_tensor("D_dbg_r0", [128, 4 * NE + 4], F32, kind="ExternalOutput").ap()
        dbg_st = nc.dram_tensor("D_dbg_st", [4, S], F32, kind="ExternalOutput").ap()
        dbg_x0 = nc.dram_tensor("D_dbg_x0", [D, 512], F32, kind="ExternalOutput").ap()
        dbg_mr = nc.dram_tensor("D_dbg_mr", [128, 512], F32, kind="ExternalOutput").ap()

    id128 = nc.inline_tensor(np.eye(128, dtype=BF), "D_id128")
    e8 = np.zeros((NE, NE * 128), dtype=BF)
    for i in range(NE):
        e8[i, i * 128:(i + 1) * 128] = 1.0
    e8sel = nc.inline_tensor(e8, "D_e8sel")

    with ExitStack() as bctx:
        E = bctx.enter_context

        wr_sb = E(nc.sbuf_tensor("D_wr_sb", [128, DT * NE], F32))
        br_sb = E(nc.sbuf_tensor("D_br_sb", [1, NE], F32))
        brb = E(nc.sbuf_tensor("D_brb", [128, NE], F32))
        b1_sb = E(nc.sbuf_tensor("D_b1_sb", [128, NE * DH2T], F32))
        b2_sb = E(nc.sbuf_tensor("D_b2_sb", [NE, D], BF16))
        id_sb = E(nc.sbuf_tensor("D_id_sb", [128, 128], BF16))
        e8_sb = E(nc.sbuf_tensor("D_e8_sb", [NE, NE * 128], BF16))
        ones_f32 = E(nc.sbuf_tensor("D_ones_f32", [128, 1], F32))
        ones_f32r = E(nc.sbuf_tensor("D_ones_f32r", [1, 128], F32))
        eps_sb = E(nc.sbuf_tensor("D_eps_sb", [1, 1], F32))
        m_rep = E(nc.sbuf_tensor("D_m_rep", [128, S], F32))
        r_rep = E(nc.sbuf_tensor("D_r_rep", [128, S], F32))

        for t in range(DT):
            P.emit("y", lambda e, t=t: e.dma_start(
                wr_sb[:, t * NE:(t + 1) * NE], wr[t * 128:(t + 1) * 128, :]),
                inc=[("ldm", 16)])
        P.emit("y", lambda e: e.dma_start(br_sb[:, :], brv), inc=[("ldm", 16)])
        P.emit("y", lambda e: e.dma_start(b1_sb[:, :], b1v), inc=[("ldm", 16)])
        P.emit("g", lambda e: e.dma_start(b2_sb[:, :], b2v), inc=[("ldm", 16)])
        P.emit("g", lambda e: e.dma_start(id_sb[:, :], id128.ap()), inc=[("ldm", 16)])
        P.emit("g", lambda e: e.dma_start(e8_sb[:, :], e8sel.ap()), inc=[("ldm", 16)])
        LDMD = 13 * 16
        P.emit("v", lambda e: e.memset(ones_f32[:, :], 1.0), inc=[("ldv", 1)])
        P.emit("v", lambda e: e.memset(ones_f32r[:, :], 1.0), inc=[("ldv", 1)])
        P.emit("v", lambda e: e.memset(eps_sb[:, :], EPS), inc=[("ldv", 1)])

        with ExitStack() as c1:
            C = c1.enter_context
            xr = [C(nc.sbuf_tensor(f"D_xr{i}", [128, S], F32)) for i in range(2)]
            sq_sb = C(nc.sbuf_tensor("D_sq", [128, S], F32))
            mv = C(nc.sbuf_tensor("D_mv", [1, S], F32))
            vv = C(nc.sbuf_tensor("D_vv", [1, S], F32))
            with ExitStack() as c1s:
                stat_ps = c1s.enter_context(nc.psum_tensor("D_stat_ps", [1, S], F32))
                stat_ps2 = c1s.enter_context(nc.psum_tensor("D_stat_ps2", [1, S], F32))
                P.gate(["t"], ["ldv"])
                for t in range(DT):
                    P.emit("y", lambda e, t=t, sl=t % 2: e.dma_start(
                        xr[sl][:, :], xin[t * 128:(t + 1) * 128, :]),
                        wait=[P.lag("sxu", 0), P.lag("ssq", 0)], inc=[("ldx", 16)])
                    P.emit("s", lambda e, sl=t % 2: e.activation(
                        sq_sb[:, :], xr[sl][:, :], AF.Square),
                        wait=["ldx", ("squ", P.val("squ"))],
                        inc=[("ssq", 1)])
                    for c in range(4):
                        P.emit("t", lambda e, t=t, c=c, sl=t % 2: e.matmul(
                            stat_ps[:, c * 512:(c + 1) * 512], ones_f32[:, :],
                            xr[sl][:, c * 512:(c + 1) * 512],
                            start=(t == 0), stop=(t == DT - 1)),
                            wait=["ldx"],
                            inc=[("sxu", 1)] if c == 3 else ())
                    for c in range(4):
                        P.emit("t", lambda e, t=t, c=c: e.matmul(
                            stat_ps2[:, c * 512:(c + 1) * 512], ones_f32[:, :],
                            sq_sb[:, c * 512:(c + 1) * 512],
                            start=(t == 0), stop=(t == DT - 1)),
                            wait=[("ssq", P.val("ssq"))],
                            inc=[("squ", 1)] if c == 3 else ())
                P.emit("v", lambda e: e.tensor_scalar(
                    mv[:, :], stat_ps[:, :], 1.0 / D, None, ALU.mult),
                    wait=["sxu", "squ"], inc=[("lnv", 1)])
                P.emit("v", lambda e: e.tensor_scalar(
                    vv[:, :], stat_ps2[:, :], 1.0 / D, None, ALU.mult),
                    wait=["sxu", "squ"], inc=[("lnv", 1)])
                P.emit("v", lambda e: e.scalar_tensor_tensor(
                    sq_sb[0:1, 0:S], mv[0:1, :], -1.0, mv[0:1, :], ALU.mult, ALU.mult),
                    wait=["lnv"], inc=[("lnv", 1)])
                P.emit("v", lambda e: e.tensor_add(vv[:, :], vv[:, :], sq_sb[0:1, 0:S]),
                       wait=["lnv"], inc=[("lnv", 1)])
                P.emit("s", lambda e: e.activation(
                    vv[:, :], vv[:, :], AF.Sqrt, bias=eps_sb[0:1, 0:1]),
                    wait=["lnv", "ldv"], inc=[("lnv", 1)])
                P.emit("v", lambda e: e.reciprocal(vv[:, :], vv[:, :]),
                       wait=["lnv"], inc=[("rstd", 1)])
                P.flush("phD1a")
            with ExitStack() as c1b:
                rep_ps = c1b.enter_context(nc.psum_tensor("D_rep_ps", [128, 512], F32))
                rep_ps2 = c1b.enter_context(nc.psum_tensor("D_rep_ps2", [128, 512], F32))
                emit_rep(P, rep_ps, rep_ps2, mv, vv, m_rep, r_rep, ones_f32r, "D")
                P.emit("t", lambda e: e.matmul(
                    rep_ps[:, 0:NE], ones_f32r[0:1, :], br_sb[0:1, :],
                    start=True, stop=True),
                    wait=["ldm", "mrv"], inc=[("brt", 1)])
                P.emit("v", lambda e: e.tensor_copy(brb[:, :], rep_ps[:, 0:NE]),
                       wait=["brt"], inc=[("brv2", 1)])
                P.flush("phD1b")

        with ExitStack() as c2:
            C = c2.enter_context
            x4c = [C(nc.sbuf_tensor(f"D_x4c{t}", [128, 512], F32)) for t in range(DT)]
            xln = [C(nc.sbuf_tensor(f"D_xln{t}", [128, 512], BF16)) for t in range(DT)]
            xlf = [C(nc.sbuf_tensor(f"D_xlf{t}", [128, 512], F32)) for t in range(DT)]
            scr = C(nc.sbuf_tensor("D_scr", [128, 512], F32))
            hT = [C(nc.sbuf_tensor(f"D_hT{i}", [128, 512], BF16)) for i in range(DH2T)]
            rl = [C(nc.sbuf_tensor(f"D_rl{i}", [128, 512], BF16)) for i in range(2)]
            w1sb = C(nc.sbuf_tensor("D_w1sb", [128, DH2T * D], BF16))
            w2sb = C(nc.sbuf_tensor("D_w2sb", [128, DT * DH2], BF16))
            y_acc = [C(nc.sbuf_tensor(f"D_yacc{t}", [128, 512], F32)) for t in range(DT)]
            mrep = [C(nc.sbuf_tensor(f"D_mrep{i}", [128, 512], BF16)) for i in range(2)]
            maskpT = C(nc.sbuf_tensor("D_maskpT", [NE, 512], BF16))
            lgt = C(nc.sbuf_tensor("D_lgt", [128, NE], F32))
            mrow = C(nc.sbuf_tensor("D_mrow", [128, 1], F32))
            nmrow = C(nc.sbuf_tensor("D_nmrow", [128, 1], F32))
            msk = C(nc.sbuf_tensor("D_msk", [128, NE], F32))
            exr = C(nc.sbuf_tensor("D_exr", [128, NE], F32))
            smr = C(nc.sbuf_tensor("D_smr", [128, 1], F32))
            mpr = C(nc.sbuf_tensor("D_mpr", [128, 1], F32))
            mkb = C(nc.sbuf_tensor("D_mkb", [128, NE], BF16))
            stl_v = C(nc.sbuf_tensor("D_stl_v", [128, 8], F32))
            stl_s = C(nc.sbuf_tensor("D_stl_s", [128, 8], F32))
            outst = [C(nc.sbuf_tensor(f"D_outst{i}", [128, 512], F32)) for i in range(2)]
            pr = C(nc.psum_tensor("D_pr", [128, NE], F32))
            tp8 = C(nc.psum_tensor("D_tp8", [NE, 128], BF16))
            pm = C(nc.psum_tensor("D_pm", [128, 512], F32))
            ph = [C(nc.psum_tensor(f"D_ph{i}", [128, 512], F32)) for i in range(2)]
            py = [C(nc.psum_tensor(f"D_py{i}", [128, 512], F32)) for i in range(2)]

            P.gate(["s", "v", "g", "t"], ["ldm", "rstd", "brv2"])
            if debug:
                P.emit("g", lambda e: e.dma_start(dbg_st[0:1, :], m_rep[0:1, :]),
                       wait=["mrv"], inc=[("dbg1", 16)])
                P.emit("g", lambda e: e.dma_start(dbg_st[1:2, :], r_rep[0:1, :]),
                       wait=["rrv"], inc=[("dbg1", 16)])
                P.emit("g", lambda e: e.dma_start(dbg_st[2:3, :], m_rep[64:65, :]),
                       wait=["mrv"], inc=[("dbg1", 16)])
                P.emit("g", lambda e: e.dma_start(dbg_st[3:4, :], r_rep[64:65, :]),
                       wait=["rrv"], inc=[("dbg1", 16)])

            for ch in range(4):
                cs = slice(ch * 512, (ch + 1) * 512)
                for t in range(DT):
                    P.emit("y", lambda e, t=t, cs=cs: e.dma_start(
                        x4c[t][:, :], xin[t * 128:(t + 1) * 128, cs]),
                        wait=[P.lag("dov", 0)], inc=[("lxc", 16)])
                    P.emit("v", lambda e, t=t, cs=cs: e.tensor_sub(
                        scr[:, :], x4c[t][:, :], m_rep[:, cs]),
                        wait=[("lxc", P.val("lxc")), "mrv", "rrv",
                              ("dxf", P.val("dxf")), ("dhm", P.val("dhm")),
                              ("drm", P.val("drm")), ("dbg1", P.val("dbg1"))],
                        inc=[("dxs", 1)])
                    P.emit("v", lambda e, t=t, cs=cs: e.tensor_mul(
                        xlf[t][:, :], scr[:, :], r_rep[:, cs]),
                        wait=[("dxs", P.val("dxs"))], inc=[("dxf", 1)])
                    P.emit("v", lambda e, t=t: e.tensor_copy(
                        xln[t][:, :], xlf[t][:, :]),
                        wait=[("dxf", P.val("dxf")), ("dxl", P.val("dxl"))],
                        inc=[("dxl", 1)])
                for tt in range(4):
                    ts_ = slice(tt * 128, (tt + 1) * 128)
                    for ct in range(DT):
                        P.emit("t", lambda e, ct=ct, ts_=ts_: e.matmul(
                            pr[:, :], xlf[ct][:, ts_], wr_sb[:, ct * NE:(ct + 1) * NE],
                            start=(ct == 0), stop=(ct == DT - 1)),
                            wait=[("dxf", P.val("dxf")), ("dlg", P.val("dlg"))],
                            inc=[("drm", 1)] if ct == DT - 1 else ())
                    P.emit("v", lambda e: e.tensor_add(lgt[:, :], brb[:, :], pr[:, :]),
                           wait=[("drm", P.val("drm")), ("dtse", P.val("dtse")),
                                 ("dbg", P.val("dbg")), ("dbg0", P.val("dbg0"))],
                           inc=[("dlg", 1)])
                    P.emit("v", lambda e: e.reduce_max(
                        mrow[:, :], lgt[:, :], axis=mybir.AxisListType.X),
                        wait=[("dlg", P.val("dlg"))], inc=[("dmx", 1)])
                    P.emit("v", lambda e: e.tensor_scalar(
                        nmrow[:, :], mrow[:, :], -1.0, None, ALU.mult),
                        wait=[("dmx", P.val("dmx"))], inc=())
                    P.emit("v", lambda e: e.memset(stl_v[:, :], 0.0),
                           wait=(), inc=[("dnm", 1)])
                    P.emit("g", lambda e: e.tensor_scalar(
                        msk[:, :], lgt[:, :], mrow[:, 0:1], None, ALU.is_ge),
                        wait=[("dnm", P.val("dnm"))], inc=[("dms", 1)])
                    P.emit("s", lambda e: e.activation(
                        exr[:, :], lgt[:, :], AF.Exp, bias=nmrow[:, 0:1]),
                        wait=[("dms", P.val("dms"))], inc=())
                    P.emit("s", lambda e: e.activation(
                        stl_s[:, :], exr[:, :], AF.Square),
                        wait=(), inc=[("dex", 1)])
                    P.emit("v", lambda e: e.reduce_sum(
                        smr[:, :], exr[:, :], axis=mybir.AxisListType.X),
                        wait=[("dex", P.val("dex"))], inc=[("dsm", 1)])
                    P.emit("v", lambda e: e.reciprocal(mpr[:, :], smr[:, :]),
                           wait=[("dsm", P.val("dsm"))], inc=())
                    P.emit("v", lambda e: e.memset(stl_v[:, :], 0.0),
                           wait=(), inc=[("dmp", 1)])
                    P.emit("g", lambda e: e.tensor_scalar(
                        mkb[:, :], msk[:, :], mpr[:, 0:1], None, ALU.mult),
                        wait=[("dmp", P.val("dmp"))], inc=[("dmk", 1)])
                    if debug and ch == 0 and tt == 0:
                        P.emit("g", lambda e: e.dma_start(dbg_r0[:, 0:NE], lgt[:, :]),
                               wait=[("dlg", P.val("dlg"))], inc=[("dbg0", 16)])
                        P.emit("g", lambda e: e.dma_start(dbg_r0[:, NE:NE + 1], mrow[:, :], allow_slow_non_contiguous=True),
                               wait=[("dnm", P.val("dnm"))], inc=[("dbg0", 16)])
                        P.emit("g", lambda e: e.dma_start(dbg_r0[:, NE + 1:2 * NE + 1], msk[:, :]),
                               wait=[("dms", P.val("dms"))], inc=[("dbg0", 16)])
                        P.emit("g", lambda e: e.dma_start(dbg_r0[:, 2 * NE + 1:3 * NE + 1], exr[:, :]),
                               wait=[("dex", P.val("dex"))], inc=[("dbg0", 16)])
                        P.emit("g", lambda e: e.dma_start(dbg_r0[:, 3 * NE + 1:3 * NE + 2], smr[:, :], allow_slow_non_contiguous=True),
                               wait=[("dsm", P.val("dsm"))], inc=[("dbg0", 16)])
                        P.emit("g", lambda e: e.dma_start(dbg_r0[:, 3 * NE + 2:3 * NE + 3], mpr[:, :], allow_slow_non_contiguous=True),
                               wait=[("dmp", P.val("dmp"))], inc=[("dbg0", 16)])
                        P.emit("g", lambda e: e.dma_start(dbg_r0[:, 3 * NE + 3:4 * NE + 3], mkb[:, :]),
                               wait=[("dmk", P.val("dmk"))], inc=[("dbg0", 16)])
                    if debug and ch == 0:
                        pass
                    if debug and ch == 3 and tt == 0:
                        P.emit("g", lambda e: e.dma_start(dbg_lg, lgt[:, :]),
                               wait=[("dlg", P.val("dlg"))], inc=[("dbg", 16)])
                        P.emit("g", lambda e: e.dma_start(dbg_mx, mrow[:, :]),
                               wait=[("dmx", P.val("dmx"))], inc=[("dbg", 16)])
                        P.emit("g", lambda e: e.dma_start(dbg_ms, msk[:, :]),
                               wait=[("dms", P.val("dms"))], inc=[("dbg", 16)])
                        P.emit("g", lambda e: e.dma_start(dbg_mb, mkb[:, :]),
                               wait=[("dmk", P.val("dmk"))], inc=[("dbg", 16)])
                    P.emit("t", lambda e: e.transpose(
                        tp8[:, :], mkb[:, 0:NE], id_sb[:, :]),
                        wait=[("dmk", P.val("dmk")), ("dtse", P.val("dtse"))],
                        inc=[("dtp", 1)])
                    P.emit("v", lambda e, ts_=ts_: e.tensor_copy(
                        maskpT[:, ts_], tp8[:, :]),
                        wait=[("dtp", P.val("dtp")), ("dmr", P.val("dmr")),
                              ("db2", P.val("db2")), ("dbgm", P.val("dbgm"))],
                        inc=[("dtse", 1)])
                for do in range(DT):
                    P.emit("t", lambda e, do=do: e.matmul(
                        py[do % 2][:, :], b2_sb[:, do * 128:(do + 1) * 128],
                        maskpT[:, :], start=True, stop=True),
                        wait=[("dtse", P.val("dtse")), P.lag("dyv", 1)],
                        inc=[("db2", 1)])
                    P.emit("v", lambda e, do=do: e.tensor_copy(
                        y_acc[do][:, :], py[do % 2][:, :]),
                        wait=[("db2", P.val("db2")), ("dov", P.val("dov"))],
                        inc=[("dyv", 1)])
                if debug:
                    P.emit("g", lambda e, ch=ch: e.dma_start(
                        dbg_mk4[ch * NE:(ch + 1) * NE, :], maskpT[:, :]),
                        wait=[("dtse", P.val("dtse"))], inc=[("dbgm", 16)])
                if debug and ch == 0:
                    for t in range(DT):
                        P.emit("g", lambda e, t=t: e.dma_start(
                            dbg_x0[t * 128:(t + 1) * 128, :], xlf[t][:, :]),
                            wait=[("dxf", P.val("dxf"))], inc=[("dbg1", 16)])
                if debug and ch == 3:
                    P.emit("g", lambda e: e.dma_start(dbg_mk, maskpT[:, :]),
                           wait=[("dtse", P.val("dtse"))], inc=[("dbg", 16)])
                    for t in range(DT):
                        P.emit("g", lambda e, t=t: e.dma_start(
                            dbg_xl[t * 128:(t + 1) * 128, :], xlf[t][:, :]),
                            wait=[("dxf", P.val("dxf"))], inc=[("dbg", 16)])
                    for do in range(DT):
                        P.emit("g", lambda e, do=do: e.dma_start(
                            dbg_y0[do * 128:(do + 1) * 128, :], y_acc[do][:, :]),
                            wait=[("dyv", P.val("dyv"))], inc=[("dbg", 16)])
                for ex in range(NE):
                    for part in range(4):
                        P.emit("g", lambda e, ex=ex, part=part: e.dma_start(
                            w1sb[:, part * 4096:(part + 1) * 4096],
                            w1e[ex * 128:(ex + 1) * 128,
                                part * 4096:(part + 1) * 4096]),
                            wait=[("dhm", P.val("dhm"))], inc=[("dw1", 16)])
                    for part in range(4):
                        P.emit("g", lambda e, ex=ex, part=part: e.dma_start(
                            w2sb[:, part * 4096:(part + 1) * 4096],
                            w2e[ex * 128:(ex + 1) * 128,
                                part * 4096:(part + 1) * 4096]),
                            wait=[("dym", P.val("dym"))], inc=[("dw2", 16)])
                    P.emit("t", lambda e, ex=ex: e.matmul(
                        pm[:, :], e8_sb[:, ex * 128:(ex + 1) * 128], maskpT[:, :],
                        start=True, stop=True),
                        wait=[("dtse", P.val("dtse")), P.lag("dmrv", 0)],
                        inc=[("dmr", 1)] + ([] if ex < NE - 1 else []))
                    P.emit("v", lambda e, sl=ex % 2: e.tensor_copy(
                        mrep[sl][:, :], pm[:, :]),
                        wait=[("dmr", P.val("dmr")), P.lag("dhtv", DH2T)],
                        inc=[("dmrv", 1)])
                    for hb in range(DH2T):
                        for t in range(DT):
                            P.emit("t", lambda e, hb=hb, t=t: e.matmul(
                                ph[hb % 2][:, :],
                                w1sb[:, hb * D + t * 128:hb * D + (t + 1) * 128],
                                xln[t][:, :], start=(t == 0), stop=(t == DT - 1)),
                                wait=[("dw1", P.val("dw1")), P.lag("dhrl", 1),
                                      ("dxl", P.val("dxl"))],
                                inc=[("dhm", 1)] if t == DT - 1 else ())
                        P.emit("s", lambda e, ex=ex, hb=hb, sl=hb % 2: e.activation(
                            rl[sl][:, :], ph[hb % 2][:, :], AF.Relu,
                            bias=b1_sb[:, ex * DH2T + hb:ex * DH2T + hb + 1]),
                            wait=[("dhm", P.val("dhm")), P.lag("dhtv", 1)],
                            inc=[("dhrl", 1)])
                        P.emit("v", lambda e, hb=hb, sl=hb % 2, msl=ex % 2: e.tensor_mul(
                            hT[hb][:, :], rl[sl][:, :], mrep[msl][:, :]),
                            wait=[("dhrl", P.val("dhrl")), ("dmrv", P.val("dmrv")),
                                  ("dym", P.val("dym"))],
                            inc=[("dhtv", 1)])
                    if debug and ch == 3 and ex == 7:
                        for hb in range(DH2T):
                            P.emit("g", lambda e, hb=hb: e.dma_start(
                                dbg_h[hb * 128:(hb + 1) * 128, :], hT[hb][:, :]),
                                wait=[("dhtv", P.val("dhtv"))], inc=[("dbg", 16)])
                        P.emit("g", lambda e: e.dma_start(dbg_mr, mrep[0][:, :]),
                               wait=[("dmrv", P.val("dmrv"))], inc=[("dbg", 16)])
                    for do in range(DT):
                        for hb in range(DH2T):
                            P.emit("t", lambda e, do=do, hb=hb: e.matmul(
                                py[do % 2][:, :],
                                w2sb[:, do * DH2 + hb * 128:do * DH2 + (hb + 1) * 128],
                                hT[hb][:, :], start=(hb == 0), stop=(hb == DH2T - 1)),
                                wait=[("dw2", P.val("dw2")), ("dhtv", P.val("dhtv")),
                                      P.lag("dyv", 1)],
                                inc=[("dym", 1)] if hb == DH2T - 1 else ())
                        P.emit("v", lambda e, do=do: e.tensor_add(
                            y_acc[do][:, :], y_acc[do][:, :], py[do % 2][:, :]),
                            wait=[("dym", P.val("dym")), ("dbg", P.val("dbg"))],
                            inc=[("dyv", 1)])
                if debug:
                    P.emit("g", lambda e, ch=ch: e.dma_start(
                        dbg_mk5[ch * NE:(ch + 1) * NE, :], maskpT[:, :]),
                        wait=[("dyv", P.val("dyv"))], inc=[("dbgm", 16)])
                if debug and ch == 3:
                    for do in range(DT):
                        P.emit("g", lambda e, do=do: e.dma_start(
                            dbg_ya[do * 128:(do + 1) * 128, :], y_acc[do][:, :]),
                            wait=[("dyv", P.val("dyv"))], inc=[("dbg", 16)])
                for do in range(DT):
                    P.emit("v", lambda e, do=do, sl=do % 2: e.tensor_add(
                        outst[sl][:, :], y_acc[do][:, :], x4c[do][:, :]),
                        wait=[("dyv", P.val("dyv")), ("lxc", P.val("lxc")),
                              P.lag("dst", 16)],
                        inc=[("dov", 1)])
                    P.emit("y", lambda e, do=do, cs=cs, sl=do % 2: e.dma_start(
                        xoT[do * 128:(do + 1) * 128, cs], outst[sl][:, :]),
                        wait=[("dov", P.val("dov"))], inc=[("dst", 16)])
            P.emit("y", None, wait=[("dst", P.val("dst"))])
            if debug:
                P.emit("g", None, wait=[("dbg", P.val("dbg"))])
            P.flush("phD2")


def prep_l1_ffn(ins):
    w = {}
    w["wr"] = _c(ins["L1_ffn_Wr"])
    w["brv"] = _c(ins["L1_ffn_br"].reshape(1, NE))
    w["w1e"] = _c(np.concatenate([host_pwide(host_blocks(_c(ins["L1_ffn_W1"][e])))
                                  for e in range(NE)], axis=0))
    w["w2e"] = _c(np.concatenate([host_pwide(host_blocks(_c(ins["L1_ffn_W2"][e])))
                                  for e in range(NE)], axis=0))
    w["b1v"] = _c(np.concatenate([host_vec(ins["L1_ffn_b1"][e]) for e in range(NE)],
                                 axis=1))
    w["b2v"] = _c(ins["L1_ffn_b2"])
    return w



def prep_l0_attn(ins):
    w = {}
    w["wq"] = host_blocks(_c(ins["L0_attn_Wq"].T))
    w["wk"] = host_blocks(_c(ins["L0_attn_Wk"].T))
    w["wv"] = host_blocks(_c(ins["L0_attn_Wv"].T))
    w["wo"] = host_blocks(_c(ins["L0_attn_Wo"].T))
    w["bqv"] = host_vec(ins["L0_attn_bq"])
    w["bkv"] = host_vec(ins["L0_attn_bk"])
    w["bvv"] = host_vec(ins["L0_attn_bv"])
    w["bov"] = host_vec(ins["L0_attn_bo"])
    w["ri"] = _c(ins["L0_attn_rel_W"][:, :D].T)
    w["rj"] = _c(ins["L0_attn_rel_W"][:, D:].T)
    w["relb"] = _c(ins["L0_attn_rel_b"].reshape(3, 1))
    w["lbT"] = _c(ins["L0_attn_logical_bias"].T)
    w["gw"] = _c(ins["L0_attn_gate_W"].T)
    w["gbv"] = _c(ins["L0_attn_gate_b"].reshape(H, 1))
    return w


def build_fused(debug=False):
    nc = bass.Bass("TRN2", target_bir_lowering=False)
    xT = nc.dram_tensor("xT", [D, S], F32, kind="ExternalInput").ap()
    xoT = nc.dram_tensor("xoT", [D, S], F32, kind="ExternalOutput").ap()
    x2scr = nc.dram_tensor("x2scr", [D, S], F32, kind="Internal").ap()
    x3scr = nc.dram_tensor("x3scr", [D, S], F32, kind="Internal").ap()
    x4scr = nc.dram_tensor("x4scr", [D, S], F32, kind="Internal").ap()
    with ExitStack() as ctx:
        P = Plan(nc, ctx)
        emit_A(nc, P, ctx, xT, x2scr, debug=debug)
        emit_B(nc, P, ctx, x2scr, x3scr, debug=debug)
        emit_C(nc, P, ctx, x3scr, x4scr, debug=debug)
        emit_D(nc, P, ctx, x4scr, xoT, debug=debug)
    return nc


def kernel(**inputs):
    ins = {k: np.asarray(v, dtype=np.float32) for k, v in inputs.items()}
    x = ins["L0_attn_hidden_states"]
    w = {}
    for pfx, prep in (("A", prep_l0_attn), ("B", prep_l0_ffn),
                      ("C", prep_l1_attn), ("D", prep_l1_ffn)):
        for k, v in prep(ins).items():
            w[f"{pfx}_{k}"] = v
    nc = build_fused()
    maps = [dict(w, xT=_c(x[c].T)) for c in range(NCORE)]
    res = run_bass_kernel_spmd(nc, maps, core_ids=list(range(NCORE)))
    out = np.stack([res.results[c]["xoT"].T for c in range(NCORE)])
    return np.ascontiguousarray(out.astype(np.float32))


# version 03
# speedup vs baseline: 1.2709x; 1.2709x over previous
import numpy as np
import ml_dtypes
from contextlib import ExitStack

import concourse.bass as bass
import concourse.mybir as mybir
from concourse.bass_utils import run_bass_kernel_spmd

F32 = mybir.dt.float32
BF16 = mybir.dt.bfloat16
AF = mybir.ActivationFunctionType
ALU = mybir.AluOpType
BF = ml_dtypes.bfloat16

D, S, H, HD = 1024, 2048, 16, 64
DT = D // 128
ST = S // 128
NCORE = 8
EPS = 1e-5


def _c(a):
    return np.ascontiguousarray(a)


def host_blocks(WT):
    IN, OUT = WT.shape
    a = WT.reshape(IN // 128, 128, OUT // 128, 128).transpose(2, 1, 0, 3)
    return _c(a.reshape(OUT, IN))


def host_vec(v):
    return _c(np.asarray(v).reshape(-1, 128).T)


def host_pwide(blocks2d):
    nb = blocks2d.shape[0] // 128
    return _c(blocks2d.reshape(nb, 128, -1).transpose(1, 0, 2).reshape(128, -1))


ENG_ATTR = {"t": "tensor", "v": "vector", "s": "scalar", "g": "gpsimd", "y": "sync"}


class Plan:

    def __init__(self, nc, ctx):
        self.nc = nc
        self.ctx = ctx
        self.ops = []
        self.cnt = {}
        self.handles = {}
        self.maxwait = {}

    def sem(self, name):
        if name not in self.cnt:
            self.cnt[name] = 0
            self.handles[name] = self.ctx.enter_context(self.nc.semaphore(name))
        return name

    def val(self, name):
        return self.cnt.get(name, 0)

    def lag(self, name, k):
        return (name, max(0, self.cnt.get(name, 0) - k))

    def emit(self, eng, fn, wait=(), inc=()):
        waits = []
        for w in wait:
            if isinstance(w, tuple):
                s, v = w
            else:
                s, v = w, self.cnt.get(w, 0)
            self.sem(s)
            if v > 0:
                waits.append((s, v))
        incs = []
        for s, k in inc:
            self.sem(s)
            self.cnt[s] += k
            incs.append((s, k))
        assert len(incs) <= 1, f"hw allows one sync update per instruction: {incs}"
        self.ops.append((eng, fn, waits, incs))

    def gate(self, engines, wait):
        for e in engines:
            self.emit(e, None, wait=wait)

    def flush(self, name=None):
        if not self.ops:
            return
        ops = self.ops
        self.ops = []
        with self.nc.allow_low_precision("bf16 matmul operands / bf16 psum transpose are deliberate"), \
             self.nc.Block() as blk:
            for engname, attr in ENG_ATTR.items():
                eops = [o for o in ops if o[0] == engname]
                if not eops:
                    continue

                def body(eng, eops=eops, engname=engname):
                    seen = self.maxwait.setdefault(engname, {})
                    for _, fn, waits, incs in eops:
                        for s, v in waits:
                            if v > seen.get(s, 0):
                                eng.wait_ge(self.handles[s], v)
                                seen[s] = v
                        inst = fn(eng) if fn is not None else None
                        for s, k in incs:
                            if inst is None:
                                inst = eng.nop()
                            inst.then_inc(self.handles[s], k)

                getattr(blk, attr)(body)



def emit_rep(P, rep_ps, rep_ps2, mv, vv, m_rep, r_rep, ones_f32r, pfx):
    for c in range(4):
        cs = slice(c * 512, (c + 1) * 512)
        P.emit("t", lambda e, cs=cs: e.matmul(
            rep_ps[:, :], ones_f32r[0:1, :], mv[0:1, cs], start=True, stop=True),
            wait=["rstd", P.lag("mrv", 0)], inc=[("mrt", 1)])
        P.emit("v", lambda e, cs=cs: e.tensor_copy(m_rep[:, cs], rep_ps[:, :]),
               wait=["mrt"], inc=[("mrv", 1)])
        P.emit("t", lambda e, cs=cs: e.matmul(
            rep_ps2[:, :], ones_f32r[0:1, :], vv[0:1, cs], start=True, stop=True),
            wait=["rstd", P.lag("rrv", 0)], inc=[("mrt", 1)])
        P.emit("v", lambda e, cs=cs: e.tensor_copy(r_rep[:, cs], rep_ps2[:, :]),
               wait=["mrt"], inc=[("rrv", 1)])



def emit_A(nc, P, ctx, xin, xout, debug=False):
    xT = xin
    wq = nc.dram_tensor("A_wq", [D, D], F32, kind="ExternalInput").ap()
    wk = nc.dram_tensor("A_wk", [D, D], F32, kind="ExternalInput").ap()
    wv = nc.dram_tensor("A_wv", [D, D], F32, kind="ExternalInput").ap()
    wo = nc.dram_tensor("A_wo", [D, D], F32, kind="ExternalInput").ap()
    bqv = nc.dram_tensor("A_bqv", [128, DT], F32, kind="ExternalInput").ap()
    bkv = nc.dram_tensor("A_bkv", [128, DT], F32, kind="ExternalInput").ap()
    bvv = nc.dram_tensor("A_bvv", [128, DT], F32, kind="ExternalInput").ap()
    bov = nc.dram_tensor("A_bov", [128, DT], F32, kind="ExternalInput").ap()
    ri = nc.dram_tensor("A_ri", [D, 3], F32, kind="ExternalInput").ap()
    rj = nc.dram_tensor("A_rj", [D, 3], F32, kind="ExternalInput").ap()
    relb = nc.dram_tensor("A_relb", [3, 1], F32, kind="ExternalInput").ap()
    lbT = nc.dram_tensor("A_lbT", [3, H], F32, kind="ExternalInput").ap()
    gw = nc.dram_tensor("A_gw", [D, H], F32, kind="ExternalInput").ap()
    gbv = nc.dram_tensor("A_gbv", [H, 1], F32, kind="ExternalInput").ap()
    x2T = xout
    if debug:
        dbg_q = nc.dram_tensor("A_dbg_q", [D, S], F32, kind="ExternalOutput").ap()
        dbg_k = nc.dram_tensor("A_dbg_k", [D, S], F32, kind="ExternalOutput").ap()
        dbg_at = nc.dram_tensor("A_dbg_at", [D, S], F32, kind="ExternalOutput").ap()

    id128 = nc.inline_tensor(np.eye(128, dtype=BF), "A_id128")
    e16 = np.zeros((16, 48), dtype=BF)
    for h in range(16):
        e16[h, 3 * h:3 * h + 3] = 1.0
    e16sel = nc.inline_tensor(e16, "A_e16sel")

    with ExitStack() as bctx:
        E = bctx.enter_context

        bq_sb = E(nc.sbuf_tensor("A_bq_sb", [128, DT], F32))
        bk_sb = E(nc.sbuf_tensor("A_bk_sb", [128, DT], F32))
        bv_sb = E(nc.sbuf_tensor("A_bv_sb", [128, DT], F32))
        bo_sb = E(nc.sbuf_tensor("A_bo_sb", [128, DT], F32))
        relb_sb = E(nc.sbuf_tensor("A_relb_sb", [3, 1], F32))
        gb_sb = E(nc.sbuf_tensor("A_gb_sb", [H, 1], F32))
        lbT_sb = E(nc.sbuf_tensor("A_lbT_sb", [3, H], F32))
        id_sb = E(nc.sbuf_tensor("A_id_sb", [128, 128], BF16))
        e16_sb = E(nc.sbuf_tensor("A_e16_sb", [16, 48], BF16))
        ones_f32 = E(nc.sbuf_tensor("A_ones_f32", [128, 1], F32))
        ones_bf = E(nc.sbuf_tensor("A_ones_bf", [1, 128], BF16))
        eps_sb = E(nc.sbuf_tensor("A_eps_sb", [1, 1], F32))
        ones_f32r = E(nc.sbuf_tensor("A_ones_f32r", [1, 128], F32))
        aT = E(nc.sbuf_tensor("A_aT", [3, S], BF16))
        cT = E(nc.sbuf_tensor("A_cT", [3, S], BF16))
        gateT = E(nc.sbuf_tensor("A_gateT", [16, S], BF16))

        for nm, sb, ap_ in (("bq", bq_sb, bqv), ("bk", bk_sb, bkv), ("bv", bv_sb, bvv),
                            ("bo", bo_sb, bov), ("rb", relb_sb, relb),
                            ("gb", gb_sb, gbv), ("lb", lbT_sb, lbT)):
            P.emit("y", lambda e, sb=sb, a=ap_: e.dma_start(sb[:, :], a), inc=[("ldm", 16)])
        P.emit("g", lambda e: e.dma_start(id_sb[:, :], id128.ap()), inc=[("ldm", 16)])
        P.emit("g", lambda e: e.dma_start(e16_sb[:, :], e16sel.ap()), inc=[("ldm", 16)])
        LDM = 9 * 16
        P.emit("v", lambda e: e.memset(ones_f32[:, :], 1.0), inc=[("ldv", 1)])
        P.emit("v", lambda e: e.memset(ones_bf[:, :], 1.0), inc=[("ldv", 1)])
        P.emit("v", lambda e: e.memset(eps_sb[:, :], EPS), inc=[("eps", 1)])
        P.emit("v", lambda e: e.memset(ones_f32r[:, :], 1.0), inc=[("ldv", 1)])
        P.emit("v", lambda e: e.tensor_scalar(bk_sb[:, :], bk_sb[:, :], 0.125, None, ALU.mult),
               wait=["ldm"], inc=[("bk8", 1)])

        qT = [E(nc.sbuf_tensor(f"A_qT{t}", [128, S], BF16)) for t in range(DT)]
        kT = [E(nc.sbuf_tensor(f"A_kT{t}", [128, S], BF16)) for t in range(DT)]
        v_aug = [E(nc.sbuf_tensor(f"A_vaug{j}", [128, H * 65], BF16)) for j in range(ST)]

        cx = bctx.enter_context(ExitStack())
        x_ln = [cx.enter_context(nc.sbuf_tensor(f"A_x_ln{t}", [128, S], BF16))
                for t in range(DT)]

        with ExitStack() as c1:
            C = c1.enter_context
            xr = [C(nc.sbuf_tensor(f"A_xr{i}", [128, S], F32)) for i in range(2)]
            sq_sb = C(nc.sbuf_tensor("A_sq", [128, S], F32))
            mv = C(nc.sbuf_tensor("A_mv", [1, S], F32))
            vv = C(nc.sbuf_tensor("A_vv", [1, S], F32))
            m_rep = C(nc.sbuf_tensor("A_m_rep", [128, S], F32))
            r_rep = C(nc.sbuf_tensor("A_r_rep", [128, S], F32))

            with ExitStack() as c1s:
                stat_ps = c1s.enter_context(nc.psum_tensor("A_stat_ps", [1, S], F32))
                stat_ps2 = c1s.enter_context(nc.psum_tensor("A_stat_ps2", [1, S], F32))
                P.gate(["t"], ["ldv"])
                for t in range(DT):
                    P.emit("y", lambda e, t=t, sl=t % 2: e.dma_start(
                        xr[sl][:, :], xT[t * 128:(t + 1) * 128, :]),
                        wait=[P.lag("sxu", 0), P.lag("ssq", 0)], inc=[("ldx", 16)])
                    P.emit("s", lambda e, t=t, sl=t % 2: e.activation(
                        sq_sb[:, :], xr[sl][:, :], AF.Square),
                        wait=["ldx", ("squ", P.val("squ"))],
                        inc=[("ssq", 1)])
                    for c in range(4):
                        P.emit("t", lambda e, t=t, c=c, sl=t % 2: e.matmul(
                            stat_ps[:, c * 512:(c + 1) * 512], ones_f32[:, :],
                            xr[sl][:, c * 512:(c + 1) * 512],
                            start=(t == 0), stop=(t == DT - 1)),
                            wait=["ldx"],
                            inc=[("sxu", 1)] if c == 3 else ())
                    for c in range(4):
                        P.emit("t", lambda e, t=t, c=c: e.matmul(
                            stat_ps2[:, c * 512:(c + 1) * 512], ones_f32[:, :],
                            sq_sb[:, c * 512:(c + 1) * 512],
                            start=(t == 0), stop=(t == DT - 1)),
                            wait=[("ssq", P.val("ssq"))],
                            inc=[("squ", 1)] if c == 3 else ())
                P.emit("v", lambda e: e.tensor_scalar(
                    mv[:, :], stat_ps[:, :], 1.0 / D, None, ALU.mult),
                    wait=["sxu", "squ"], inc=[("lnv", 1)])
                P.emit("v", lambda e: e.tensor_scalar(
                    vv[:, :], stat_ps2[:, :], 1.0 / D, None, ALU.mult),
                    wait=["sxu", "squ"], inc=[("lnv", 1)])
                P.emit("v", lambda e: e.scalar_tensor_tensor(
                    sq_sb[0:1, 0:S], mv[0:1, :], -1.0, mv[0:1, :], ALU.mult, ALU.mult),
                    wait=["lnv"], inc=[("lnv", 1)])
                P.emit("v", lambda e: e.tensor_add(vv[:, :], vv[:, :], sq_sb[0:1, 0:S]),
                       wait=["lnv"], inc=[("lnv", 1)])
                P.emit("s", lambda e: e.activation(
                    vv[:, :], vv[:, :], AF.Sqrt, bias=eps_sb[0:1, 0:1]),
                    wait=["lnv", "eps"], inc=[("lnv", 1)])
                P.emit("v", lambda e: e.reciprocal(vv[:, :], vv[:, :]),
                       wait=["lnv"], inc=[("rstd", 1)])
                P.flush("ph1a")
            with ExitStack() as c1b:
                rep_ps = c1b.enter_context(nc.psum_tensor("A_rep_ps", [128, 512], F32))
                rep_ps2 = c1b.enter_context(nc.psum_tensor("A_rep_ps2", [128, 512], F32))
                emit_rep(P, rep_ps, rep_ps2, mv, vv, m_rep, r_rep, ones_f32r, "A")
                for t in range(DT):
                    P.emit("y", lambda e, t=t, sl=t % 2: e.dma_start(
                        xr[sl][:, :], xT[t * 128:(t + 1) * 128, :]),
                        wait=[P.lag("xsub", 1)], inc=[("ldx", 16)])
                    P.emit("v", lambda e, t=t, sl=t % 2: e.tensor_sub(
                        sq_sb[:, :], xr[sl][:, :], m_rep[:, :]),
                        wait=["mrv", "rrv", "ldx",
                              ("xlnd", P.val("xlnd"))],
                        inc=[("xsub", 1)])
                    P.emit("v", lambda e, t=t: e.tensor_mul(
                        x_ln[t][:, :], sq_sb[:, :], r_rep[:, :]),
                        wait=[("xsub", P.val("xsub"))], inc=[("xlnd", 1)])
                P.flush("ph1b")

        with ExitStack() as c2:
            C = c2.enter_context
            wbuf = [C(nc.sbuf_tensor(f"A_wbuf{i}", [128, D], BF16)) for i in range(2)]
            rwbuf = C(nc.sbuf_tensor("A_rwbuf", [128, 6 * DT], BF16))
            gwbuf = C(nc.sbuf_tensor("A_gwbuf", [128, H * DT], BF16))
            vT_t = C(nc.sbuf_tensor("A_vT_t", [128, S], BF16))
            pp = [C(nc.psum_tensor(f"A_pp{i}", [128, 512], F32)) for i in range(4)]
            tp = C(nc.psum_tensor("A_tp", [128, 128], BF16))
            p3 = C(nc.psum_tensor("A_p3", [3, 512], F32))
            p16 = C(nc.psum_tensor("A_p16", [16, 512], F32))

            P.gate(["s", "v", "g"], ["ldm", "bk8"])

            for t in range(DT):
                P.emit("g", lambda e, t=t: e.dma_start(
                    rwbuf[:, t * 6:t * 6 + 3], ri[t * 128:(t + 1) * 128, :]), inc=[("ldr", 16)])
                P.emit("g", lambda e, t=t: e.dma_start(
                    rwbuf[:, t * 6 + 3:t * 6 + 6], rj[t * 128:(t + 1) * 128, :]), inc=[("ldr", 16)])
                P.emit("g", lambda e, t=t: e.dma_start(
                    gwbuf[:, t * H:(t + 1) * H], gw[t * 128:(t + 1) * 128, :]), inc=[("ldr", 16)])

            for c in range(4):
                cs = slice(c * 512, (c + 1) * 512)
                for t in range(DT):
                    P.emit("t", lambda e, t=t, cs=cs: e.matmul(
                        p3[:, :], rwbuf[:, t * 6:t * 6 + 3], x_ln[t][:, cs],
                        start=(t == 0), stop=(t == DT - 1)),
                        wait=[("ldr", 48 * DT), ("aev", P.val("aev"))],
                        inc=[("limm", 1)] if t == DT - 1 else ())
                P.emit("s", lambda e, cs=cs: e.activation(
                    aT[:, cs], p3[:, :], AF.Exp, bias=relb_sb[:, 0:1]),
                    wait=[("limm", P.val("limm"))], inc=[("aev", 1)])
                for t in range(DT):
                    P.emit("t", lambda e, t=t, cs=cs: e.matmul(
                        p3[:, :], rwbuf[:, t * 6 + 3:t * 6 + 6], x_ln[t][:, cs],
                        start=(t == 0), stop=(t == DT - 1)),
                        wait=[("aev", P.val("aev"))],
                        inc=[("ljmm", 1)] if t == DT - 1 else ())
                P.emit("s", lambda e, cs=cs: e.activation(
                    cT[:, cs], p3[:, :], AF.Exp),
                    wait=[("ljmm", P.val("ljmm"))], inc=[("aev", 1)])
                for t in range(DT):
                    P.emit("t", lambda e, t=t, cs=cs: e.matmul(
                        p16[:, :], gwbuf[:, t * H:(t + 1) * H], x_ln[t][:, cs],
                        start=(t == 0), stop=(t == DT - 1)),
                        wait=[("gev", P.val("gev"))],
                        inc=[("gmm", 1)] if t == DT - 1 else ())
                P.emit("s", lambda e, cs=cs: e.activation(
                    gateT[:, cs], p16[:, :], AF.Sigmoid, bias=gb_sb[:, 0:1]),
                    wait=[("gmm", P.val("gmm"))], inc=[("gev", 1)])

            def proj_dmajor(wdram, bias_sb, scale, out_tiles, ppbase):
                for ob in range(DT):
                    slot = ob % 2
                    P.emit("g", lambda e, ob=ob, slot=slot, w=wdram: e.dma_start(
                        wbuf[slot][:, :], w[ob * 128:(ob + 1) * 128, :]),
                        wait=[P.lag("pev", 4)], inc=[("wld", 16)])
                    for c in range(4):
                        cs = slice(c * 512, (c + 1) * 512)
                        ps = pp[ppbase + (c % 2)]
                        for t in range(DT):
                            P.emit("t", lambda e, slot=slot, t=t, cs=cs, ps=ps: e.matmul(
                                ps[:, :], wbuf[slot][:, t * 128:(t + 1) * 128],
                                x_ln[t][:, cs], start=(t == 0), stop=(t == DT - 1)),
                                wait=[("wld", P.val("wld")), P.lag("pev", 1)],
                                inc=[("pmm", 1)] if t == DT - 1 else ())
                        P.emit("s", lambda e, ob=ob, cs=cs, ps=ps, b=bias_sb, sc=scale,
                               ot=out_tiles: e.activation(
                            ot[ob][:, cs], ps[:, :], AF.Identity,
                            bias=b[:, ob:ob + 1], scale=sc),
                            wait=[("pmm", P.val("pmm"))], inc=[("pev", 1)])

            proj_dmajor(wq, bq_sb, 1.0, qT, 0)
            proj_dmajor(wk, bk_sb, 0.125, kT, 0)

            for j in range(ST):
                P.emit("v", lambda e, j=j: e.memset(v_aug[j][:, :], 1.0),
                       inc=[("vamem", 1)])
            for ob in range(DT):
                slot = ob % 2
                P.emit("g", lambda e, ob=ob, slot=slot: e.dma_start(
                    wbuf[slot][:, :], wv[ob * 128:(ob + 1) * 128, :]),
                    wait=[P.lag("pev", 4)], inc=[("wld", 16)])
                for c in range(4):
                    cs = slice(c * 512, (c + 1) * 512)
                    ps = pp[2 + (c % 2)]
                    for t in range(DT):
                        P.emit("t", lambda e, slot=slot, t=t, cs=cs, ps=ps: e.matmul(
                            ps[:, :], wbuf[slot][:, t * 128:(t + 1) * 128],
                            x_ln[t][:, cs], start=(t == 0), stop=(t == DT - 1)),
                            wait=[("wld", P.val("wld")), P.lag("pev", 1)],
                            inc=[("vmm", 1)] if t == DT - 1 else ())
                    P.emit("s", lambda e, ob=ob, cs=cs, ps=ps: e.activation(
                        vT_t[:, cs], ps[:, :], AF.Identity, bias=bv_sb[:, ob:ob + 1]),
                        wait=[("vmm", P.val("vmm")), ("vtr", P.val("vtr"))],
                        inc=[("pev", 1)])
                for j in range(ST):
                    P.emit("t", lambda e, j=j: e.transpose(
                        tp[:, :], vT_t[:, j * 128:(j + 1) * 128], id_sb[:, :]),
                        wait=[("pev", P.val("pev")), ("vtrc", P.val("vtrc"))],
                        inc=[("vtr", 1)])
                    P.emit("v", lambda e, ob=ob, j=j: e.tensor_copy(
                        v_aug[j][:, (2 * ob) * 65:(2 * ob) * 65 + 64], tp[:, 0:64]),
                        wait=[("vtr", P.val("vtr")), "vamem"], inc=[("vtrc", 1)])
                    P.emit("v", lambda e, ob=ob, j=j: e.tensor_copy(
                        v_aug[j][:, (2 * ob + 1) * 65:(2 * ob + 1) * 65 + 64],
                        tp[:, 64:128]),
                        wait=[("vtr", P.val("vtr")), "vamem"], inc=[("vtrc", 1)])
            P.flush("ph2")

        cx.close()
        attnT = [E(nc.sbuf_tensor(f"A_attnT{t}", [128, S], BF16)) for t in range(DT)]

        with ExitStack() as c3:
            C = c3.enter_context
            r_t = [C(nc.sbuf_tensor(f"A_r{j}", [128, 512], BF16)) for j in range(ST)]
            E2 = [C(nc.sbuf_tensor(f"A_E2_{i}", [128, 1024], BF16)) for i in range(2)]
            Tb = [C(nc.sbuf_tensor(f"A_Tb{i}", [128, 512], BF16)) for i in range(2)]
            qz = [C(nc.sbuf_tensor(f"A_qz{i}", [128, 512], BF16)) for i in range(2)]
            abuf = C(nc.sbuf_tensor("A_abuf", [3, 512], BF16))
            t1buf = C(nc.sbuf_tensor("A_t1buf", [3, 512], BF16))
            rcp = C(nc.sbuf_tensor("A_rcp", [1, 512], BF16))
            rep64 = C(nc.sbuf_tensor("A_rep64", [64, 512], BF16))
            lnD = C(nc.sbuf_tensor("A_lnD", [128, 512], F32))
            avcp = [C(nc.sbuf_tensor(f"A_avcp{i}", [65, 512], F32)) for i in range(2)]
            ps_S = [C(nc.psum_tensor(f"A_ps_S{i}", [128, 1024], F32)) for i in range(2)]
            ps_N = C(nc.psum_tensor("A_ps_N", [128, 512], F32))
            ps_AV = C(nc.psum_tensor("A_ps_AV", [65, 512], F32))
            ps_g3 = C(nc.psum_tensor("A_ps_g3", [3, 512], F32))
            ps_bc = C(nc.psum_tensor("A_ps_bc", [64, 512], F32))

            P.emit("v", lambda e: e.memset(qz[0][:, :], 0.0), inc=[("qzi", 1)])
            P.emit("v", lambda e: e.memset(qz[1][:, :], 0.0), inc=[("qzi", 1)])

            for ic in range(4):
                cs = slice(ic * 512, (ic + 1) * 512)
                for j in range(ST):
                    P.emit("t", lambda e, j=j, cs=cs: e.matmul(
                        ps_N[:, :], cT[:, j * 128:(j + 1) * 128], aT[:, cs],
                        start=True, stop=True),
                        wait=[("rln", P.val("rln")), ("tmv", P.val("tmv"))],
                        inc=[("dmm", 1)])
                    P.emit("s", lambda e: e.activation(lnD[:, :], ps_N[:, :], AF.Ln),
                           wait=[("dmm", P.val("dmm")), ("rex", P.val("rex"))],
                           inc=[("rln", 1)])
                    P.emit("s", lambda e, j=j: e.activation(
                        r_t[j][:, :], lnD[:, :], AF.Exp, scale=-1.0),
                        wait=[("rln", P.val("rln")), ("tmv", P.val("tmv"))],
                        inc=[("rex", 1)])
                for h in range(H):
                    par = h % 2
                    ht = h // 2
                    P.emit("v", lambda e, par=par, ht=ht, cs=cs: e.tensor_copy(
                        qz[par][par * 64:par * 64 + 64, :],
                        qT[ht][par * 64:par * 64 + 64, cs]),
                        wait=["qzi", P.lag("qzu" + str(par), 0)],
                        inc=[("qzc" + str(par), 1)])
                    P.emit("t", lambda e, h=h, cs=cs: e.matmul(
                        ps_g3[:, :], e16_sb[:, 3 * h:3 * h + 3], gateT[:, cs],
                        start=True, stop=True),
                        wait=[("abv", P.val("abv"))], inc=[("g3m", 1)])
                    P.emit("v", lambda e, h=h, cs=cs: e.tensor_scalar(
                        t1buf[:, :], aT[:, cs], lbT_sb[:, h:h + 1], None, ALU.mult),
                        wait=[("abv", P.val("abv"))], inc=[("t1v", 1)])
                    P.emit("v", lambda e: e.tensor_mul(abuf[:, :], t1buf[:, :], ps_g3[:, :]),
                           wait=[("g3m", P.val("g3m")), ("t1v", P.val("t1v"))],
                           inc=[("abv", 1)])
                    for grp in range(8):
                        half = grp % 2
                        sb = ps_S[half]
                        eh = "exh" + str(half)
                        ah = "avh" + str(half)
                        for jj2 in range(2):
                            j = grp * 2 + jj2
                            col = slice(jj2 * 512, (jj2 + 1) * 512)
                            P.emit("t", lambda e, ht=ht, j=j, col=col, sb=sb, par=par: e.matmul(
                                sb[:, col], kT[ht][:, j * 128:(j + 1) * 128],
                                qz[par][:, :], start=True, stop=False),
                                wait=[("qzc" + str(par), P.val("qzc" + str(par))),
                                      (eh, P.val(eh))],
                                inc=[("qzu" + str(par), 1)] if (grp == 7 and jj2 == 1) else ())
                            P.emit("t", lambda e, j=j: e.matmul(
                                ps_N[:, :], cT[:, j * 128:(j + 1) * 128], abuf[:, :],
                                start=True, stop=True),
                                wait=[("abv", P.val("abv")), ("tmv", P.val("tmv")),
                                      ("rln", P.val("rln"))],
                                inc=[("nmm", 1)])
                            P.emit("v", lambda e, j=j, jj2=jj2: e.tensor_mul(
                                Tb[jj2][:, :], ps_N[:, :], r_t[j][:, :]),
                                wait=[("nmm", P.val("nmm")), ("rex", P.val("rex")),
                                      P.lag("idm", 1)],
                                inc=[("tmv", 1)])
                            P.emit("t", lambda e, sb=sb, col=col, jj2=jj2: e.matmul(
                                sb[:, col], id_sb[:, :], Tb[jj2][:, :],
                                start=False, stop=True),
                                wait=[("tmv", P.val("tmv"))], inc=[("idm", 1)])
                        P.emit("s", lambda e, sb=sb, half=half: e.activation(
                            E2[half][:, :], sb[:, :], AF.Exp),
                            wait=[("idm", P.val("idm")), ("ava", P.val("ava"))],
                            inc=[(eh, 1)])
                        for jj2 in range(2):
                            j = grp * 2 + jj2
                            P.emit("t", lambda e, h=h, j=j, jj2=jj2, half=half, grp=grp: e.matmul(
                                ps_AV[:, :], v_aug[j][:, h * 65:h * 65 + 65],
                                E2[half][:, jj2 * 512:(jj2 + 1) * 512],
                                start=(grp == 0 and jj2 == 0),
                                stop=(grp == 7 and jj2 == 1)),
                                wait=[(eh, P.val(eh)), ("avn", P.val("avn"))],
                                inc=[("ava", 1)] if jj2 == 1 else ())
                    P.emit("v", lambda e, sl=h % 2: e.tensor_copy(
                        avcp[sl][:, :], ps_AV[:, :]),
                        wait=[("ava", P.val("ava")), P.lag("anm", 1)], inc=[("avn", 1)])
                    P.emit("v", lambda e, sl=h % 2: e.reciprocal(rcp[:, :], avcp[sl][64:65, :]),
                           wait=[("avn", P.val("avn")), ("bcm", P.val("bcm"))], inc=[("rcv", 1)])
                    P.emit("t", lambda e: e.matmul(
                        ps_bc[:, :], ones_bf[:, 0:64], rcp[:, :], start=True, stop=True),
                        wait=[("rcv", P.val("rcv")), P.lag("rpv", 0)], inc=[("bcm", 1)])
                    P.emit("v", lambda e: e.tensor_copy(rep64[:, :], ps_bc[:, :]),
                           wait=[("bcm", P.val("bcm"))], inc=[("rpv", 1)])
                    P.emit("v", lambda e, par=par, ht=ht, cs=cs, sl=h % 2: e.tensor_mul(
                        attnT[ht][par * 64:par * 64 + 64, cs],
                        avcp[sl][0:64, :], rep64[:, :]),
                        wait=[("rpv", P.val("rpv"))],
                        inc=[("anm", 1)])
            P.flush("ph3")

        with ExitStack() as c4:
            C = c4.enter_context
            wbuf2 = [C(nc.sbuf_tensor(f"A_wbuf2_{i}", [128, D], BF16)) for i in range(2)]
            xc = [C(nc.sbuf_tensor(f"A_xc{i}", [128, 512], F32)) for i in range(4)]
            outst = [C(nc.sbuf_tensor(f"A_outst{i}", [128, 512], F32)) for i in range(2)]
            ppo = [C(nc.psum_tensor(f"A_ppo{i}", [128, 512], F32)) for i in range(2)]

            for ob in range(DT):
                slot = ob % 2
                P.emit("g", lambda e, ob=ob, slot=slot: e.dma_start(
                    wbuf2[slot][:, :], wo[ob * 128:(ob + 1) * 128, :]),
                    wait=[P.lag("oev", 4)], inc=[("owl", 16)])
                for c in range(4):
                    cs = slice(c * 512, (c + 1) * 512)
                    P.emit("y", lambda e, ob=ob, cs=cs, xs=c % 4: e.dma_start(
                        xc[xs][:, :], xT[ob * 128:(ob + 1) * 128, cs]),
                        wait=[P.lag("oev", 2)], inc=[("lxc", 16)])
                    for t in range(DT):
                        P.emit("t", lambda e, slot=slot, t=t, cs=cs, c=c: e.matmul(
                            ppo[c % 2][:, :], wbuf2[slot][:, t * 128:(t + 1) * 128],
                            attnT[t][:, cs], start=(t == 0), stop=(t == DT - 1)),
                            wait=[("owl", P.val("owl")), P.lag("oev", 1)],
                            inc=[("omm", 1)] if t == DT - 1 else ())
                    P.emit("v", lambda e, ob=ob, c=c: e.scalar_tensor_tensor(
                        outst[c % 2][:, :], ppo[c % 2][:, :], bo_sb[:, ob:ob + 1],
                        xc[c % 4][:, :], ALU.add, ALU.add),
                        wait=[("omm", P.val("omm")), ("lxc", P.val("lxc")),
                              P.lag("ost", 16)],
                        inc=[("oev", 1)])
                    P.emit("y", lambda e, ob=ob, cs=cs, c=c: e.dma_start(
                        x2T[ob * 128:(ob + 1) * 128, cs], outst[c % 2][:, :]),
                        wait=[("oev", P.val("oev"))], inc=[("ost", 16)])
            if debug:
                for t in range(DT):
                    P.emit("g", lambda e, t=t: e.dma_start(
                        dbg_q[t * 128:(t + 1) * 128, :], qT[t][:, :]), inc=[("dbg", 16)])
                    P.emit("g", lambda e, t=t: e.dma_start(
                        dbg_k[t * 128:(t + 1) * 128, :], kT[t][:, :]), inc=[("dbg", 16)])
                    P.emit("g", lambda e, t=t: e.dma_start(
                        dbg_at[t * 128:(t + 1) * 128, :], attnT[t][:, :]), inc=[("dbg", 16)])
                P.emit("g", None, wait=[("dbg", P.val("dbg"))])
            P.emit("y", None, wait=[("ost", P.val("ost"))])
            P.flush("ph4")






DH = 4096
DHT = DH // 128


def emit_B(nc, P, ctx, xin, xout, debug=False):
    fw1 = nc.dram_tensor("B_fw1", [DH, D], F32, kind="ExternalInput").ap()
    fw2 = nc.dram_tensor("B_fw2", [DH, D], F32, kind="ExternalInput").ap()
    fh1 = nc.dram_tensor("B_fh1", [DH, DH], F32, kind="ExternalInput").ap()
    fh2 = nc.dram_tensor("B_fh2", [DH, DH], F32, kind="ExternalInput").ap()
    fw3 = nc.dram_tensor("B_fw3", [D, DH], F32, kind="ExternalInput").ap()
    s1v = nc.dram_tensor("B_s1v", [128, DHT], F32, kind="ExternalInput").ap()
    s2v = nc.dram_tensor("B_s2v", [128, DHT], F32, kind="ExternalInput").ap()
    hs1v = nc.dram_tensor("B_hs1v", [128, DHT], F32, kind="ExternalInput").ap()
    hs2v = nc.dram_tensor("B_hs2v", [128, DHT], F32, kind="ExternalInput").ap()
    s3v = nc.dram_tensor("B_s3v", [128, DT], F32, kind="ExternalInput").ap()
    xoT = xout

    with ExitStack() as bctx:
        E = bctx.enter_context

        s1_sb = E(nc.sbuf_tensor("B_s1_sb", [128, DHT], F32))
        s2_sb = E(nc.sbuf_tensor("B_s2_sb", [128, DHT], F32))
        hs1_sb = E(nc.sbuf_tensor("B_hs1_sb", [128, DHT], F32))
        hs2_sb = E(nc.sbuf_tensor("B_hs2_sb", [128, DHT], F32))
        s3_sb = E(nc.sbuf_tensor("B_s3_sb", [128, DT], F32))
        ones_f32 = E(nc.sbuf_tensor("B_ones_f32", [128, 1], F32))
        ones_f32r = E(nc.sbuf_tensor("B_ones_f32r", [1, 128], F32))
        eps_sb = E(nc.sbuf_tensor("B_eps_sb", [1, 1], F32))
        mv = E(nc.sbuf_tensor("B_mv", [1, S], F32))
        vv = E(nc.sbuf_tensor("B_vv", [1, S], F32))
        m_rep = E(nc.sbuf_tensor("B_m_rep", [128, S], F32))
        r_rep = E(nc.sbuf_tensor("B_r_rep", [128, S], F32))

        for nm, sb, ap_ in (("s1", s1_sb, s1v), ("s2", s2_sb, s2v),
                            ("hs1", hs1_sb, hs1v), ("hs2", hs2_sb, hs2v),
                            ("s3", s3_sb, s3v)):
            P.emit("y", lambda e, sb=sb, a=ap_: e.dma_start(sb[:, :], a), inc=[("ldm", 16)])
        LDMB = 5 * 16
        P.emit("v", lambda e: e.memset(ones_f32[:, :], 1.0), inc=[("ldv", 1)])
        P.emit("v", lambda e: e.memset(ones_f32r[:, :], 1.0), inc=[("ldv", 1)])
        P.emit("v", lambda e: e.memset(eps_sb[:, :], EPS), inc=[("ldv", 1)])

        with ExitStack() as c1:
            C = c1.enter_context
            xr = [C(nc.sbuf_tensor(f"B_xr{i}", [128, S], F32)) for i in range(2)]
            sq_sb = C(nc.sbuf_tensor("B_sq", [128, S], F32))
            with ExitStack() as c1s:
                stat_ps = c1s.enter_context(nc.psum_tensor("B_stat_ps", [1, S], F32))
                stat_ps2 = c1s.enter_context(nc.psum_tensor("B_stat_ps2", [1, S], F32))
                P.gate(["t"], ["ldv"])
                for t in range(DT):
                    P.emit("y", lambda e, t=t, sl=t % 2: e.dma_start(
                        xr[sl][:, :], xin[t * 128:(t + 1) * 128, :]),
                        wait=[P.lag("sxu", 0), P.lag("ssq", 0)], inc=[("ldx", 16)])
                    P.emit("s", lambda e, sl=t % 2: e.activation(
                        sq_sb[:, :], xr[sl][:, :], AF.Square),
                        wait=["ldx", ("squ", P.val("squ"))],
                        inc=[("ssq", 1)])
                    for c in range(4):
                        P.emit("t", lambda e, t=t, c=c, sl=t % 2: e.matmul(
                            stat_ps[:, c * 512:(c + 1) * 512], ones_f32[:, :],
                            xr[sl][:, c * 512:(c + 1) * 512],
                            start=(t == 0), stop=(t == DT - 1)),
                            wait=["ldx"],
                            inc=[("sxu", 1)] if c == 3 else ())
                    for c in range(4):
                        P.emit("t", lambda e, t=t, c=c: e.matmul(
                            stat_ps2[:, c * 512:(c + 1) * 512], ones_f32[:, :],
                            sq_sb[:, c * 512:(c + 1) * 512],
                            start=(t == 0), stop=(t == DT - 1)),
                            wait=[("ssq", P.val("ssq"))],
                            inc=[("squ", 1)] if c == 3 else ())
                P.emit("v", lambda e: e.tensor_scalar(
                    mv[:, :], stat_ps[:, :], 1.0 / D, None, ALU.mult),
                    wait=["sxu", "squ"], inc=[("lnv", 1)])
                P.emit("v", lambda e: e.tensor_scalar(
                    vv[:, :], stat_ps2[:, :], 1.0 / D, None, ALU.mult),
                    wait=["sxu", "squ"], inc=[("lnv", 1)])
                P.emit("v", lambda e: e.scalar_tensor_tensor(
                    sq_sb[0:1, 0:S], mv[0:1, :], -1.0, mv[0:1, :], ALU.mult, ALU.mult),
                    wait=["lnv"], inc=[("lnv", 1)])
                P.emit("v", lambda e: e.tensor_add(vv[:, :], vv[:, :], sq_sb[0:1, 0:S]),
                       wait=["lnv"], inc=[("lnv", 1)])
                P.emit("s", lambda e: e.activation(
                    vv[:, :], vv[:, :], AF.Sqrt, bias=eps_sb[0:1, 0:1]),
                    wait=["lnv", "ldv"], inc=[("lnv", 1)])
                P.emit("v", lambda e: e.reciprocal(vv[:, :], vv[:, :]),
                       wait=["lnv"], inc=[("rstd", 1)])
                P.flush("phB1a")
            with ExitStack() as c1b:
                rep_ps = c1b.enter_context(nc.psum_tensor("B_rep_ps", [128, 512], F32))
                rep_ps2 = c1b.enter_context(nc.psum_tensor("B_rep_ps2", [128, 512], F32))
                emit_rep(P, rep_ps, rep_ps2, mv, vv, m_rep, r_rep, ones_f32r, "B")
                P.flush("phB1b")

        with ExitStack() as c2:
            C = c2.enter_context
            x2c = [C(nc.sbuf_tensor(f"B_x2c{t}", [128, 512], F32)) for t in range(DT)]
            xln = [C(nc.sbuf_tensor(f"B_xln{t}", [128, 512], BF16)) for t in range(DT)]
            scr = C(nc.sbuf_tensor("B_scr", [128, 512], F32))
            a_sb = [C(nc.sbuf_tensor(f"B_a_sb{i}", [128, 512], F32)) for i in range(2)]
            hA = [C(nc.sbuf_tensor(f"B_hA{i}", [128, 512], BF16)) for i in range(DHT)]
            hB = [C(nc.sbuf_tensor(f"B_hB{i}", [128, 512], BF16)) for i in range(DHT)]
            wb1 = [C(nc.sbuf_tensor(f"B_wb1_{i}", [128, D], BF16)) for i in range(8)]
            wb4 = [C(nc.sbuf_tensor(f"B_wb4_{i}", [128, DH], BF16)) for i in range(4)]
            outst = [C(nc.sbuf_tensor(f"B_outst{i}", [128, 512], F32)) for i in range(2)]
            pz = [C(nc.psum_tensor(f"B_pz{i}", [128, 512], F32)) for i in range(4)]
            pg = [C(nc.psum_tensor(f"B_pg{i}", [128, 512], F32)) for i in range(2)]

            P.gate(["s", "v", "g"], ["ldm", "rstd"])

            for ch in range(4):
                cs = slice(ch * 512, (ch + 1) * 512)
                for t in range(DT):
                    P.emit("y", lambda e, t=t, cs=cs: e.dma_start(
                        x2c[t][:, :], xin[t * 128:(t + 1) * 128, cs]),
                        wait=[P.lag("bs4", 1)], inc=[("lxc", 16)])
                    P.emit("v", lambda e, t=t, cs=cs: e.tensor_sub(
                        scr[:, :], x2c[t][:, :], m_rep[:, cs]),
                        wait=[("lxc", P.val("lxc")), "mrv", "rrv",
                              ("bxl", P.val("bxl")), ("bz1", P.val("bz1")),
                              ("bz2", P.val("bz2"))],
                        inc=[("bxs", 1)])
                    P.emit("v", lambda e, t=t, cs=cs: e.tensor_mul(
                        xln[t][:, :], scr[:, :], r_rep[:, cs]),
                        wait=[("bxs", P.val("bxs"))], inc=[("bxl", 1)])
                for ob in range(DHT):
                    sl1, sl2 = (ob % 4), 4 + (ob % 4)
                    P.emit("g", lambda e, ob=ob, sl=sl1: e.dma_start(
                        wb1[sl][:, :], fw1[ob * 128:(ob + 1) * 128, :]),
                        wait=[P.lag("bz1", 3)], inc=[("bw1", 16)])
                    P.emit("g", lambda e, ob=ob, sl=sl2: e.dma_start(
                        wb1[sl][:, :], fw2[ob * 128:(ob + 1) * 128, :]),
                        wait=[P.lag("bz2", 3)], inc=[("bw1", 16)])
                    for t in range(DT):
                        P.emit("t", lambda e, ob=ob, sl=sl1, t=t: e.matmul(
                            pz[ob % 2][:, :], wb1[sl][:, t * 128:(t + 1) * 128],
                            xln[t][:, :], start=(t == 0), stop=(t == DT - 1)),
                            wait=[("bw1", P.val("bw1")), ("bxl", P.val("bxl")),
                                  P.lag("bsi", 1)],
                            inc=[("bz1", 1)] if t == DT - 1 else ())
                    for t in range(DT):
                        P.emit("t", lambda e, ob=ob, sl=sl2, t=t: e.matmul(
                            pz[2 + ob % 2][:, :], wb1[sl][:, t * 128:(t + 1) * 128],
                            xln[t][:, :], start=(t == 0), stop=(t == DT - 1)),
                            wait=[("bw1", P.val("bw1")), P.lag("bhv", 1)],
                            inc=[("bz2", 1)] if t == DT - 1 else ())
                    P.emit("s", lambda e, ob=ob, sl=ob % 2: e.activation(
                        a_sb[sl][:, :], pz[ob % 2][:, :], AF.Silu,
                        scale=s1_sb[:, ob:ob + 1]),
                        wait=[("bz1", P.val("bz1")), P.lag("bhv", 1)],
                        inc=[("bsi", 1)])
                    P.emit("v", lambda e, ob=ob, sl=ob % 2: e.scalar_tensor_tensor(
                        hA[ob][:, :], pz[2 + ob % 2][:, :], s2_sb[:, ob:ob + 1],
                        a_sb[sl][:, :], ALU.mult, ALU.mult),
                        wait=[("bz2", P.val("bz2")), ("bsi", P.val("bsi")),
                              ("bgu", P.val("bgu"))],
                        inc=[("bhv", 1)])
                for ob in range(DHT):
                    P.emit("g", lambda e, ob=ob, sl=ob % 4: e.dma_start(
                        wb4[sl][:, :], fh1[ob * 128:(ob + 1) * 128, :]),
                        wait=[P.lag("bgu", 3)], inc=[("bw4", 16)])
                    for t in range(DHT):
                        P.emit("t", lambda e, ob=ob, sl=ob % 4, t=t: e.matmul(
                            pg[ob % 2][:, :], wb4[sl][:, t * 128:(t + 1) * 128],
                            hA[t][:, :], start=(t == 0), stop=(t == DHT - 1)),
                            wait=[("bw4", P.val("bw4")), ("bhv", P.val("bhv")),
                                  P.lag("bg2", 1)],
                            inc=[("bgu", 1)] if t == DHT - 1 else ())
                    P.emit("s", lambda e, ob=ob: e.activation(
                        hB[ob][:, :], pg[ob % 2][:, :], AF.Gelu,
                        scale=hs1_sb[:, ob:ob + 1]),
                        wait=[("bgu", P.val("bgu"))], inc=[("bg2", 1)])
                for ob in range(DHT):
                    P.emit("g", lambda e, ob=ob, sl=ob % 4: e.dma_start(
                        wb4[sl][:, :], fh2[ob * 128:(ob + 1) * 128, :]),
                        wait=[P.lag("bgu", 3)], inc=[("bw4", 16)])
                    for t in range(DHT):
                        P.emit("t", lambda e, ob=ob, sl=ob % 4, t=t: e.matmul(
                            pg[ob % 2][:, :], wb4[sl][:, t * 128:(t + 1) * 128],
                            hB[t][:, :], start=(t == 0), stop=(t == DHT - 1)),
                            wait=[("bw4", P.val("bw4")), ("bg2", P.val("bg2")),
                                  P.lag("bg3", 1)],
                            inc=[("bgu", 1)] if t == DHT - 1 else ())
                    P.emit("s", lambda e, ob=ob: e.activation(
                        hA[ob][:, :], pg[ob % 2][:, :], AF.Gelu,
                        scale=hs2_sb[:, ob:ob + 1]),
                        wait=[("bgu", P.val("bgu"))], inc=[("bg3", 1)])
                for do in range(DT):
                    P.emit("g", lambda e, do=do, sl=do % 4: e.dma_start(
                        wb4[sl][:, :], fw3[do * 128:(do + 1) * 128, :]),
                        wait=[P.lag("bgu", 3)], inc=[("bw4", 16)])
                    for t in range(DHT):
                        P.emit("t", lambda e, do=do, sl=do % 4, t=t: e.matmul(
                            pg[do % 2][:, :], wb4[sl][:, t * 128:(t + 1) * 128],
                            hA[t][:, :], start=(t == 0), stop=(t == DHT - 1)),
                            wait=[("bw4", P.val("bw4")), ("bg3", P.val("bg3")),
                                  P.lag("bs4", 1)],
                            inc=[("bgu", 1)] if t == DHT - 1 else ())
                    P.emit("v", lambda e, do=do: e.scalar_tensor_tensor(
                        outst[do % 2][:, :], pg[do % 2][:, :], s3_sb[:, do:do + 1],
                        x2c[do][:, :], ALU.mult, ALU.add),
                        wait=[("bgu", P.val("bgu")), ("lxc", P.val("lxc")),
                              P.lag("bst", 16)],
                        inc=[("bs4", 1)])
                    P.emit("y", lambda e, do=do, cs=cs, sl=do % 2: e.dma_start(
                        xoT[do * 128:(do + 1) * 128, cs], outst[sl][:, :]),
                        wait=[("bs4", P.val("bs4"))], inc=[("bst", 16)])
            P.emit("y", None, wait=[("bst", P.val("bst"))])
            P.flush("phB2")


def prep_l0_ffn(ins):
    w = {}
    w["fw1"] = host_blocks(_c(ins["L0_ffn_w1_w"].T))
    w["fw2"] = host_blocks(_c(ins["L0_ffn_w2_w"].T))
    w["fh1"] = host_blocks(_c(ins["L0_ffn_h1_w"].T))
    w["fh2"] = host_blocks(_c(ins["L0_ffn_h2_w"].T))
    w["fw3"] = host_blocks(_c(ins["L0_ffn_w3_w"].T))
    w["s1v"] = host_vec(ins["L0_ffn_w1_s"])
    w["s2v"] = host_vec(ins["L0_ffn_w2_s"])
    w["hs1v"] = host_vec(ins["L0_ffn_h1_s"])
    w["hs2v"] = host_vec(ins["L0_ffn_h2_s"])
    w["s3v"] = host_vec(ins["L0_ffn_w3_s"])
    return w





def emit_C(nc, P, ctx, xin, xout, debug=False):
    wq = nc.dram_tensor("C_wq", [D, D], F32, kind="ExternalInput").ap()
    wk = nc.dram_tensor("C_wk", [D, D], F32, kind="ExternalInput").ap()
    wv = nc.dram_tensor("C_wv", [D, D], F32, kind="ExternalInput").ap()
    wo = nc.dram_tensor("C_wo", [D, D], F32, kind="ExternalInput").ap()
    bqv = nc.dram_tensor("C_bqv", [128, DT], F32, kind="ExternalInput").ap()
    bkv = nc.dram_tensor("C_bkv", [128, DT], F32, kind="ExternalInput").ap()
    bvv = nc.dram_tensor("C_bvv", [128, DT], F32, kind="ExternalInput").ap()
    bov = nc.dram_tensor("C_bov", [128, DT], F32, kind="ExternalInput").ap()
    xoT = xout

    id128 = nc.inline_tensor(np.eye(128, dtype=BF), "C_id128")
    band = np.zeros((128, 1024), dtype=BF)
    for rel in range(-1, 3):
        for p in range(128):
            for i in range(256):
                if abs(128 * rel + p - i) <= 64:
                    band[p, (rel + 1) * 256 + i] = 1.0
    bandc = nc.inline_tensor(band, "C_bandc")

    with ExitStack() as bctx:
        E = bctx.enter_context

        bq_sb = E(nc.sbuf_tensor("C_bq_sb", [128, DT], F32))
        bk_sb = E(nc.sbuf_tensor("C_bk_sb", [128, DT], F32))
        bv_sb = E(nc.sbuf_tensor("C_bv_sb", [128, DT], F32))
        bo_sb = E(nc.sbuf_tensor("C_bo_sb", [128, DT], F32))
        id_sb = E(nc.sbuf_tensor("C_id_sb", [128, 128], BF16))
        band_sb = E(nc.sbuf_tensor("C_band_sb", [128, 1024], BF16))
        ones_f32 = E(nc.sbuf_tensor("C_ones_f32", [128, 1], F32))
        ones_f32r = E(nc.sbuf_tensor("C_ones_f32r", [1, 128], F32))
        ones_bf = E(nc.sbuf_tensor("C_ones_bf", [1, 128], BF16))
        eps_sb = E(nc.sbuf_tensor("C_eps_sb", [1, 1], F32))

        for sb, ap_ in ((bq_sb, bqv), (bk_sb, bkv), (bv_sb, bvv), (bo_sb, bov)):
            P.emit("y", lambda e, sb=sb, a=ap_: e.dma_start(sb[:, :], a), inc=[("ldm", 16)])
        P.emit("g", lambda e: e.dma_start(id_sb[:, :], id128.ap()), inc=[("ldm", 16)])
        P.emit("g", lambda e: e.dma_start(band_sb[:, :], bandc.ap()), inc=[("ldm", 16)])
        LDMC = 6 * 16
        P.emit("v", lambda e: e.memset(ones_f32[:, :], 1.0), inc=[("ldv", 1)])
        P.emit("v", lambda e: e.memset(ones_f32r[:, :], 1.0), inc=[("ldv", 1)])
        P.emit("v", lambda e: e.memset(ones_bf[:, :], 1.0), inc=[("ldv", 1)])
        P.emit("v", lambda e: e.memset(eps_sb[:, :], EPS), inc=[("ldv", 1)])
        P.emit("v", lambda e: e.tensor_scalar(bk_sb[:, :], bk_sb[:, :], 0.125, None, ALU.mult),
               wait=["ldm"], inc=[("bk8", 1)])

        qT = [E(nc.sbuf_tensor(f"C_qT{t}", [128, S], BF16)) for t in range(DT)]
        kT = [E(nc.sbuf_tensor(f"C_kT{t}", [128, S], BF16)) for t in range(DT)]
        v_aug = [E(nc.sbuf_tensor(f"C_vaug{j}", [128, H * 65], BF16)) for j in range(ST)]

        cx = bctx.enter_context(ExitStack())
        x_ln = [cx.enter_context(nc.sbuf_tensor(f"C_x_ln{t}", [128, S], BF16))
                for t in range(DT)]

        with ExitStack() as c1:
            C = c1.enter_context
            xr = [C(nc.sbuf_tensor(f"C_xr{i}", [128, S], F32)) for i in range(2)]
            sq_sb = C(nc.sbuf_tensor("C_sq", [128, S], F32))
            mv = C(nc.sbuf_tensor("C_mv", [1, S], F32))
            vv = C(nc.sbuf_tensor("C_vv", [1, S], F32))
            m_rep = C(nc.sbuf_tensor("C_m_rep", [128, S], F32))
            r_rep = C(nc.sbuf_tensor("C_r_rep", [128, S], F32))
            with ExitStack() as c1s:
                stat_ps = c1s.enter_context(nc.psum_tensor("C_stat_ps", [1, S], F32))
                stat_ps2 = c1s.enter_context(nc.psum_tensor("C_stat_ps2", [1, S], F32))
                P.gate(["t"], ["ldv"])
                for t in range(DT):
                    P.emit("y", lambda e, t=t, sl=t % 2: e.dma_start(
                        xr[sl][:, :], xin[t * 128:(t + 1) * 128, :]),
                        wait=[P.lag("sxu", 0), P.lag("ssq", 0)], inc=[("ldx", 16)])
                    P.emit("s", lambda e, sl=t % 2: e.activation(
                        sq_sb[:, :], xr[sl][:, :], AF.Square),
                        wait=["ldx", ("squ", P.val("squ"))],
                        inc=[("ssq", 1)])
                    for c in range(4):
                        P.emit("t", lambda e, t=t, c=c, sl=t % 2: e.matmul(
                            stat_ps[:, c * 512:(c + 1) * 512], ones_f32[:, :],
                            xr[sl][:, c * 512:(c + 1) * 512],
                            start=(t == 0), stop=(t == DT - 1)),
                            wait=["ldx"],
                            inc=[("sxu", 1)] if c == 3 else ())
                    for c in range(4):
                        P.emit("t", lambda e, t=t, c=c: e.matmul(
                            stat_ps2[:, c * 512:(c + 1) * 512], ones_f32[:, :],
                            sq_sb[:, c * 512:(c + 1) * 512],
                            start=(t == 0), stop=(t == DT - 1)),
                            wait=[("ssq", P.val("ssq"))],
                            inc=[("squ", 1)] if c == 3 else ())
                P.emit("v", lambda e: e.tensor_scalar(
                    mv[:, :], stat_ps[:, :], 1.0 / D, None, ALU.mult),
                    wait=["sxu", "squ"], inc=[("lnv", 1)])
                P.emit("v", lambda e: e.tensor_scalar(
                    vv[:, :], stat_ps2[:, :], 1.0 / D, None, ALU.mult),
                    wait=["sxu", "squ"], inc=[("lnv", 1)])
                P.emit("v", lambda e: e.scalar_tensor_tensor(
                    sq_sb[0:1, 0:S], mv[0:1, :], -1.0, mv[0:1, :], ALU.mult, ALU.mult),
                    wait=["lnv"], inc=[("lnv", 1)])
                P.emit("v", lambda e: e.tensor_add(vv[:, :], vv[:, :], sq_sb[0:1, 0:S]),
                       wait=["lnv"], inc=[("lnv", 1)])
                P.emit("s", lambda e: e.activation(
                    vv[:, :], vv[:, :], AF.Sqrt, bias=eps_sb[0:1, 0:1]),
                    wait=["lnv", "ldv"], inc=[("lnv", 1)])
                P.emit("v", lambda e: e.reciprocal(vv[:, :], vv[:, :]),
                       wait=["lnv"], inc=[("rstd", 1)])
                P.flush("phC1a")
            with ExitStack() as c1b:
                rep_ps = c1b.enter_context(nc.psum_tensor("C_rep_ps", [128, 512], F32))
                rep_ps2 = c1b.enter_context(nc.psum_tensor("C_rep_ps2", [128, 512], F32))
                emit_rep(P, rep_ps, rep_ps2, mv, vv, m_rep, r_rep, ones_f32r, "C")
                for t in range(DT):
                    P.emit("y", lambda e, t=t, sl=t % 2: e.dma_start(
                        xr[sl][:, :], xin[t * 128:(t + 1) * 128, :]),
                        wait=[P.lag("xsub", 1)], inc=[("ldx", 16)])
                    P.emit("v", lambda e, t=t, sl=t % 2: e.tensor_sub(
                        sq_sb[:, :], xr[sl][:, :], m_rep[:, :]),
                        wait=["mrv", "rrv", "ldx",
                              ("xlnd", P.val("xlnd"))],
                        inc=[("xsub", 1)])
                    P.emit("v", lambda e, t=t: e.tensor_mul(
                        x_ln[t][:, :], sq_sb[:, :], r_rep[:, :]),
                        wait=[("xsub", P.val("xsub"))], inc=[("xlnd", 1)])
                P.flush("phC1b")

        with ExitStack() as c2:
            C = c2.enter_context
            wbuf = [C(nc.sbuf_tensor(f"C_wbuf{i}", [128, D], BF16)) for i in range(2)]
            vT_t = C(nc.sbuf_tensor("C_vT_t", [128, S], BF16))
            pp = [C(nc.psum_tensor(f"C_pp{i}", [128, 512], F32)) for i in range(4)]
            tp = C(nc.psum_tensor("C_tp", [128, 128], BF16))
            P.gate(["s", "v", "g"], ["ldm", "bk8"])

            def proj_dmajor(wdram, bias_sb, scale, out_tiles, ppbase):
                for ob in range(DT):
                    slot = ob % 2
                    P.emit("g", lambda e, ob=ob, slot=slot, w=wdram: e.dma_start(
                        wbuf[slot][:, :], w[ob * 128:(ob + 1) * 128, :]),
                        wait=[P.lag("pev", 4)], inc=[("wld", 16)])
                    for c in range(4):
                        cs = slice(c * 512, (c + 1) * 512)
                        ps = pp[ppbase + (c % 2)]
                        for t in range(DT):
                            P.emit("t", lambda e, slot=slot, t=t, cs=cs, ps=ps: e.matmul(
                                ps[:, :], wbuf[slot][:, t * 128:(t + 1) * 128],
                                x_ln[t][:, cs], start=(t == 0), stop=(t == DT - 1)),
                                wait=[("wld", P.val("wld")), P.lag("pev", 1),
                                      ("xlnd", DT)],
                                inc=[("pmm", 1)] if t == DT - 1 else ())
                        P.emit("s", lambda e, ob=ob, cs=cs, ps=ps, b=bias_sb, sc=scale,
                               ot=out_tiles: e.activation(
                            ot[ob][:, cs], ps[:, :], AF.Identity,
                            bias=b[:, ob:ob + 1], scale=sc),
                            wait=[("pmm", P.val("pmm"))], inc=[("pev", 1)])

            proj_dmajor(wq, bq_sb, 1.0, qT, 0)
            proj_dmajor(wk, bk_sb, 0.125, kT, 0)

            for j in range(ST):
                P.emit("v", lambda e, j=j: e.memset(v_aug[j][:, :], 1.0),
                       inc=[("vamem", 1)])
            for ob in range(DT):
                slot = ob % 2
                P.emit("g", lambda e, ob=ob, slot=slot: e.dma_start(
                    wbuf[slot][:, :], wv[ob * 128:(ob + 1) * 128, :]),
                    wait=[P.lag("pev", 4)], inc=[("wld", 16)])
                for c in range(4):
                    cs = slice(c * 512, (c + 1) * 512)
                    ps = pp[2 + (c % 2)]
                    for t in range(DT):
                        P.emit("t", lambda e, slot=slot, t=t, cs=cs, ps=ps: e.matmul(
                            ps[:, :], wbuf[slot][:, t * 128:(t + 1) * 128],
                            x_ln[t][:, cs], start=(t == 0), stop=(t == DT - 1)),
                            wait=[("wld", P.val("wld")), P.lag("pev", 1)],
                            inc=[("vmm", 1)] if t == DT - 1 else ())
                    P.emit("s", lambda e, ob=ob, cs=cs, ps=ps: e.activation(
                        vT_t[:, cs], ps[:, :], AF.Identity, bias=bv_sb[:, ob:ob + 1]),
                        wait=[("vmm", P.val("vmm")), ("vtr", P.val("vtr"))],
                        inc=[("pev", 1)])
                for j in range(ST):
                    P.emit("t", lambda e, j=j: e.transpose(
                        tp[:, :], vT_t[:, j * 128:(j + 1) * 128], id_sb[:, :]),
                        wait=[("pev", P.val("pev")), ("vtrc", P.val("vtrc"))],
                        inc=[("vtr", 1)])
                    P.emit("v", lambda e, ob=ob, j=j: e.tensor_copy(
                        v_aug[j][:, (2 * ob) * 65:(2 * ob) * 65 + 64], tp[:, 0:64]),
                        wait=[("vtr", P.val("vtr")), "vamem"], inc=[("vtrc", 1)])
                    P.emit("v", lambda e, ob=ob, j=j: e.tensor_copy(
                        v_aug[j][:, (2 * ob + 1) * 65:(2 * ob + 1) * 65 + 64],
                        tp[:, 64:128]),
                        wait=[("vtr", P.val("vtr")), "vamem"], inc=[("vtrc", 1)])
            P.flush("phC2")

        cx.close()
        attnT = [E(nc.sbuf_tensor(f"C_attnT{t}", [128, S], BF16)) for t in range(DT)]

        with ExitStack() as c3:
            C = c3.enter_context
            Em = [C(nc.sbuf_tensor(f"C_Em{i}", [128, 1024], BF16)) for i in range(2)]
            qz = [C(nc.sbuf_tensor(f"C_qz{i}", [128, 256], BF16)) for i in range(2)]
            rcp = C(nc.sbuf_tensor("C_rcp", [1, 256], BF16))
            rep64 = C(nc.sbuf_tensor("C_rep64", [64, 256], BF16))
            avcp = [C(nc.sbuf_tensor(f"C_avcp{i}", [65, 256], F32)) for i in range(2)]
            ps_S = [C(nc.psum_tensor(f"C_ps_S{i}", [128, 1024], F32)) for i in range(2)]
            ps_AV = C(nc.psum_tensor("C_ps_AV", [65, 256], F32))
            ps_bc = C(nc.psum_tensor("C_ps_bc", [64, 256], F32))

            P.emit("v", lambda e: e.memset(qz[0][:, :], 0.0), inc=[("qzi", 1)])
            P.emit("v", lambda e: e.memset(qz[1][:, :], 0.0), inc=[("qzi", 1)])

            NIC = 8
            for ic in range(NIC):
                cs = slice(ic * 256, (ic + 1) * 256)
                jlist = [jj for jj in range(2 * ic - 1, 2 * ic + 3) if 0 <= jj < ST]
                rel0 = jlist[0] - 2 * ic
                nj = len(jlist)
                for h in range(H):
                    par = h % 2
                    ht = h // 2
                    half = (ic * H + h) % 2
                    sb = ps_S[half]
                    eh = "exh" + str(half)
                    P.emit("v", lambda e, par=par, ht=ht, cs=cs: e.tensor_copy(
                        qz[par][par * 64:par * 64 + 64, :],
                        qT[ht][par * 64:par * 64 + 64, cs]),
                        wait=["qzi", ("qzu" + str(par), P.val("qzu" + str(par)))],
                        inc=[("qzc" + str(par), 1)])
                    for idx, jj in enumerate(jlist):
                        P.emit("t", lambda e, ht=ht, jj=jj, idx=idx, sb=sb, par=par,
                               last=(idx == nj - 1): e.matmul(
                            sb[:, idx * 256:(idx + 1) * 256],
                            kT[ht][:, jj * 128:(jj + 1) * 128],
                            qz[par][:, :], start=True, stop=True),
                            wait=[("qzc" + str(par), P.val("qzc" + str(par))),
                                  (eh, P.val(eh))],
                            inc=[("qzu" + str(par), 1)] if idx == nj - 1 else ())
                    P.emit("s", lambda e, sb=sb, nj=nj, half=half: e.activation(
                        Em[half][:, 0:nj * 256], sb[:, 0:nj * 256], AF.Exp),
                        wait=[("qzu" + str(par), P.val("qzu" + str(par))),
                              ("emv", P.val("emv")), ("ava", P.val("ava"))],
                        inc=[(eh, 1)])
                    P.emit("v", lambda e, nj=nj, rel0=rel0, half=half: e.tensor_mul(
                        Em[half][:, 0:nj * 256], Em[half][:, 0:nj * 256],
                        band_sb[:, (rel0 + 1) * 256:(rel0 + 1 + nj) * 256]),
                        wait=[(eh, P.val(eh))], inc=[("emv", 1)])
                    for idx, jj in enumerate(jlist):
                        P.emit("t", lambda e, h=h, jj=jj, idx=idx, half=half,
                               st=(idx == 0), sp=(idx == nj - 1): e.matmul(
                            ps_AV[:, :], v_aug[jj][:, h * 65:h * 65 + 65],
                            Em[half][:, idx * 256:(idx + 1) * 256],
                            start=st, stop=sp),
                            wait=[("emv", P.val("emv")), ("avn", P.val("avn"))],
                            inc=[("ava", 1)] if idx == nj - 1 else ())
                    P.emit("v", lambda e, sl=h % 2: e.tensor_copy(
                        avcp[sl][:, :], ps_AV[:, :]),
                        wait=[("ava", P.val("ava")), P.lag("anm", 1)], inc=[("avn", 1)])
                    P.emit("v", lambda e, sl=h % 2: e.reciprocal(
                        rcp[:, :], avcp[sl][64:65, :]),
                        wait=[("avn", P.val("avn")), ("bcm", P.val("bcm"))],
                        inc=[("rcv", 1)])
                    P.emit("t", lambda e: e.matmul(
                        ps_bc[:, :], ones_bf[:, 0:64], rcp[:, :], start=True, stop=True),
                        wait=[("rcv", P.val("rcv")), P.lag("rpv", 0)], inc=[("bcm", 1)])
                    P.emit("v", lambda e: e.tensor_copy(rep64[:, :], ps_bc[:, :]),
                           wait=[("bcm", P.val("bcm"))], inc=[("rpv", 1)])
                    P.emit("v", lambda e, par=par, ht=ht, cs=cs, sl=h % 2: e.tensor_mul(
                        attnT[ht][par * 64:par * 64 + 64, cs],
                        avcp[sl][0:64, :], rep64[:, :]),
                        wait=[("rpv", P.val("rpv"))], inc=[("anm", 1)])
            P.flush("phC3")

        with ExitStack() as c4:
            C = c4.enter_context
            wbuf2 = [C(nc.sbuf_tensor(f"C_wbuf2_{i}", [128, D], BF16)) for i in range(2)]
            xc = [C(nc.sbuf_tensor(f"C_xc{i}", [128, 512], F32)) for i in range(4)]
            outst = [C(nc.sbuf_tensor(f"C_outst{i}", [128, 512], F32)) for i in range(2)]
            ppo = [C(nc.psum_tensor(f"C_ppo{i}", [128, 512], F32)) for i in range(2)]
            for ob in range(DT):
                slot = ob % 2
                P.emit("g", lambda e, ob=ob, slot=slot: e.dma_start(
                    wbuf2[slot][:, :], wo[ob * 128:(ob + 1) * 128, :]),
                    wait=[P.lag("oev", 4)], inc=[("owl", 16)])
                for c in range(4):
                    cs = slice(c * 512, (c + 1) * 512)
                    P.emit("y", lambda e, ob=ob, cs=cs, xs=c % 4: e.dma_start(
                        xc[xs][:, :], xin[ob * 128:(ob + 1) * 128, cs]),
                        wait=[P.lag("oev", 2)], inc=[("lxc", 16)])
                    for t in range(DT):
                        P.emit("t", lambda e, slot=slot, t=t, cs=cs, c=c: e.matmul(
                            ppo[c % 2][:, :], wbuf2[slot][:, t * 128:(t + 1) * 128],
                            attnT[t][:, cs], start=(t == 0), stop=(t == DT - 1)),
                            wait=[("owl", P.val("owl")), P.lag("oev", 1)],
                            inc=[("omm", 1)] if t == DT - 1 else ())
                    P.emit("v", lambda e, ob=ob, c=c: e.scalar_tensor_tensor(
                        outst[c % 2][:, :], ppo[c % 2][:, :], bo_sb[:, ob:ob + 1],
                        xc[c % 4][:, :], ALU.add, ALU.add),
                        wait=[("omm", P.val("omm")), ("lxc", P.val("lxc")),
                              P.lag("ost", 16)],
                        inc=[("oev", 1)])
                    P.emit("y", lambda e, ob=ob, cs=cs, c=c: e.dma_start(
                        xoT[ob * 128:(ob + 1) * 128, cs], outst[c % 2][:, :]),
                        wait=[("oev", P.val("oev"))], inc=[("ost", 16)])
            P.emit("y", None, wait=[("ost", P.val("ost"))])
            P.flush("phC4")


def prep_l1_attn(ins):
    w = {}
    w["wq"] = host_blocks(_c(ins["L1_attn_Wq"].T))
    w["wk"] = host_blocks(_c(ins["L1_attn_Wk"].T))
    w["wv"] = host_blocks(_c(ins["L1_attn_Wv"].T))
    w["wo"] = host_blocks(_c(ins["L1_attn_Wo"].T))
    w["bqv"] = host_vec(ins["L1_attn_bq"])
    w["bkv"] = host_vec(ins["L1_attn_bk"])
    w["bvv"] = host_vec(ins["L1_attn_bv"])
    w["bov"] = host_vec(ins["L1_attn_bo"])
    return w





NE = 8
DH2 = 2048
DH2T = DH2 // 128


def emit_D(nc, P, ctx, xin, xout, debug=False):
    wr = nc.dram_tensor("D_wr", [D, NE], F32, kind="ExternalInput").ap()
    brv = nc.dram_tensor("D_brv", [1, NE], F32, kind="ExternalInput").ap()
    w1e = nc.dram_tensor("D_w1e", [NE * 128, DH2T * D], F32, kind="ExternalInput").ap()
    w2e = nc.dram_tensor("D_w2e", [NE * 128, DT * DH2], F32, kind="ExternalInput").ap()
    b1v = nc.dram_tensor("D_b1v", [128, NE * DH2T], F32, kind="ExternalInput").ap()
    b2v = nc.dram_tensor("D_b2v", [NE, D], F32, kind="ExternalInput").ap()
    xoT = xout
    if debug:
        dbg_mk = nc.dram_tensor("D_dbg_mk", [NE, 512], F32, kind="ExternalOutput").ap()
        dbg_mk4 = nc.dram_tensor("D_dbg_mk4", [4 * NE, 512], F32, kind="ExternalOutput").ap()
        dbg_mk5 = nc.dram_tensor("D_dbg_mk5", [4 * NE, 512], F32, kind="ExternalOutput").ap()
        dbg_y0 = nc.dram_tensor("D_dbg_y0", [D, 512], F32, kind="ExternalOutput").ap()
        dbg_h = nc.dram_tensor("D_dbg_h", [DH2, 512], F32, kind="ExternalOutput").ap()
        dbg_ya = nc.dram_tensor("D_dbg_ya", [D, 512], F32, kind="ExternalOutput").ap()
        dbg_xl = nc.dram_tensor("D_dbg_xl", [D, 512], F32, kind="ExternalOutput").ap()
        dbg_lg = nc.dram_tensor("D_dbg_lg", [128, NE], F32, kind="ExternalOutput").ap()
        dbg_mx = nc.dram_tensor("D_dbg_mx", [128, 1], F32, kind="ExternalOutput").ap()
        dbg_ms = nc.dram_tensor("D_dbg_ms", [128, NE], F32, kind="ExternalOutput").ap()
        dbg_mb = nc.dram_tensor("D_dbg_mb", [128, NE], F32, kind="ExternalOutput").ap()
        dbg_r0 = nc.dram_tensor("D_dbg_r0", [128, 4 * NE + 4], F32, kind="ExternalOutput").ap()
        dbg_st = nc.dram_tensor("D_dbg_st", [4, S], F32, kind="ExternalOutput").ap()
        dbg_x0 = nc.dram_tensor("D_dbg_x0", [D, 512], F32, kind="ExternalOutput").ap()
        dbg_mr = nc.dram_tensor("D_dbg_mr", [128, 512], F32, kind="ExternalOutput").ap()

    id128 = nc.inline_tensor(np.eye(128, dtype=BF), "D_id128")
    e8 = np.zeros((NE, NE * 128), dtype=BF)
    for i in range(NE):
        e8[i, i * 128:(i + 1) * 128] = 1.0
    e8sel = nc.inline_tensor(e8, "D_e8sel")

    with ExitStack() as bctx:
        E = bctx.enter_context

        wr_sb = E(nc.sbuf_tensor("D_wr_sb", [128, DT * NE], F32))
        br_sb = E(nc.sbuf_tensor("D_br_sb", [1, NE], F32))
        brb = E(nc.sbuf_tensor("D_brb", [128, NE], F32))
        b1_sb = E(nc.sbuf_tensor("D_b1_sb", [128, NE * DH2T], F32))
        b2_sb = E(nc.sbuf_tensor("D_b2_sb", [NE, D], BF16))
        id_sb = E(nc.sbuf_tensor("D_id_sb", [128, 128], BF16))
        e8_sb = E(nc.sbuf_tensor("D_e8_sb", [NE, NE * 128], BF16))
        ones_f32 = E(nc.sbuf_tensor("D_ones_f32", [128, 1], F32))
        ones_f32r = E(nc.sbuf_tensor("D_ones_f32r", [1, 128], F32))
        eps_sb = E(nc.sbuf_tensor("D_eps_sb", [1, 1], F32))
        m_rep = E(nc.sbuf_tensor("D_m_rep", [128, S], F32))
        r_rep = E(nc.sbuf_tensor("D_r_rep", [128, S], F32))

        for t in range(DT):
            P.emit("y", lambda e, t=t: e.dma_start(
                wr_sb[:, t * NE:(t + 1) * NE], wr[t * 128:(t + 1) * 128, :]),
                inc=[("ldm", 16)])
        P.emit("y", lambda e: e.dma_start(br_sb[:, :], brv), inc=[("ldm", 16)])
        P.emit("y", lambda e: e.dma_start(b1_sb[:, :], b1v), inc=[("ldm", 16)])
        P.emit("g", lambda e: e.dma_start(b2_sb[:, :], b2v), inc=[("ldm", 16)])
        P.emit("g", lambda e: e.dma_start(id_sb[:, :], id128.ap()), inc=[("ldm", 16)])
        P.emit("g", lambda e: e.dma_start(e8_sb[:, :], e8sel.ap()), inc=[("ldm", 16)])
        LDMD = 13 * 16
        P.emit("v", lambda e: e.memset(ones_f32[:, :], 1.0), inc=[("ldv", 1)])
        P.emit("v", lambda e: e.memset(ones_f32r[:, :], 1.0), inc=[("ldv", 1)])
        P.emit("v", lambda e: e.memset(eps_sb[:, :], EPS), inc=[("ldv", 1)])

        with ExitStack() as c1:
            C = c1.enter_context
            xr = [C(nc.sbuf_tensor(f"D_xr{i}", [128, S], F32)) for i in range(2)]
            sq_sb = C(nc.sbuf_tensor("D_sq", [128, S], F32))
            mv = C(nc.sbuf_tensor("D_mv", [1, S], F32))
            vv = C(nc.sbuf_tensor("D_vv", [1, S], F32))
            with ExitStack() as c1s:
                stat_ps = c1s.enter_context(nc.psum_tensor("D_stat_ps", [1, S], F32))
                stat_ps2 = c1s.enter_context(nc.psum_tensor("D_stat_ps2", [1, S], F32))
                P.gate(["t"], ["ldv"])
                for t in range(DT):
                    P.emit("y", lambda e, t=t, sl=t % 2: e.dma_start(
                        xr[sl][:, :], xin[t * 128:(t + 1) * 128, :]),
                        wait=[P.lag("sxu", 0), P.lag("ssq", 0)], inc=[("ldx", 16)])
                    P.emit("s", lambda e, sl=t % 2: e.activation(
                        sq_sb[:, :], xr[sl][:, :], AF.Square),
                        wait=["ldx", ("squ", P.val("squ"))],
                        inc=[("ssq", 1)])
                    for c in range(4):
                        P.emit("t", lambda e, t=t, c=c, sl=t % 2: e.matmul(
                            stat_ps[:, c * 512:(c + 1) * 512], ones_f32[:, :],
                            xr[sl][:, c * 512:(c + 1) * 512],
                            start=(t == 0), stop=(t == DT - 1)),
                            wait=["ldx"],
                            inc=[("sxu", 1)] if c == 3 else ())
                    for c in range(4):
                        P.emit("t", lambda e, t=t, c=c: e.matmul(
                            stat_ps2[:, c * 512:(c + 1) * 512], ones_f32[:, :],
                            sq_sb[:, c * 512:(c + 1) * 512],
                            start=(t == 0), stop=(t == DT - 1)),
                            wait=[("ssq", P.val("ssq"))],
                            inc=[("squ", 1)] if c == 3 else ())
                P.emit("v", lambda e: e.tensor_scalar(
                    mv[:, :], stat_ps[:, :], 1.0 / D, None, ALU.mult),
                    wait=["sxu", "squ"], inc=[("lnv", 1)])
                P.emit("v", lambda e: e.tensor_scalar(
                    vv[:, :], stat_ps2[:, :], 1.0 / D, None, ALU.mult),
                    wait=["sxu", "squ"], inc=[("lnv", 1)])
                P.emit("v", lambda e: e.scalar_tensor_tensor(
                    sq_sb[0:1, 0:S], mv[0:1, :], -1.0, mv[0:1, :], ALU.mult, ALU.mult),
                    wait=["lnv"], inc=[("lnv", 1)])
                P.emit("v", lambda e: e.tensor_add(vv[:, :], vv[:, :], sq_sb[0:1, 0:S]),
                       wait=["lnv"], inc=[("lnv", 1)])
                P.emit("s", lambda e: e.activation(
                    vv[:, :], vv[:, :], AF.Sqrt, bias=eps_sb[0:1, 0:1]),
                    wait=["lnv", "ldv"], inc=[("lnv", 1)])
                P.emit("v", lambda e: e.reciprocal(vv[:, :], vv[:, :]),
                       wait=["lnv"], inc=[("rstd", 1)])
                P.flush("phD1a")
            with ExitStack() as c1b:
                rep_ps = c1b.enter_context(nc.psum_tensor("D_rep_ps", [128, 512], F32))
                rep_ps2 = c1b.enter_context(nc.psum_tensor("D_rep_ps2", [128, 512], F32))
                emit_rep(P, rep_ps, rep_ps2, mv, vv, m_rep, r_rep, ones_f32r, "D")
                P.emit("t", lambda e: e.matmul(
                    rep_ps[:, 0:NE], ones_f32r[0:1, :], br_sb[0:1, :],
                    start=True, stop=True),
                    wait=["ldm", "mrv"], inc=[("brt", 1)])
                P.emit("v", lambda e: e.tensor_copy(brb[:, :], rep_ps[:, 0:NE]),
                       wait=["brt"], inc=[("brv2", 1)])
                P.flush("phD1b")

        with ExitStack() as c2:
            C = c2.enter_context
            x4c = [C(nc.sbuf_tensor(f"D_x4c{t}", [128, 512], F32)) for t in range(DT)]
            xln = [C(nc.sbuf_tensor(f"D_xln{t}", [128, 512], BF16)) for t in range(DT)]
            xlf = [C(nc.sbuf_tensor(f"D_xlf{t}", [128, 512], F32)) for t in range(DT)]
            scr = C(nc.sbuf_tensor("D_scr", [128, 512], F32))
            hT = [C(nc.sbuf_tensor(f"D_hT{i}", [128, 512], BF16)) for i in range(DH2T)]
            rl = [C(nc.sbuf_tensor(f"D_rl{i}", [128, 512], BF16)) for i in range(2)]
            w1sb = C(nc.sbuf_tensor("D_w1sb", [128, DH2T * D], BF16))
            w2sb = C(nc.sbuf_tensor("D_w2sb", [128, DT * DH2], BF16))
            y_acc = [C(nc.sbuf_tensor(f"D_yacc{t}", [128, 512], F32)) for t in range(DT)]
            mrep = [C(nc.sbuf_tensor(f"D_mrep{i}", [128, 512], BF16)) for i in range(2)]
            maskpT = C(nc.sbuf_tensor("D_maskpT", [NE, 512], BF16))
            lgt = C(nc.sbuf_tensor("D_lgt", [128, NE], F32))
            mrow = C(nc.sbuf_tensor("D_mrow", [128, 1], F32))
            nmrow = C(nc.sbuf_tensor("D_nmrow", [128, 1], F32))
            msk = C(nc.sbuf_tensor("D_msk", [128, NE], F32))
            exr = C(nc.sbuf_tensor("D_exr", [128, NE], F32))
            smr = C(nc.sbuf_tensor("D_smr", [128, 1], F32))
            mpr = C(nc.sbuf_tensor("D_mpr", [128, 1], F32))
            mkb = C(nc.sbuf_tensor("D_mkb", [128, NE], BF16))
            stl_v = C(nc.sbuf_tensor("D_stl_v", [128, 8], F32))
            stl_s = C(nc.sbuf_tensor("D_stl_s", [128, 8], F32))
            outst = [C(nc.sbuf_tensor(f"D_outst{i}", [128, 512], F32)) for i in range(2)]
            pr = C(nc.psum_tensor("D_pr", [128, NE], F32))
            tp8 = C(nc.psum_tensor("D_tp8", [NE, 128], BF16))
            pm = C(nc.psum_tensor("D_pm", [128, 512], F32))
            ph = [C(nc.psum_tensor(f"D_ph{i}", [128, 512], F32)) for i in range(2)]
            py = [C(nc.psum_tensor(f"D_py{i}", [128, 512], F32)) for i in range(2)]

            P.gate(["s", "v", "g", "t"], ["ldm", "rstd", "brv2"])
            if debug:
                P.emit("g", lambda e: e.dma_start(dbg_st[0:1, :], m_rep[0:1, :]),
                       wait=["mrv"], inc=[("dbg1", 16)])
                P.emit("g", lambda e: e.dma_start(dbg_st[1:2, :], r_rep[0:1, :]),
                       wait=["rrv"], inc=[("dbg1", 16)])
                P.emit("g", lambda e: e.dma_start(dbg_st[2:3, :], m_rep[64:65, :]),
                       wait=["mrv"], inc=[("dbg1", 16)])
                P.emit("g", lambda e: e.dma_start(dbg_st[3:4, :], r_rep[64:65, :]),
                       wait=["rrv"], inc=[("dbg1", 16)])

            for ch in range(4):
                cs = slice(ch * 512, (ch + 1) * 512)
                for t in range(DT):
                    P.emit("y", lambda e, t=t, cs=cs: e.dma_start(
                        x4c[t][:, :], xin[t * 128:(t + 1) * 128, cs]),
                        wait=[P.lag("dov", 0)], inc=[("lxc", 16)])
                    P.emit("v", lambda e, t=t, cs=cs: e.tensor_sub(
                        scr[:, :], x4c[t][:, :], m_rep[:, cs]),
                        wait=[("lxc", P.val("lxc")), "mrv", "rrv",
                              ("dxf", P.val("dxf")), ("dhm", P.val("dhm")),
                              ("drm", P.val("drm")), ("dbg1", P.val("dbg1"))],
                        inc=[("dxs", 1)])
                    P.emit("v", lambda e, t=t, cs=cs: e.tensor_mul(
                        xlf[t][:, :], scr[:, :], r_rep[:, cs]),
                        wait=[("dxs", P.val("dxs"))], inc=[("dxf", 1)])
                    P.emit("v", lambda e, t=t: e.tensor_copy(
                        xln[t][:, :], xlf[t][:, :]),
                        wait=[("dxf", P.val("dxf")), ("dxl", P.val("dxl"))],
                        inc=[("dxl", 1)])
                for tt in range(4):
                    ts_ = slice(tt * 128, (tt + 1) * 128)
                    for ct in range(DT):
                        P.emit("t", lambda e, ct=ct, ts_=ts_: e.matmul(
                            pr[:, :], xlf[ct][:, ts_], wr_sb[:, ct * NE:(ct + 1) * NE],
                            start=(ct == 0), stop=(ct == DT - 1)),
                            wait=[("dxf", P.val("dxf")), ("dlg", P.val("dlg"))],
                            inc=[("drm", 1)] if ct == DT - 1 else ())
                    P.emit("v", lambda e: e.tensor_add(lgt[:, :], brb[:, :], pr[:, :]),
                           wait=[("drm", P.val("drm")), ("dtse", P.val("dtse")),
                                 ("dbg", P.val("dbg")), ("dbg0", P.val("dbg0"))],
                           inc=[("dlg", 1)])
                    P.emit("v", lambda e: e.reduce_max(
                        mrow[:, :], lgt[:, :], axis=mybir.AxisListType.X),
                        wait=[("dlg", P.val("dlg"))], inc=[("dmx", 1)])
                    P.emit("v", lambda e: e.tensor_scalar(
                        nmrow[:, :], mrow[:, :], -1.0, None, ALU.mult),
                        wait=[("dmx", P.val("dmx"))], inc=())
                    P.emit("v", lambda e: e.memset(stl_v[:, :], 0.0),
                           wait=(), inc=[("dnm", 1)])
                    P.emit("g", lambda e: e.tensor_scalar(
                        msk[:, :], lgt[:, :], mrow[:, 0:1], None, ALU.is_ge),
                        wait=[("dnm", P.val("dnm"))], inc=[("dms", 1)])
                    P.emit("s", lambda e: e.activation(
                        exr[:, :], lgt[:, :], AF.Exp, bias=nmrow[:, 0:1]),
                        wait=[("dms", P.val("dms"))], inc=())
                    P.emit("s", lambda e: e.activation(
                        stl_s[:, :], exr[:, :], AF.Square),
                        wait=(), inc=[("dex", 1)])
                    P.emit("v", lambda e: e.reduce_sum(
                        smr[:, :], exr[:, :], axis=mybir.AxisListType.X),
                        wait=[("dex", P.val("dex"))], inc=[("dsm", 1)])
                    P.emit("v", lambda e: e.reciprocal(mpr[:, :], smr[:, :]),
                           wait=[("dsm", P.val("dsm"))], inc=())
                    P.emit("v", lambda e: e.memset(stl_v[:, :], 0.0),
                           wait=(), inc=[("dmp", 1)])
                    P.emit("g", lambda e: e.tensor_scalar(
                        mkb[:, :], msk[:, :], mpr[:, 0:1], None, ALU.mult),
                        wait=[("dmp", P.val("dmp"))], inc=[("dmk", 1)])
                    if debug and ch == 0 and tt == 0:
                        P.emit("g", lambda e: e.dma_start(dbg_r0[:, 0:NE], lgt[:, :]),
                               wait=[("dlg", P.val("dlg"))], inc=[("dbg0", 16)])
                        P.emit("g", lambda e: e.dma_start(dbg_r0[:, NE:NE + 1], mrow[:, :], allow_slow_non_contiguous=True),
                               wait=[("dnm", P.val("dnm"))], inc=[("dbg0", 16)])
                        P.emit("g", lambda e: e.dma_start(dbg_r0[:, NE + 1:2 * NE + 1], msk[:, :]),
                               wait=[("dms", P.val("dms"))], inc=[("dbg0", 16)])
                        P.emit("g", lambda e: e.dma_start(dbg_r0[:, 2 * NE + 1:3 * NE + 1], exr[:, :]),
                               wait=[("dex", P.val("dex"))], inc=[("dbg0", 16)])
                        P.emit("g", lambda e: e.dma_start(dbg_r0[:, 3 * NE + 1:3 * NE + 2], smr[:, :], allow_slow_non_contiguous=True),
                               wait=[("dsm", P.val("dsm"))], inc=[("dbg0", 16)])
                        P.emit("g", lambda e: e.dma_start(dbg_r0[:, 3 * NE + 2:3 * NE + 3], mpr[:, :], allow_slow_non_contiguous=True),
                               wait=[("dmp", P.val("dmp"))], inc=[("dbg0", 16)])
                        P.emit("g", lambda e: e.dma_start(dbg_r0[:, 3 * NE + 3:4 * NE + 3], mkb[:, :]),
                               wait=[("dmk", P.val("dmk"))], inc=[("dbg0", 16)])
                    if debug and ch == 0:
                        pass
                    if debug and ch == 3 and tt == 0:
                        P.emit("g", lambda e: e.dma_start(dbg_lg, lgt[:, :]),
                               wait=[("dlg", P.val("dlg"))], inc=[("dbg", 16)])
                        P.emit("g", lambda e: e.dma_start(dbg_mx, mrow[:, :]),
                               wait=[("dmx", P.val("dmx"))], inc=[("dbg", 16)])
                        P.emit("g", lambda e: e.dma_start(dbg_ms, msk[:, :]),
                               wait=[("dms", P.val("dms"))], inc=[("dbg", 16)])
                        P.emit("g", lambda e: e.dma_start(dbg_mb, mkb[:, :]),
                               wait=[("dmk", P.val("dmk"))], inc=[("dbg", 16)])
                    P.emit("t", lambda e: e.transpose(
                        tp8[:, :], mkb[:, 0:NE], id_sb[:, :]),
                        wait=[("dmk", P.val("dmk")), ("dtse", P.val("dtse"))],
                        inc=[("dtp", 1)])
                    P.emit("v", lambda e, ts_=ts_: e.tensor_copy(
                        maskpT[:, ts_], tp8[:, :]),
                        wait=[("dtp", P.val("dtp")), ("dmr", P.val("dmr")),
                              ("db2", P.val("db2")), ("dbgm", P.val("dbgm"))],
                        inc=[("dtse", 1)])
                for do in range(DT):
                    P.emit("t", lambda e, do=do: e.matmul(
                        py[do % 2][:, :], b2_sb[:, do * 128:(do + 1) * 128],
                        maskpT[:, :], start=True, stop=True),
                        wait=[("dtse", P.val("dtse")), P.lag("dyv", 1)],
                        inc=[("db2", 1)])
                    P.emit("v", lambda e, do=do: e.tensor_copy(
                        y_acc[do][:, :], py[do % 2][:, :]),
                        wait=[("db2", P.val("db2")), ("dov", P.val("dov"))],
                        inc=[("dyv", 1)])
                if debug:
                    P.emit("g", lambda e, ch=ch: e.dma_start(
                        dbg_mk4[ch * NE:(ch + 1) * NE, :], maskpT[:, :]),
                        wait=[("dtse", P.val("dtse"))], inc=[("dbgm", 16)])
                if debug and ch == 0:
                    for t in range(DT):
                        P.emit("g", lambda e, t=t: e.dma_start(
                            dbg_x0[t * 128:(t + 1) * 128, :], xlf[t][:, :]),
                            wait=[("dxf", P.val("dxf"))], inc=[("dbg1", 16)])
                if debug and ch == 3:
                    P.emit("g", lambda e: e.dma_start(dbg_mk, maskpT[:, :]),
                           wait=[("dtse", P.val("dtse"))], inc=[("dbg", 16)])
                    for t in range(DT):
                        P.emit("g", lambda e, t=t: e.dma_start(
                            dbg_xl[t * 128:(t + 1) * 128, :], xlf[t][:, :]),
                            wait=[("dxf", P.val("dxf"))], inc=[("dbg", 16)])
                    for do in range(DT):
                        P.emit("g", lambda e, do=do: e.dma_start(
                            dbg_y0[do * 128:(do + 1) * 128, :], y_acc[do][:, :]),
                            wait=[("dyv", P.val("dyv"))], inc=[("dbg", 16)])
                for ex in range(NE):
                    for part in range(4):
                        P.emit("g", lambda e, ex=ex, part=part: e.dma_start(
                            w1sb[:, part * 4096:(part + 1) * 4096],
                            w1e[ex * 128:(ex + 1) * 128,
                                part * 4096:(part + 1) * 4096]),
                            wait=[("dhm", P.val("dhm"))], inc=[("dw1", 16)])
                    for part in range(4):
                        P.emit("g", lambda e, ex=ex, part=part: e.dma_start(
                            w2sb[:, part * 4096:(part + 1) * 4096],
                            w2e[ex * 128:(ex + 1) * 128,
                                part * 4096:(part + 1) * 4096]),
                            wait=[("dym", P.val("dym"))], inc=[("dw2", 16)])
                    P.emit("t", lambda e, ex=ex: e.matmul(
                        pm[:, :], e8_sb[:, ex * 128:(ex + 1) * 128], maskpT[:, :],
                        start=True, stop=True),
                        wait=[("dtse", P.val("dtse")), P.lag("dmrv", 0)],
                        inc=[("dmr", 1)] + ([] if ex < NE - 1 else []))
                    P.emit("v", lambda e, sl=ex % 2: e.tensor_copy(
                        mrep[sl][:, :], pm[:, :]),
                        wait=[("dmr", P.val("dmr")), P.lag("dhtv", DH2T)],
                        inc=[("dmrv", 1)])
                    for hb in range(DH2T):
                        for t in range(DT):
                            P.emit("t", lambda e, hb=hb, t=t: e.matmul(
                                ph[hb % 2][:, :],
                                w1sb[:, hb * D + t * 128:hb * D + (t + 1) * 128],
                                xln[t][:, :], start=(t == 0), stop=(t == DT - 1)),
                                wait=[("dw1", P.val("dw1")), P.lag("dhrl", 1),
                                      ("dxl", P.val("dxl"))],
                                inc=[("dhm", 1)] if t == DT - 1 else ())
                        P.emit("s", lambda e, ex=ex, hb=hb, sl=hb % 2: e.activation(
                            rl[sl][:, :], ph[hb % 2][:, :], AF.Relu,
                            bias=b1_sb[:, ex * DH2T + hb:ex * DH2T + hb + 1]),
                            wait=[("dhm", P.val("dhm")), P.lag("dhtv", 1)],
                            inc=[("dhrl", 1)])
                        P.emit("v", lambda e, hb=hb, sl=hb % 2, msl=ex % 2: e.tensor_mul(
                            hT[hb][:, :], rl[sl][:, :], mrep[msl][:, :]),
                            wait=[("dhrl", P.val("dhrl")), ("dmrv", P.val("dmrv")),
                                  ("dym", P.val("dym"))],
                            inc=[("dhtv", 1)])
                    if debug and ch == 3 and ex == 7:
                        for hb in range(DH2T):
                            P.emit("g", lambda e, hb=hb: e.dma_start(
                                dbg_h[hb * 128:(hb + 1) * 128, :], hT[hb][:, :]),
                                wait=[("dhtv", P.val("dhtv"))], inc=[("dbg", 16)])
                        P.emit("g", lambda e: e.dma_start(dbg_mr, mrep[0][:, :]),
                               wait=[("dmrv", P.val("dmrv"))], inc=[("dbg", 16)])
                    for do in range(DT):
                        for hb in range(DH2T):
                            P.emit("t", lambda e, do=do, hb=hb: e.matmul(
                                py[do % 2][:, :],
                                w2sb[:, do * DH2 + hb * 128:do * DH2 + (hb + 1) * 128],
                                hT[hb][:, :], start=(hb == 0), stop=(hb == DH2T - 1)),
                                wait=[("dw2", P.val("dw2")), ("dhtv", P.val("dhtv")),
                                      P.lag("dyv", 1)],
                                inc=[("dym", 1)] if hb == DH2T - 1 else ())
                        P.emit("v", lambda e, do=do: e.tensor_add(
                            y_acc[do][:, :], y_acc[do][:, :], py[do % 2][:, :]),
                            wait=[("dym", P.val("dym")), ("dbg", P.val("dbg"))],
                            inc=[("dyv", 1)])
                if debug:
                    P.emit("g", lambda e, ch=ch: e.dma_start(
                        dbg_mk5[ch * NE:(ch + 1) * NE, :], maskpT[:, :]),
                        wait=[("dyv", P.val("dyv"))], inc=[("dbgm", 16)])
                if debug and ch == 3:
                    for do in range(DT):
                        P.emit("g", lambda e, do=do: e.dma_start(
                            dbg_ya[do * 128:(do + 1) * 128, :], y_acc[do][:, :]),
                            wait=[("dyv", P.val("dyv"))], inc=[("dbg", 16)])
                for do in range(DT):
                    P.emit("v", lambda e, do=do, sl=do % 2: e.tensor_add(
                        outst[sl][:, :], y_acc[do][:, :], x4c[do][:, :]),
                        wait=[("dyv", P.val("dyv")), ("lxc", P.val("lxc")),
                              P.lag("dst", 16)],
                        inc=[("dov", 1)])
                    P.emit("y", lambda e, do=do, cs=cs, sl=do % 2: e.dma_start(
                        xoT[do * 128:(do + 1) * 128, cs], outst[sl][:, :]),
                        wait=[("dov", P.val("dov"))], inc=[("dst", 16)])
            P.emit("y", None, wait=[("dst", P.val("dst"))])
            if debug:
                P.emit("g", None, wait=[("dbg", P.val("dbg"))])
            P.flush("phD2")


def prep_l1_ffn(ins):
    w = {}
    w["wr"] = _c(ins["L1_ffn_Wr"])
    w["brv"] = _c(ins["L1_ffn_br"].reshape(1, NE))
    w["w1e"] = _c(np.concatenate([host_pwide(host_blocks(_c(ins["L1_ffn_W1"][e])))
                                  for e in range(NE)], axis=0))
    w["w2e"] = _c(np.concatenate([host_pwide(host_blocks(_c(ins["L1_ffn_W2"][e])))
                                  for e in range(NE)], axis=0))
    w["b1v"] = _c(np.concatenate([host_vec(ins["L1_ffn_b1"][e]) for e in range(NE)],
                                 axis=1))
    w["b2v"] = _c(ins["L1_ffn_b2"])
    return w



def prep_l0_attn(ins):
    w = {}
    w["wq"] = host_blocks(_c(ins["L0_attn_Wq"].T))
    w["wk"] = host_blocks(_c(ins["L0_attn_Wk"].T))
    w["wv"] = host_blocks(_c(ins["L0_attn_Wv"].T))
    w["wo"] = host_blocks(_c(ins["L0_attn_Wo"].T))
    w["bqv"] = host_vec(ins["L0_attn_bq"])
    w["bkv"] = host_vec(ins["L0_attn_bk"])
    w["bvv"] = host_vec(ins["L0_attn_bv"])
    w["bov"] = host_vec(ins["L0_attn_bo"])
    w["ri"] = _c(ins["L0_attn_rel_W"][:, :D].T)
    w["rj"] = _c(ins["L0_attn_rel_W"][:, D:].T)
    w["relb"] = _c(ins["L0_attn_rel_b"].reshape(3, 1))
    w["lbT"] = _c(ins["L0_attn_logical_bias"].T)
    w["gw"] = _c(ins["L0_attn_gate_W"].T)
    w["gbv"] = _c(ins["L0_attn_gate_b"].reshape(H, 1))
    return w


def build_fused(debug=False):
    nc = bass.Bass("TRN2", target_bir_lowering=False)
    xT = nc.dram_tensor("xT", [D, S], F32, kind="ExternalInput").ap()
    xoT = nc.dram_tensor("xoT", [D, S], F32, kind="ExternalOutput").ap()
    x2scr = nc.dram_tensor("x2scr", [D, S], F32, kind="Internal").ap()
    x3scr = nc.dram_tensor("x3scr", [D, S], F32, kind="Internal").ap()
    x4scr = nc.dram_tensor("x4scr", [D, S], F32, kind="Internal").ap()
    with ExitStack() as ctx:
        P = Plan(nc, ctx)
        emit_A(nc, P, ctx, xT, x2scr, debug=debug)
        emit_B(nc, P, ctx, x2scr, x3scr, debug=debug)
        emit_C(nc, P, ctx, x3scr, x4scr, debug=debug)
        emit_D(nc, P, ctx, x4scr, xoT, debug=debug)
    return nc


def kernel(**inputs):
    ins = {k: np.asarray(v, dtype=np.float32) for k, v in inputs.items()}
    x = ins["L0_attn_hidden_states"]
    w = {}
    for pfx, prep in (("A", prep_l0_attn), ("B", prep_l0_ffn),
                      ("C", prep_l1_attn), ("D", prep_l1_ffn)):
        for k, v in prep(ins).items():
            w[f"{pfx}_{k}"] = v
    nc = build_fused()
    maps = [dict(w, xT=_c(x[c].T)) for c in range(NCORE)]
    res = run_bass_kernel_spmd(nc, maps, core_ids=list(range(NCORE)))
    out = np.stack([res.results[c]["xoT"].T for c in range(NCORE)])
    return np.ascontiguousarray(out.astype(np.float32))


# version 07 (final)
# speedup vs baseline: 1.4580x; 1.1472x over previous
import numpy as np
import ml_dtypes
from contextlib import ExitStack

import concourse.bass as bass
import concourse.mybir as mybir
from concourse.bass_utils import run_bass_kernel_spmd

F32 = mybir.dt.float32
BF16 = mybir.dt.bfloat16
AF = mybir.ActivationFunctionType
ALU = mybir.AluOpType
BF = ml_dtypes.bfloat16

D, S, H, HD = 1024, 2048, 16, 64
DT = D // 128
ST = S // 128
NCORE = 8
EPS = 1e-5


def _c(a):
    return np.ascontiguousarray(a)


def host_blocks(WT):
    IN, OUT = WT.shape
    a = WT.reshape(IN // 128, 128, OUT // 128, 128).transpose(2, 1, 0, 3)
    return _c(a.reshape(OUT, IN))


def host_vec(v):
    return _c(np.asarray(v).reshape(-1, 128).T)


def host_pwide(blocks2d):
    nb = blocks2d.shape[0] // 128
    return _c(blocks2d.reshape(nb, 128, -1).transpose(1, 0, 2).reshape(128, -1))


ENG_ATTR = {"t": "tensor", "v": "vector", "s": "scalar", "g": "gpsimd", "y": "sync"}


class Plan:

    def __init__(self, nc, ctx):
        self.nc = nc
        self.ctx = ctx
        self.ops = []
        self.cnt = {}
        self.handles = {}
        self.maxwait = {}

    def sem(self, name):
        if name not in self.cnt:
            self.cnt[name] = 0
            self.handles[name] = self.ctx.enter_context(self.nc.semaphore(name))
        return name

    def val(self, name):
        return self.cnt.get(name, 0)

    def lag(self, name, k):
        return (name, max(0, self.cnt.get(name, 0) - k))

    def emit(self, eng, fn, wait=(), inc=()):
        waits = []
        for w in wait:
            if isinstance(w, tuple):
                s, v = w
            else:
                s, v = w, self.cnt.get(w, 0)
            self.sem(s)
            if v > 0:
                waits.append((s, v))
        incs = []
        for s, k in inc:
            self.sem(s)
            self.cnt[s] += k
            incs.append((s, k))
        assert len(incs) <= 1, f"hw allows one sync update per instruction: {incs}"
        self.ops.append((eng, fn, waits, incs))

    def gate(self, engines, wait):
        for e in engines:
            self.emit(e, None, wait=wait)

    def flush(self, name=None):
        if not self.ops:
            return
        ops = self.ops
        self.ops = []
        with self.nc.allow_low_precision("bf16 matmul operands / bf16 psum transpose are deliberate"), \
             self.nc.Block() as blk:
            for engname, attr in ENG_ATTR.items():
                eops = [o for o in ops if o[0] == engname]
                if not eops:
                    continue

                def body(eng, eops=eops, engname=engname):
                    seen = self.maxwait.setdefault(engname, {})
                    for _, fn, waits, incs in eops:
                        for s, v in waits:
                            if v > seen.get(s, 0):
                                eng.wait_ge(self.handles[s], v)
                                seen[s] = v
                        inst = fn(eng) if fn is not None else None
                        for s, k in incs:
                            if inst is None:
                                inst = eng.nop()
                            inst.then_inc(self.handles[s], k)

                getattr(blk, attr)(body)



def emit_rep(P, rep_ps, rep_ps2, mv, vv, m_rep, r_rep, ones_f32r, pfx):
    for c in range(4):
        cs = slice(c * 512, (c + 1) * 512)
        P.emit("t", lambda e, cs=cs: e.matmul(
            rep_ps[:, :], ones_f32r[0:1, :], mv[0:1, cs], start=True, stop=True),
            wait=["rstd", P.lag("mrv", 0)], inc=[("mrt", 1)])
        P.emit("v", lambda e, cs=cs: e.tensor_copy(m_rep[:, cs], rep_ps[:, :]),
               wait=["mrt"], inc=[("mrv", 1)])
        P.emit("t", lambda e, cs=cs: e.matmul(
            rep_ps2[:, :], ones_f32r[0:1, :], vv[0:1, cs], start=True, stop=True),
            wait=["rstd", P.lag("rrv", 0)], inc=[("mrt", 1)])
        P.emit("v", lambda e, cs=cs: e.tensor_copy(r_rep[:, cs], rep_ps2[:, :]),
               wait=["mrt"], inc=[("rrv", 1)])



def emit_A(nc, P, ctx, xin, xout, debug=False):
    xT = xin
    wq = nc.dram_tensor("A_wq", [D, D], F32, kind="ExternalInput").ap()
    wk = nc.dram_tensor("A_wk", [D, D], F32, kind="ExternalInput").ap()
    wv = nc.dram_tensor("A_wv", [D, D], F32, kind="ExternalInput").ap()
    wo = nc.dram_tensor("A_wo", [D, D], F32, kind="ExternalInput").ap()
    bqv = nc.dram_tensor("A_bqv", [128, DT], F32, kind="ExternalInput").ap()
    bkv = nc.dram_tensor("A_bkv", [128, DT], F32, kind="ExternalInput").ap()
    bvv = nc.dram_tensor("A_bvv", [128, DT], F32, kind="ExternalInput").ap()
    bov = nc.dram_tensor("A_bov", [128, DT], F32, kind="ExternalInput").ap()
    ri = nc.dram_tensor("A_ri", [D, 3], F32, kind="ExternalInput").ap()
    rj = nc.dram_tensor("A_rj", [D, 3], F32, kind="ExternalInput").ap()
    relb = nc.dram_tensor("A_relb", [3, 1], F32, kind="ExternalInput").ap()
    lbT = nc.dram_tensor("A_lbT", [3, H], F32, kind="ExternalInput").ap()
    gw = nc.dram_tensor("A_gw", [D, H], F32, kind="ExternalInput").ap()
    gbv = nc.dram_tensor("A_gbv", [H, 1], F32, kind="ExternalInput").ap()
    x2T = xout
    if debug:
        dbg_q = nc.dram_tensor("A_dbg_q", [D, S], F32, kind="ExternalOutput").ap()
        dbg_k = nc.dram_tensor("A_dbg_k", [D, S], F32, kind="ExternalOutput").ap()
        dbg_at = nc.dram_tensor("A_dbg_at", [D, S], F32, kind="ExternalOutput").ap()

    id128 = nc.inline_tensor(np.eye(128, dtype=BF), "A_id128")
    e16 = np.zeros((16, 48), dtype=BF)
    for h in range(16):
        e16[h, 3 * h:3 * h + 3] = 1.0
    e16sel = nc.inline_tensor(e16, "A_e16sel")

    with ExitStack() as bctx:
        E = bctx.enter_context

        bq_sb = E(nc.sbuf_tensor("A_bq_sb", [128, DT], F32))
        bk_sb = E(nc.sbuf_tensor("A_bk_sb", [128, DT], F32))
        bv_sb = E(nc.sbuf_tensor("A_bv_sb", [128, DT], F32))
        bo_sb = E(nc.sbuf_tensor("A_bo_sb", [128, DT], F32))
        relb_sb = E(nc.sbuf_tensor("A_relb_sb", [3, 1], F32))
        gb_sb = E(nc.sbuf_tensor("A_gb_sb", [H, 1], F32))
        lbT_sb = E(nc.sbuf_tensor("A_lbT_sb", [3, H], F32))
        id_sb = E(nc.sbuf_tensor("A_id_sb", [128, 128], BF16))
        e16_sb = E(nc.sbuf_tensor("A_e16_sb", [16, 48], BF16))
        ones_f32 = E(nc.sbuf_tensor("A_ones_f32", [128, 1], F32))
        ones_bf = E(nc.sbuf_tensor("A_ones_bf", [1, 128], BF16))
        eps_sb = E(nc.sbuf_tensor("A_eps_sb", [1, 1], F32))
        ones_f32r = E(nc.sbuf_tensor("A_ones_f32r", [1, 128], F32))
        aT = E(nc.sbuf_tensor("A_aT", [3, S], BF16))
        cT = E(nc.sbuf_tensor("A_cT", [3, S], BF16))
        gateT = E(nc.sbuf_tensor("A_gateT", [16, S], BF16))

        for nm, sb, ap_ in (("bq", bq_sb, bqv), ("bk", bk_sb, bkv), ("bv", bv_sb, bvv),
                            ("bo", bo_sb, bov), ("rb", relb_sb, relb),
                            ("gb", gb_sb, gbv), ("lb", lbT_sb, lbT)):
            P.emit("y", lambda e, sb=sb, a=ap_: e.dma_start(sb[:, :], a), inc=[("ldm", 16)])
        P.emit("g", lambda e: e.dma_start(id_sb[:, :], id128.ap()), inc=[("ldm", 16)])
        P.emit("g", lambda e: e.dma_start(e16_sb[:, :], e16sel.ap()), inc=[("ldm", 16)])
        LDM = 9 * 16
        P.emit("v", lambda e: e.memset(ones_f32[:, :], 1.0), inc=[("ldv", 1)])
        P.emit("v", lambda e: e.memset(ones_bf[:, :], 1.0), inc=[("ldv", 1)])
        P.emit("v", lambda e: e.memset(eps_sb[:, :], EPS), inc=[("eps", 1)])
        P.emit("v", lambda e: e.memset(ones_f32r[:, :], 1.0), inc=[("ldv", 1)])
        P.emit("v", lambda e: e.tensor_scalar(bk_sb[:, :], bk_sb[:, :], 0.125, None, ALU.mult),
               wait=["ldm"], inc=[("bk8", 1)])

        qT = [E(nc.sbuf_tensor(f"A_qT{t}", [128, S], BF16)) for t in range(DT)]
        kT = [E(nc.sbuf_tensor(f"A_kT{t}", [128, S], BF16)) for t in range(DT)]
        v_aug = [E(nc.sbuf_tensor(f"A_vaug{j}", [128, H * 65], BF16)) for j in range(ST)]

        cx = bctx.enter_context(ExitStack())
        x_ln = [cx.enter_context(nc.sbuf_tensor(f"A_x_ln{t}", [128, S], BF16))
                for t in range(DT)]

        with ExitStack() as c1:
            C = c1.enter_context
            xr = [C(nc.sbuf_tensor(f"A_xr{i}", [128, S], F32)) for i in range(2)]
            sq_sb = C(nc.sbuf_tensor("A_sq", [128, S], F32))
            mv = C(nc.sbuf_tensor("A_mv", [1, S], F32))
            vv = C(nc.sbuf_tensor("A_vv", [1, S], F32))
            m_rep = C(nc.sbuf_tensor("A_m_rep", [128, S], F32))
            r_rep = C(nc.sbuf_tensor("A_r_rep", [128, S], F32))

            with ExitStack() as c1s:
                stat_ps = c1s.enter_context(nc.psum_tensor("A_stat_ps", [1, S], F32))
                stat_ps2 = c1s.enter_context(nc.psum_tensor("A_stat_ps2", [1, S], F32))
                P.gate(["t"], ["ldv"])
                for t in range(DT):
                    P.emit("y", lambda e, t=t, sl=t % 2: e.dma_start(
                        xr[sl][:, :], xT[t * 128:(t + 1) * 128, :]),
                        wait=[P.lag("sxu", 0), P.lag("ssq", 0)], inc=[("ldx", 16)])
                    P.emit("s", lambda e, t=t, sl=t % 2: e.activation(
                        sq_sb[:, :], xr[sl][:, :], AF.Square),
                        wait=["ldx", ("squ", P.val("squ"))],
                        inc=[("ssq", 1)])
                    for c in range(4):
                        P.emit("t", lambda e, t=t, c=c, sl=t % 2: e.matmul(
                            stat_ps[:, c * 512:(c + 1) * 512], ones_f32[:, :],
                            xr[sl][:, c * 512:(c + 1) * 512],
                            start=(t == 0), stop=(t == DT - 1)),
                            wait=["ldx"],
                            inc=[("sxu", 1)] if c == 3 else ())
                    for c in range(4):
                        P.emit("t", lambda e, t=t, c=c: e.matmul(
                            stat_ps2[:, c * 512:(c + 1) * 512], ones_f32[:, :],
                            sq_sb[:, c * 512:(c + 1) * 512],
                            start=(t == 0), stop=(t == DT - 1)),
                            wait=[("ssq", P.val("ssq"))],
                            inc=[("squ", 1)] if c == 3 else ())
                P.emit("v", lambda e: e.tensor_scalar(
                    mv[:, :], stat_ps[:, :], 1.0 / D, None, ALU.mult),
                    wait=["sxu", "squ"], inc=[("lnv", 1)])
                P.emit("v", lambda e: e.tensor_scalar(
                    vv[:, :], stat_ps2[:, :], 1.0 / D, None, ALU.mult),
                    wait=["sxu", "squ"], inc=[("lnv", 1)])
                P.emit("v", lambda e: e.scalar_tensor_tensor(
                    sq_sb[0:1, 0:S], mv[0:1, :], -1.0, mv[0:1, :], ALU.mult, ALU.mult),
                    wait=["lnv"], inc=[("lnv", 1)])
                P.emit("v", lambda e: e.tensor_add(vv[:, :], vv[:, :], sq_sb[0:1, 0:S]),
                       wait=["lnv"], inc=[("lnv", 1)])
                P.emit("v", lambda e: e.memset(sq_sb[0:1, 0:8], 0.0),
                       inc=[("lnv", 1)])
                P.emit("s", lambda e: e.activation(
                    vv[:, :], vv[:, :], AF.Sqrt, bias=eps_sb[0:1, 0:1]),
                    wait=["lnv", "eps"], inc=())
                P.emit("s", lambda e: e.activation(
                    sq_sb[0:1, 0:8], sq_sb[0:1, 0:8], AF.Square),
                    inc=[("lnv", 1)])
                P.emit("v", lambda e: e.reciprocal(vv[:, :], vv[:, :]),
                       wait=["lnv"], inc=())
                P.emit("v", lambda e: e.memset(sq_sb[0:1, 0:8], 0.0),
                       inc=[("rstd", 1)])
                P.flush("ph1a")
            with ExitStack() as c1b:
                rep_ps = c1b.enter_context(nc.psum_tensor("A_rep_ps", [128, 512], F32))
                rep_ps2 = c1b.enter_context(nc.psum_tensor("A_rep_ps2", [128, 512], F32))
                emit_rep(P, rep_ps, rep_ps2, mv, vv, m_rep, r_rep, ones_f32r, "A")
                for t in range(DT):
                    P.emit("y", lambda e, t=t, sl=t % 2: e.dma_start(
                        xr[sl][:, :], xT[t * 128:(t + 1) * 128, :]),
                        wait=[P.lag("xsub", 1)], inc=[("ldx", 16)])
                    P.emit("v", lambda e, t=t, sl=t % 2: e.tensor_sub(
                        sq_sb[:, :], xr[sl][:, :], m_rep[:, :]),
                        wait=["mrv", "rrv", "ldx",
                              ("xlnd", P.val("xlnd"))],
                        inc=[("xsub", 1)])
                    P.emit("v", lambda e, t=t: e.tensor_mul(
                        x_ln[t][:, :], sq_sb[:, :], r_rep[:, :]),
                        wait=[("xsub", P.val("xsub"))], inc=[("xlnd", 1)])
                P.flush("ph1b")

        with ExitStack() as c2:
            C = c2.enter_context
            wbuf = [C(nc.sbuf_tensor(f"A_wbuf{i}", [128, D], BF16)) for i in range(2)]
            rwbuf = C(nc.sbuf_tensor("A_rwbuf", [128, 6 * DT], BF16))
            pstl = C(nc.sbuf_tensor("A_pstl", [1, 8], F32))
            gwbuf = C(nc.sbuf_tensor("A_gwbuf", [128, H * DT], BF16))
            vT_t = C(nc.sbuf_tensor("A_vT_t", [128, S], BF16))
            pp = [C(nc.psum_tensor(f"A_pp{i}", [128, 512], F32)) for i in range(4)]
            tp = C(nc.psum_tensor("A_tp", [128, 128], BF16))
            p3 = C(nc.psum_tensor("A_p3", [3, 512], F32))
            p16 = C(nc.psum_tensor("A_p16", [16, 512], F32))

            P.gate(["s", "v", "g"], ["ldm", "bk8"])

            for t in range(DT):
                P.emit("g", lambda e, t=t: e.dma_start(
                    rwbuf[:, t * 6:t * 6 + 3], ri[t * 128:(t + 1) * 128, :]), inc=[("ldr", 16)])
                P.emit("g", lambda e, t=t: e.dma_start(
                    rwbuf[:, t * 6 + 3:t * 6 + 6], rj[t * 128:(t + 1) * 128, :]), inc=[("ldr", 16)])
                P.emit("g", lambda e, t=t: e.dma_start(
                    gwbuf[:, t * H:(t + 1) * H], gw[t * 128:(t + 1) * 128, :]), inc=[("ldr", 16)])

            for c in range(4):
                cs = slice(c * 512, (c + 1) * 512)
                for t in range(DT):
                    P.emit("t", lambda e, t=t, cs=cs: e.matmul(
                        p3[:, :], rwbuf[:, t * 6:t * 6 + 3], x_ln[t][:, cs],
                        start=(t == 0), stop=(t == DT - 1)),
                        wait=[("ldr", 48 * DT), ("aev", P.val("aev"))],
                        inc=[("limm", 1)] if t == DT - 1 else ())
                P.emit("s", lambda e, cs=cs: e.activation(
                    aT[:, cs], p3[:, :], AF.Exp, bias=relb_sb[:, 0:1]),
                    wait=[("limm", P.val("limm"))], inc=())
                P.emit("s", lambda e: e.activation(pstl[:, :], pstl[:, :], AF.Square),
                       inc=[("aev", 1)])
                for t in range(DT):
                    P.emit("t", lambda e, t=t, cs=cs: e.matmul(
                        p3[:, :], rwbuf[:, t * 6 + 3:t * 6 + 6], x_ln[t][:, cs],
                        start=(t == 0), stop=(t == DT - 1)),
                        wait=[("aev", P.val("aev"))],
                        inc=[("ljmm", 1)] if t == DT - 1 else ())
                P.emit("s", lambda e, cs=cs: e.activation(
                    cT[:, cs], p3[:, :], AF.Exp),
                    wait=[("ljmm", P.val("ljmm"))], inc=())
                P.emit("s", lambda e: e.activation(pstl[:, :], pstl[:, :], AF.Square),
                       inc=[("aev", 1)])
                for t in range(DT):
                    P.emit("t", lambda e, t=t, cs=cs: e.matmul(
                        p16[:, :], gwbuf[:, t * H:(t + 1) * H], x_ln[t][:, cs],
                        start=(t == 0), stop=(t == DT - 1)),
                        wait=[("gev", P.val("gev"))],
                        inc=[("gmm", 1)] if t == DT - 1 else ())
                P.emit("s", lambda e, cs=cs: e.activation(
                    gateT[:, cs], p16[:, :], AF.Sigmoid, bias=gb_sb[:, 0:1]),
                    wait=[("gmm", P.val("gmm"))], inc=())
                P.emit("s", lambda e: e.activation(pstl[:, :], pstl[:, :], AF.Square),
                       inc=[("gev", 1)])

            def proj_dmajor(wdram, bias_sb, scale, out_tiles, ppbase):
                for ob in range(DT):
                    slot = ob % 2
                    P.emit("g", lambda e, ob=ob, slot=slot, w=wdram: e.dma_start(
                        wbuf[slot][:, :], w[ob * 128:(ob + 1) * 128, :]),
                        wait=[P.lag("pev", 4)], inc=[("wld", 16)])
                    for c in range(4):
                        cs = slice(c * 512, (c + 1) * 512)
                        ps = pp[ppbase + (c % 2)]
                        for t in range(DT):
                            P.emit("t", lambda e, slot=slot, t=t, cs=cs, ps=ps: e.matmul(
                                ps[:, :], wbuf[slot][:, t * 128:(t + 1) * 128],
                                x_ln[t][:, cs], start=(t == 0), stop=(t == DT - 1)),
                                wait=[("wld", P.val("wld")), P.lag("pev", 1)],
                                inc=[("pmm", 1)] if t == DT - 1 else ())
                        P.emit("s", lambda e, ob=ob, cs=cs, ps=ps, b=bias_sb, sc=scale,
                               ot=out_tiles: e.activation(
                            ot[ob][:, cs], ps[:, :], AF.Identity,
                            bias=b[:, ob:ob + 1], scale=sc),
                            wait=[("pmm", P.val("pmm"))], inc=[("pev", 1)])

            proj_dmajor(wq, bq_sb, 1.0, qT, 0)
            proj_dmajor(wk, bk_sb, 0.125, kT, 0)

            for j in range(ST):
                P.emit("v", lambda e, j=j: e.memset(v_aug[j][:, :], 1.0),
                       inc=[("vamem", 1)])
            for ob in range(DT):
                slot = ob % 2
                P.emit("g", lambda e, ob=ob, slot=slot: e.dma_start(
                    wbuf[slot][:, :], wv[ob * 128:(ob + 1) * 128, :]),
                    wait=[P.lag("pev", 4)], inc=[("wld", 16)])
                for c in range(4):
                    cs = slice(c * 512, (c + 1) * 512)
                    ps = pp[2 + (c % 2)]
                    for t in range(DT):
                        P.emit("t", lambda e, slot=slot, t=t, cs=cs, ps=ps: e.matmul(
                            ps[:, :], wbuf[slot][:, t * 128:(t + 1) * 128],
                            x_ln[t][:, cs], start=(t == 0), stop=(t == DT - 1)),
                            wait=[("wld", P.val("wld")), P.lag("pev", 1)],
                            inc=[("vmm", 1)] if t == DT - 1 else ())
                    P.emit("s", lambda e, ob=ob, cs=cs, ps=ps: e.activation(
                        vT_t[:, cs], ps[:, :], AF.Identity, bias=bv_sb[:, ob:ob + 1]),
                        wait=[("vmm", P.val("vmm")), ("vtr", P.val("vtr"))],
                        inc=[("pev", 1)])
                for j in range(ST):
                    P.emit("t", lambda e, j=j: e.transpose(
                        tp[:, :], vT_t[:, j * 128:(j + 1) * 128], id_sb[:, :]),
                        wait=[("pev", P.val("pev")), ("vtrc", P.val("vtrc"))],
                        inc=[("vtr", 1)])
                    P.emit("v", lambda e, ob=ob, j=j: e.tensor_copy(
                        v_aug[j][:, (2 * ob) * 65:(2 * ob) * 65 + 64], tp[:, 0:64]),
                        wait=[("vtr", P.val("vtr")), "vamem"], inc=[("vtrc", 1)])
                    P.emit("v", lambda e, ob=ob, j=j: e.tensor_copy(
                        v_aug[j][:, (2 * ob + 1) * 65:(2 * ob + 1) * 65 + 64],
                        tp[:, 64:128]),
                        wait=[("vtr", P.val("vtr")), "vamem"], inc=[("vtrc", 1)])
            P.flush("ph2")

        cx.close()
        attnT = [E(nc.sbuf_tensor(f"A_attnT{t}", [128, S], BF16)) for t in range(DT)]

        with ExitStack() as c3:
            C = c3.enter_context
            r_t = [C(nc.sbuf_tensor(f"A_r{j}", [128, 512], BF16)) for j in range(ST)]
            E2 = [C(nc.sbuf_tensor(f"A_E2_{i}", [128, 1024], BF16)) for i in range(2)]
            TbA = [[C(nc.sbuf_tensor(f"A_Tb{b}_{j}", [128, 512], BF16))
                    for j in range(ST)] for b in range(2)]
            tmv_base = [0, 0]
            idm_done = [0, 0]
            qz = [C(nc.sbuf_tensor(f"A_qz{i}", [128, 512], BF16)) for i in range(2)]
            abuf = C(nc.sbuf_tensor("A_abuf", [3, 512], BF16))
            t1buf = C(nc.sbuf_tensor("A_t1buf", [3, 512], BF16))
            rcp = C(nc.sbuf_tensor("A_rcp", [1, 512], BF16))
            rep64 = C(nc.sbuf_tensor("A_rep64", [64, 512], BF16))
            lnD0 = C(nc.sbuf_tensor("A_lnD0", [128, 512], F32))
            astl = C(nc.sbuf_tensor("A_stl", [1, 8], F32))
            lnD = [lnD0, lnD0]
            lnr = lnD0[0:1, :]
            avcp = [C(nc.sbuf_tensor(f"A_avcp{i}", [65, 512], F32)) for i in range(2)]
            ps_S = [C(nc.psum_tensor(f"A_ps_S{i}", [128, 1024], F32)) for i in range(2)]
            ps_N = [C(nc.psum_tensor(f"A_ps_N{i}", [128, 512], F32)) for i in range(2)]
            ps_AV = C(nc.psum_tensor("A_ps_AV", [65, 512], F32))
            ps_bc = C(nc.psum_tensor("A_ps_bc", [64, 512], F32))

            P.emit("v", lambda e: e.memset(astl[:, :], 0.0), inc=[("qzi", 1)])
            P.emit("v", lambda e: e.memset(qz[0][:, :], 0.0), inc=[("qzi", 1)])
            P.emit("v", lambda e: e.memset(qz[1][:, :], 0.0), inc=[("qzi", 1)])

            for ic in range(4):
                cs = slice(ic * 512, (ic + 1) * 512)
                for j in range(ST):
                    P.emit("t", lambda e, j=j, cs=cs, ps=ps_N[j % 2]: e.matmul(
                        ps[:, :], cT[:, j * 128:(j + 1) * 128], aT[:, cs],
                        start=True, stop=True),
                        wait=[P.lag("rln", 1), ("tmv", P.val("tmv"))],
                        inc=[("dmm", 1)])
                    P.emit("s", lambda e, j=j, ps=ps_N[j % 2]: e.activation(
                        lnD[j % 2][:, :], ps[:, :], AF.Ln),
                        wait=[("dmm", P.val("dmm")), P.lag("rex", 1)],
                        inc=[("rln", 1)])
                    P.emit("s", lambda e, j=j: e.activation(
                        r_t[j][:, :], lnD[j % 2][:, :], AF.Exp, scale=-1.0),
                        wait=[("rln", P.val("rln")), ("tmv", P.val("tmv"))],
                        inc=[("rex", 1)])
                def stage(h):
                    bank = h % 2
                    P.emit("t", lambda e, h=h, cs=cs: e.matmul(
                        ps_N[0][0:3, :], e16_sb[:, 3 * h:3 * h + 3], gateT[:, cs],
                        start=True, stop=True),
                        wait=[("nmm", P.val("nmm")), ("tmv", P.val("tmv")),
                              ("rln", P.val("rln"))],
                        inc=[("g3m", 1)])
                    P.emit("v", lambda e, h=h, cs=cs: e.tensor_scalar(
                        t1buf[:, :], aT[:, cs], lbT_sb[:, h:h + 1], None, ALU.mult),
                        wait=[("nmm", P.val("nmm"))], inc=[("t1v", 1)])
                    P.emit("v", lambda e: e.tensor_mul(abuf[:, :], t1buf[:, :],
                                                       ps_N[0][0:3, :]),
                           wait=[("g3m", P.val("g3m")), ("t1v", P.val("t1v"))],
                           inc=())
                    P.emit("v", lambda e: e.memset(astl[:, :], 0.0),
                           inc=[("abv", 1)])
                    tb = P.val("tmv")
                    for j in range(ST):
                        P.emit("t", lambda e, j=j, ps=ps_N[j % 2]: e.matmul(
                            ps[:, :], cT[:, j * 128:(j + 1) * 128], abuf[:, :],
                            start=True, stop=True),
                            wait=[("abv", P.val("abv")), P.lag("tmv", 1),
                                  ("rln", P.val("rln"))],
                            inc=[("nmm", 1)])
                        P.emit("v", lambda e, j=j, bank=bank, ps=ps_N[j % 2]: e.tensor_mul(
                            TbA[bank][j][:, :], ps[:, :], r_t[j][:, :]),
                            wait=[("nmm", P.val("nmm")), ("rex", P.val("rex")),
                                  ("idm", idm_done[bank])],
                            inc=[("tmv", 1)])
                    tmv_base[bank] = tb

                def consume(h):
                    bank = h % 2
                    par = h % 2
                    ht = h // 2
                    P.emit("v", lambda e, par=par, ht=ht, cs=cs: e.tensor_copy(
                        qz[par][par * 64:par * 64 + 64, :],
                        qT[ht][par * 64:par * 64 + 64, cs]),
                        wait=["qzi", P.lag("qzu" + str(par), 0)],
                        inc=())
                    P.emit("v", lambda e: e.memset(astl[:, :], 0.0),
                           inc=[("qzc" + str(par), 1)])
                    for grp in range(8):
                        half = grp % 2
                        sb = ps_S[half]
                        eh = "exh" + str(half)
                        for jj2 in range(2):
                            j = grp * 2 + jj2
                            col = slice(jj2 * 512, (jj2 + 1) * 512)
                            P.emit("t", lambda e, ht=ht, j=j, col=col, sb=sb, par=par: e.matmul(
                                sb[:, col], kT[ht][:, j * 128:(j + 1) * 128],
                                qz[par][:, :], start=True, stop=False),
                                wait=[("qzc" + str(par), P.val("qzc" + str(par))),
                                      (eh, P.val(eh))],
                                inc=())
                            P.emit("t", lambda e, sb=sb, col=col, j=j, bank=bank: e.matmul(
                                sb[:, col], id_sb[:, :], TbA[bank][j][:, :],
                                start=False, stop=True),
                                wait=[("tmv", tmv_base[bank] + j + 1)],
                                inc=[("idm", 1)])
                        P.emit("s", lambda e, sb=sb, half=half: e.activation(
                            E2[half][:, :], sb[:, :], AF.Exp),
                            wait=[("idm", P.val("idm")), P.lag("ava", 1)],
                            inc=[(eh, 1)])
                        for jj2 in range(2):
                            j = grp * 2 + jj2
                            P.emit("t", lambda e, h=h, j=j, jj2=jj2, half=half, grp=grp: e.matmul(
                                ps_AV[:, :], v_aug[j][:, h * 65:h * 65 + 65],
                                E2[half][:, jj2 * 512:(jj2 + 1) * 512],
                                start=(grp == 0 and jj2 == 0),
                                stop=(grp == 7 and jj2 == 1)),
                                wait=[(eh, P.val(eh)), ("avn", P.val("avn"))],
                                inc=[("ava", 1)] if jj2 == 1 else ())
                    idm_done[bank] = P.val("idm")
                    P.emit("v", lambda e, sl=h % 2: e.tensor_copy(
                        avcp[sl][:, :], ps_AV[:, :]),
                        wait=[("ava", P.val("ava")), P.lag("anm", 1)], inc=[("avn", 1)])
                    P.emit("v", lambda e: e.memset(astl[:, :], 0.0),
                        inc=[("avn", 1)])
                    P.emit("s", lambda e, sl=h % 2: e.activation(
                        lnr, avcp[sl][64:65, :], AF.Ln),
                        wait=[("avn", P.val("avn")), ("bcm", P.val("bcm"))],
                        inc=[("rln", 1)])
                    P.emit("s", lambda e: e.activation(
                        rcp[:, :], lnr, AF.Exp, scale=-1.0),
                        wait=[("rln", P.val("rln"))], inc=())
                    P.emit("s", lambda e: e.activation(
                        astl[:, :], astl[:, :], AF.Square),
                        inc=[("rcv", 1)])
                    P.emit("t", lambda e: e.matmul(
                        ps_bc[:, :], ones_bf[:, 0:64], rcp[:, :], start=True, stop=True),
                        wait=[("rcv", P.val("rcv")), P.lag("rpv", 0)], inc=[("bcm", 1)])
                    P.emit("v", lambda e: e.tensor_copy(rep64[:, :], ps_bc[:, :]),
                           wait=[("bcm", P.val("bcm"))], inc=[("rpv", 1)])
                    P.emit("v", lambda e, par=par, ht=ht, cs=cs, sl=h % 2: e.tensor_mul(
                        attnT[ht][par * 64:par * 64 + 64, cs],
                        avcp[sl][0:64, :], rep64[:, :]),
                        wait=[("rpv", P.val("rpv"))],
                        inc=[("anm", 1)])

                stage(0)
                for h in range(H):
                    if h + 1 < H:
                        stage(h + 1)
                    consume(h)
            P.flush("ph3")

        with ExitStack() as c4:
            C = c4.enter_context
            wbuf2 = [C(nc.sbuf_tensor(f"A_wbuf2_{i}", [128, D], BF16)) for i in range(2)]
            xc = [C(nc.sbuf_tensor(f"A_xc{i}", [128, 512], F32)) for i in range(4)]
            outst = [C(nc.sbuf_tensor(f"A_outst{i}", [128, 512], F32)) for i in range(2)]
            ppo = [C(nc.psum_tensor(f"A_ppo{i}", [128, 512], F32)) for i in range(2)]

            for ob in range(DT):
                slot = ob % 2
                P.emit("g", lambda e, ob=ob, slot=slot: e.dma_start(
                    wbuf2[slot][:, :], wo[ob * 128:(ob + 1) * 128, :]),
                    wait=[P.lag("oev", 4)], inc=[("owl", 16)])
                for c in range(4):
                    cs = slice(c * 512, (c + 1) * 512)
                    P.emit("y", lambda e, ob=ob, cs=cs, xs=c % 4: e.dma_start(
                        xc[xs][:, :], xT[ob * 128:(ob + 1) * 128, cs]),
                        wait=[P.lag("oev", 2)], inc=[("lxc", 16)])
                    for t in range(DT):
                        P.emit("t", lambda e, slot=slot, t=t, cs=cs, c=c: e.matmul(
                            ppo[c % 2][:, :], wbuf2[slot][:, t * 128:(t + 1) * 128],
                            attnT[t][:, cs], start=(t == 0), stop=(t == DT - 1)),
                            wait=[("owl", P.val("owl")), P.lag("oev", 1)],
                            inc=[("omm", 1)] if t == DT - 1 else ())
                    P.emit("v", lambda e, ob=ob, c=c: e.scalar_tensor_tensor(
                        outst[c % 2][:, :], ppo[c % 2][:, :], bo_sb[:, ob:ob + 1],
                        xc[c % 4][:, :], ALU.add, ALU.add),
                        wait=[("omm", P.val("omm")), ("lxc", P.val("lxc")),
                              P.lag("ost", 16)],
                        inc=[("oev", 1)])
                    P.emit("y", lambda e, ob=ob, cs=cs, c=c: e.dma_start(
                        x2T[ob * 128:(ob + 1) * 128, cs], outst[c % 2][:, :]),
                        wait=[("oev", P.val("oev"))], inc=[("ost", 16)])
            if debug:
                for t in range(DT):
                    P.emit("g", lambda e, t=t: e.dma_start(
                        dbg_q[t * 128:(t + 1) * 128, :], qT[t][:, :]), inc=[("dbg", 16)])
                    P.emit("g", lambda e, t=t: e.dma_start(
                        dbg_k[t * 128:(t + 1) * 128, :], kT[t][:, :]), inc=[("dbg", 16)])
                    P.emit("g", lambda e, t=t: e.dma_start(
                        dbg_at[t * 128:(t + 1) * 128, :], attnT[t][:, :]), inc=[("dbg", 16)])
                P.emit("g", None, wait=[("dbg", P.val("dbg"))])
            P.emit("y", None, wait=[("ost", P.val("ost"))])
            P.flush("ph4")






DH = 4096
DHT = DH // 128


def emit_B(nc, P, ctx, xin, xout, debug=False):
    fw1 = nc.dram_tensor("B_fw1", [DH, D], F32, kind="ExternalInput").ap()
    fw2 = nc.dram_tensor("B_fw2", [DH, D], F32, kind="ExternalInput").ap()
    fh1 = nc.dram_tensor("B_fh1", [DH, DH], F32, kind="ExternalInput").ap()
    fh2 = nc.dram_tensor("B_fh2", [DH, DH], F32, kind="ExternalInput").ap()
    fw3 = nc.dram_tensor("B_fw3", [D, DH], F32, kind="ExternalInput").ap()
    s1v = nc.dram_tensor("B_s1v", [128, DHT], F32, kind="ExternalInput").ap()
    s2v = nc.dram_tensor("B_s2v", [128, DHT], F32, kind="ExternalInput").ap()
    hs1v = nc.dram_tensor("B_hs1v", [128, DHT], F32, kind="ExternalInput").ap()
    hs2v = nc.dram_tensor("B_hs2v", [128, DHT], F32, kind="ExternalInput").ap()
    s3v = nc.dram_tensor("B_s3v", [128, DT], F32, kind="ExternalInput").ap()
    xoT = xout

    with ExitStack() as bctx:
        E = bctx.enter_context

        s1_sb = E(nc.sbuf_tensor("B_s1_sb", [128, DHT], F32))
        s2_sb = E(nc.sbuf_tensor("B_s2_sb", [128, DHT], F32))
        hs1_sb = E(nc.sbuf_tensor("B_hs1_sb", [128, DHT], F32))
        hs2_sb = E(nc.sbuf_tensor("B_hs2_sb", [128, DHT], F32))
        s3_sb = E(nc.sbuf_tensor("B_s3_sb", [128, DT], F32))
        ones_f32 = E(nc.sbuf_tensor("B_ones_f32", [128, 1], F32))
        ones_f32r = E(nc.sbuf_tensor("B_ones_f32r", [1, 128], F32))
        eps_sb = E(nc.sbuf_tensor("B_eps_sb", [1, 1], F32))
        mv = E(nc.sbuf_tensor("B_mv", [1, S], F32))
        vv = E(nc.sbuf_tensor("B_vv", [1, S], F32))
        m_rep = E(nc.sbuf_tensor("B_m_rep", [128, S], F32))
        r_rep = E(nc.sbuf_tensor("B_r_rep", [128, S], F32))

        for nm, sb, ap_ in (("s1", s1_sb, s1v), ("s2", s2_sb, s2v),
                            ("hs1", hs1_sb, hs1v), ("hs2", hs2_sb, hs2v),
                            ("s3", s3_sb, s3v)):
            P.emit("y", lambda e, sb=sb, a=ap_: e.dma_start(sb[:, :], a), inc=[("ldm", 16)])
        LDMB = 5 * 16
        P.emit("v", lambda e: e.memset(ones_f32[:, :], 1.0), inc=[("ldv", 1)])
        P.emit("v", lambda e: e.memset(ones_f32r[:, :], 1.0), inc=[("ldv", 1)])
        P.emit("v", lambda e: e.memset(eps_sb[:, :], EPS), inc=[("ldv", 1)])

        with ExitStack() as c1:
            C = c1.enter_context
            xr = [C(nc.sbuf_tensor(f"B_xr{i}", [128, S], F32)) for i in range(2)]
            sq_sb = C(nc.sbuf_tensor("B_sq", [128, S], F32))
            with ExitStack() as c1s:
                stat_ps = c1s.enter_context(nc.psum_tensor("B_stat_ps", [1, S], F32))
                stat_ps2 = c1s.enter_context(nc.psum_tensor("B_stat_ps2", [1, S], F32))
                P.gate(["t"], ["ldv"])
                for t in range(DT):
                    P.emit("y", lambda e, t=t, sl=t % 2: e.dma_start(
                        xr[sl][:, :], xin[t * 128:(t + 1) * 128, :]),
                        wait=[P.lag("sxu", 0), P.lag("ssq", 0)], inc=[("ldx", 16)])
                    P.emit("s", lambda e, sl=t % 2: e.activation(
                        sq_sb[:, :], xr[sl][:, :], AF.Square),
                        wait=["ldx", ("squ", P.val("squ"))],
                        inc=[("ssq", 1)])
                    for c in range(4):
                        P.emit("t", lambda e, t=t, c=c, sl=t % 2: e.matmul(
                            stat_ps[:, c * 512:(c + 1) * 512], ones_f32[:, :],
                            xr[sl][:, c * 512:(c + 1) * 512],
                            start=(t == 0), stop=(t == DT - 1)),
                            wait=["ldx"],
                            inc=[("sxu", 1)] if c == 3 else ())
                    for c in range(4):
                        P.emit("t", lambda e, t=t, c=c: e.matmul(
                            stat_ps2[:, c * 512:(c + 1) * 512], ones_f32[:, :],
                            sq_sb[:, c * 512:(c + 1) * 512],
                            start=(t == 0), stop=(t == DT - 1)),
                            wait=[("ssq", P.val("ssq"))],
                            inc=[("squ", 1)] if c == 3 else ())
                P.emit("v", lambda e: e.tensor_scalar(
                    mv[:, :], stat_ps[:, :], 1.0 / D, None, ALU.mult),
                    wait=["sxu", "squ"], inc=[("lnv", 1)])
                P.emit("v", lambda e: e.tensor_scalar(
                    vv[:, :], stat_ps2[:, :], 1.0 / D, None, ALU.mult),
                    wait=["sxu", "squ"], inc=[("lnv", 1)])
                P.emit("v", lambda e: e.scalar_tensor_tensor(
                    sq_sb[0:1, 0:S], mv[0:1, :], -1.0, mv[0:1, :], ALU.mult, ALU.mult),
                    wait=["lnv"], inc=[("lnv", 1)])
                P.emit("v", lambda e: e.tensor_add(vv[:, :], vv[:, :], sq_sb[0:1, 0:S]),
                       wait=["lnv"], inc=[("lnv", 1)])
                P.emit("v", lambda e: e.memset(sq_sb[0:1, 0:8], 0.0),
                       inc=[("lnv", 1)])
                P.emit("s", lambda e: e.activation(
                    vv[:, :], vv[:, :], AF.Sqrt, bias=eps_sb[0:1, 0:1]),
                    wait=["lnv", "ldv"], inc=())
                P.emit("s", lambda e: e.activation(
                    sq_sb[0:1, 0:8], sq_sb[0:1, 0:8], AF.Square),
                    inc=[("lnv", 1)])
                P.emit("v", lambda e: e.reciprocal(vv[:, :], vv[:, :]),
                       wait=["lnv"], inc=())
                P.emit("v", lambda e: e.memset(sq_sb[0:1, 0:8], 0.0),
                       inc=[("rstd", 1)])
                P.flush("phB1a")
            with ExitStack() as c1b:
                rep_ps = c1b.enter_context(nc.psum_tensor("B_rep_ps", [128, 512], F32))
                rep_ps2 = c1b.enter_context(nc.psum_tensor("B_rep_ps2", [128, 512], F32))
                emit_rep(P, rep_ps, rep_ps2, mv, vv, m_rep, r_rep, ones_f32r, "B")
                P.flush("phB1b")

        with ExitStack() as c2:
            C = c2.enter_context
            x2c = [C(nc.sbuf_tensor(f"B_x2c{t}", [128, 512], F32)) for t in range(DT)]
            xln = [C(nc.sbuf_tensor(f"B_xln{t}", [128, 512], BF16)) for t in range(DT)]
            scr = C(nc.sbuf_tensor("B_scr", [128, 512], F32))
            a_sb = [C(nc.sbuf_tensor(f"B_a_sb{i}", [128, 512], F32)) for i in range(2)]
            hA = [C(nc.sbuf_tensor(f"B_hA{i}", [128, 512], BF16)) for i in range(DHT)]
            hB = [C(nc.sbuf_tensor(f"B_hB{i}", [128, 512], BF16)) for i in range(DHT)]
            wb1 = [C(nc.sbuf_tensor(f"B_wb1_{i}", [128, D], BF16)) for i in range(8)]
            wb4 = [C(nc.sbuf_tensor(f"B_wb4_{i}", [128, DH], BF16)) for i in range(4)]
            outst = [C(nc.sbuf_tensor(f"B_outst{i}", [128, 512], F32)) for i in range(2)]
            bstl = C(nc.sbuf_tensor("B_stl", [1, 8], F32))
            pz = [C(nc.psum_tensor(f"B_pz{i}", [128, 512], F32)) for i in range(4)]
            pg = [C(nc.psum_tensor(f"B_pg{i}", [128, 512], F32)) for i in range(2)]

            P.gate(["s", "v", "g"], ["ldm", "rstd"])

            for ch in range(4):
                cs = slice(ch * 512, (ch + 1) * 512)
                for t in range(DT):
                    P.emit("y", lambda e, t=t, cs=cs: e.dma_start(
                        x2c[t][:, :], xin[t * 128:(t + 1) * 128, cs]),
                        wait=[P.lag("bs4", 1)], inc=[("lxc", 16)])
                    P.emit("v", lambda e, t=t, cs=cs: e.tensor_sub(
                        scr[:, :], x2c[t][:, :], m_rep[:, cs]),
                        wait=[("lxc", P.val("lxc")), "mrv", "rrv",
                              ("bxl", P.val("bxl")), ("bz1", P.val("bz1")),
                              ("bz2", P.val("bz2"))],
                        inc=[("bxs", 1)])
                    P.emit("v", lambda e, t=t, cs=cs: e.tensor_mul(
                        xln[t][:, :], scr[:, :], r_rep[:, cs]),
                        wait=[("bxs", P.val("bxs"))], inc=())
                    P.emit("v", lambda e: e.memset(bstl[:, :], 0.0),
                           inc=[("bxl", 1)])
                for ob in range(DHT):
                    sl1, sl2 = (ob % 4), 4 + (ob % 4)
                    P.emit("g", lambda e, ob=ob, sl=sl1: e.dma_start(
                        wb1[sl][:, :], fw1[ob * 128:(ob + 1) * 128, :]),
                        wait=[P.lag("bz1", 3)], inc=[("bw1", 16)])
                    P.emit("g", lambda e, ob=ob, sl=sl2: e.dma_start(
                        wb1[sl][:, :], fw2[ob * 128:(ob + 1) * 128, :]),
                        wait=[P.lag("bz2", 3)], inc=[("bw1", 16)])
                    for t in range(DT):
                        P.emit("t", lambda e, ob=ob, sl=sl1, t=t: e.matmul(
                            pz[ob % 2][:, :], wb1[sl][:, t * 128:(t + 1) * 128],
                            xln[t][:, :], start=(t == 0), stop=(t == DT - 1)),
                            wait=[("bw1", P.val("bw1")), ("bxl", P.val("bxl")),
                                  P.lag("bsi", 1)],
                            inc=[("bz1", 1)] if t == DT - 1 else ())
                    for t in range(DT):
                        P.emit("t", lambda e, ob=ob, sl=sl2, t=t: e.matmul(
                            pz[2 + ob % 2][:, :], wb1[sl][:, t * 128:(t + 1) * 128],
                            xln[t][:, :], start=(t == 0), stop=(t == DT - 1)),
                            wait=[("bw1", P.val("bw1")), P.lag("bhv", 1)],
                            inc=[("bz2", 1)] if t == DT - 1 else ())
                    P.emit("s", lambda e, ob=ob, sl=ob % 2: e.activation(
                        a_sb[sl][:, :], pz[ob % 2][:, :], AF.Silu,
                        scale=s1_sb[:, ob:ob + 1]),
                        wait=[("bz1", P.val("bz1")), P.lag("bhv", 1)],
                        inc=())
                    P.emit("s", lambda e: e.activation(bstl[:, :], bstl[:, :], AF.Square),
                           inc=[("bsi", 1)])
                    P.emit("v", lambda e, ob=ob, sl=ob % 2: e.scalar_tensor_tensor(
                        hA[ob][:, :], pz[2 + ob % 2][:, :], s2_sb[:, ob:ob + 1],
                        a_sb[sl][:, :], ALU.mult, ALU.mult),
                        wait=[("bz2", P.val("bz2")), ("bsi", P.val("bsi")),
                              ("bgu", P.val("bgu"))],
                        inc=())
                    P.emit("v", lambda e: e.memset(bstl[:, :], 0.0),
                           inc=[("bhv", 1)])
                for ob in range(DHT):
                    P.emit("g", lambda e, ob=ob, sl=ob % 4: e.dma_start(
                        wb4[sl][:, :], fh1[ob * 128:(ob + 1) * 128, :]),
                        wait=[P.lag("bgu", 3)], inc=[("bw4", 16)])
                    for t in range(DHT):
                        P.emit("t", lambda e, ob=ob, sl=ob % 4, t=t: e.matmul(
                            pg[ob % 2][:, :], wb4[sl][:, t * 128:(t + 1) * 128],
                            hA[t][:, :], start=(t == 0), stop=(t == DHT - 1)),
                            wait=[("bw4", P.val("bw4")), ("bhv", P.val("bhv")),
                                  P.lag("bg2", 1)],
                            inc=[("bgu", 1)] if t == DHT - 1 else ())
                    P.emit("s", lambda e, ob=ob: e.activation(
                        hB[ob][:, :], pg[ob % 2][:, :], AF.Gelu,
                        scale=hs1_sb[:, ob:ob + 1]),
                        wait=[("bgu", P.val("bgu"))], inc=())
                    P.emit("s", lambda e: e.activation(bstl[:, :], bstl[:, :], AF.Square),
                           inc=[("bg2", 1)])
                for ob in range(DHT):
                    P.emit("g", lambda e, ob=ob, sl=ob % 4: e.dma_start(
                        wb4[sl][:, :], fh2[ob * 128:(ob + 1) * 128, :]),
                        wait=[P.lag("bgu", 3)], inc=[("bw4", 16)])
                    for t in range(DHT):
                        P.emit("t", lambda e, ob=ob, sl=ob % 4, t=t: e.matmul(
                            pg[ob % 2][:, :], wb4[sl][:, t * 128:(t + 1) * 128],
                            hB[t][:, :], start=(t == 0), stop=(t == DHT - 1)),
                            wait=[("bw4", P.val("bw4")), ("bg2", P.val("bg2")),
                                  P.lag("bg3", 1)],
                            inc=[("bgu", 1)] if t == DHT - 1 else ())
                    P.emit("s", lambda e, ob=ob: e.activation(
                        hA[ob][:, :], pg[ob % 2][:, :], AF.Gelu,
                        scale=hs2_sb[:, ob:ob + 1]),
                        wait=[("bgu", P.val("bgu"))], inc=())
                    P.emit("s", lambda e: e.activation(bstl[:, :], bstl[:, :], AF.Square),
                           inc=[("bg3", 1)])
                for do in range(DT):
                    P.emit("g", lambda e, do=do, sl=do % 4: e.dma_start(
                        wb4[sl][:, :], fw3[do * 128:(do + 1) * 128, :]),
                        wait=[P.lag("bgu", 3)], inc=[("bw4", 16)])
                    for t in range(DHT):
                        P.emit("t", lambda e, do=do, sl=do % 4, t=t: e.matmul(
                            pg[do % 2][:, :], wb4[sl][:, t * 128:(t + 1) * 128],
                            hA[t][:, :], start=(t == 0), stop=(t == DHT - 1)),
                            wait=[("bw4", P.val("bw4")), ("bg3", P.val("bg3")),
                                  P.lag("bs4", 1)],
                            inc=[("bgu", 1)] if t == DHT - 1 else ())
                    P.emit("v", lambda e, do=do: e.scalar_tensor_tensor(
                        outst[do % 2][:, :], pg[do % 2][:, :], s3_sb[:, do:do + 1],
                        x2c[do][:, :], ALU.mult, ALU.add),
                        wait=[("bgu", P.val("bgu")), ("lxc", P.val("lxc")),
                              P.lag("bst", 16)],
                        inc=[("bs4", 1)])
                    P.emit("y", lambda e, do=do, cs=cs, sl=do % 2: e.dma_start(
                        xoT[do * 128:(do + 1) * 128, cs], outst[sl][:, :]),
                        wait=[("bs4", P.val("bs4"))], inc=[("bst", 16)])
            P.emit("y", None, wait=[("bst", P.val("bst"))])
            P.flush("phB2")


def prep_l0_ffn(ins):
    w = {}
    w["fw1"] = host_blocks(_c(ins["L0_ffn_w1_w"].T))
    w["fw2"] = host_blocks(_c(ins["L0_ffn_w2_w"].T))
    w["fh1"] = host_blocks(_c(ins["L0_ffn_h1_w"].T))
    w["fh2"] = host_blocks(_c(ins["L0_ffn_h2_w"].T))
    w["fw3"] = host_blocks(_c(ins["L0_ffn_w3_w"].T))
    w["s1v"] = host_vec(ins["L0_ffn_w1_s"])
    w["s2v"] = host_vec(ins["L0_ffn_w2_s"])
    w["hs1v"] = host_vec(ins["L0_ffn_h1_s"])
    w["hs2v"] = host_vec(ins["L0_ffn_h2_s"])
    w["s3v"] = host_vec(ins["L0_ffn_w3_s"])
    return w





def emit_C(nc, P, ctx, xin, xout, debug=False):
    wq = nc.dram_tensor("C_wq", [D, D], F32, kind="ExternalInput").ap()
    wk = nc.dram_tensor("C_wk", [D, D], F32, kind="ExternalInput").ap()
    wv = nc.dram_tensor("C_wv", [D, D], F32, kind="ExternalInput").ap()
    wo = nc.dram_tensor("C_wo", [D, D], F32, kind="ExternalInput").ap()
    bqv = nc.dram_tensor("C_bqv", [128, DT], F32, kind="ExternalInput").ap()
    bkv = nc.dram_tensor("C_bkv", [128, DT], F32, kind="ExternalInput").ap()
    bvv = nc.dram_tensor("C_bvv", [128, DT], F32, kind="ExternalInput").ap()
    bov = nc.dram_tensor("C_bov", [128, DT], F32, kind="ExternalInput").ap()
    xoT = xout

    id128 = nc.inline_tensor(np.eye(128, dtype=BF), "C_id128")
    band = np.zeros((128, 1024), dtype=BF)
    for rel in range(-1, 3):
        for p in range(128):
            for i in range(256):
                if abs(128 * rel + p - i) <= 64:
                    band[p, (rel + 1) * 256 + i] = 1.0
    bandc = nc.inline_tensor(band, "C_bandc")

    with ExitStack() as bctx:
        E = bctx.enter_context

        bq_sb = E(nc.sbuf_tensor("C_bq_sb", [128, DT], F32))
        bk_sb = E(nc.sbuf_tensor("C_bk_sb", [128, DT], F32))
        bv_sb = E(nc.sbuf_tensor("C_bv_sb", [128, DT], F32))
        bo_sb = E(nc.sbuf_tensor("C_bo_sb", [128, DT], F32))
        id_sb = E(nc.sbuf_tensor("C_id_sb", [128, 128], BF16))
        band_sb = E(nc.sbuf_tensor("C_band_sb", [128, 1024], BF16))
        ones_f32 = E(nc.sbuf_tensor("C_ones_f32", [128, 1], F32))
        ones_f32r = E(nc.sbuf_tensor("C_ones_f32r", [1, 128], F32))
        ones_bf = E(nc.sbuf_tensor("C_ones_bf", [1, 128], BF16))
        eps_sb = E(nc.sbuf_tensor("C_eps_sb", [1, 1], F32))

        for sb, ap_ in ((bq_sb, bqv), (bk_sb, bkv), (bv_sb, bvv), (bo_sb, bov)):
            P.emit("y", lambda e, sb=sb, a=ap_: e.dma_start(sb[:, :], a), inc=[("ldm", 16)])
        P.emit("g", lambda e: e.dma_start(id_sb[:, :], id128.ap()), inc=[("ldm", 16)])
        P.emit("g", lambda e: e.dma_start(band_sb[:, :], bandc.ap()), inc=[("ldm", 16)])
        LDMC = 6 * 16
        P.emit("v", lambda e: e.memset(ones_f32[:, :], 1.0), inc=[("ldv", 1)])
        P.emit("v", lambda e: e.memset(ones_f32r[:, :], 1.0), inc=[("ldv", 1)])
        P.emit("v", lambda e: e.memset(ones_bf[:, :], 1.0), inc=[("ldv", 1)])
        P.emit("v", lambda e: e.memset(eps_sb[:, :], EPS), inc=[("ldv", 1)])
        P.emit("v", lambda e: e.tensor_scalar(bk_sb[:, :], bk_sb[:, :], 0.125, None, ALU.mult),
               wait=["ldm"], inc=[("bk8", 1)])

        qT = [E(nc.sbuf_tensor(f"C_qT{t}", [128, S], BF16)) for t in range(DT)]
        kT = [E(nc.sbuf_tensor(f"C_kT{t}", [128, S], BF16)) for t in range(DT)]
        v_aug = [E(nc.sbuf_tensor(f"C_vaug{j}", [128, H * 65], BF16)) for j in range(ST)]

        cx = bctx.enter_context(ExitStack())
        x_ln = [cx.enter_context(nc.sbuf_tensor(f"C_x_ln{t}", [128, S], BF16))
                for t in range(DT)]

        with ExitStack() as c1:
            C = c1.enter_context
            xr = [C(nc.sbuf_tensor(f"C_xr{i}", [128, S], F32)) for i in range(2)]
            sq_sb = C(nc.sbuf_tensor("C_sq", [128, S], F32))
            mv = C(nc.sbuf_tensor("C_mv", [1, S], F32))
            vv = C(nc.sbuf_tensor("C_vv", [1, S], F32))
            m_rep = C(nc.sbuf_tensor("C_m_rep", [128, S], F32))
            r_rep = C(nc.sbuf_tensor("C_r_rep", [128, S], F32))
            with ExitStack() as c1s:
                stat_ps = c1s.enter_context(nc.psum_tensor("C_stat_ps", [1, S], F32))
                stat_ps2 = c1s.enter_context(nc.psum_tensor("C_stat_ps2", [1, S], F32))
                P.gate(["t"], ["ldv"])
                for t in range(DT):
                    P.emit("y", lambda e, t=t, sl=t % 2: e.dma_start(
                        xr[sl][:, :], xin[t * 128:(t + 1) * 128, :]),
                        wait=[P.lag("sxu", 0), P.lag("ssq", 0)], inc=[("ldx", 16)])
                    P.emit("s", lambda e, sl=t % 2: e.activation(
                        sq_sb[:, :], xr[sl][:, :], AF.Square),
                        wait=["ldx", ("squ", P.val("squ"))],
                        inc=[("ssq", 1)])
                    for c in range(4):
                        P.emit("t", lambda e, t=t, c=c, sl=t % 2: e.matmul(
                            stat_ps[:, c * 512:(c + 1) * 512], ones_f32[:, :],
                            xr[sl][:, c * 512:(c + 1) * 512],
                            start=(t == 0), stop=(t == DT - 1)),
                            wait=["ldx"],
                            inc=[("sxu", 1)] if c == 3 else ())
                    for c in range(4):
                        P.emit("t", lambda e, t=t, c=c: e.matmul(
                            stat_ps2[:, c * 512:(c + 1) * 512], ones_f32[:, :],
                            sq_sb[:, c * 512:(c + 1) * 512],
                            start=(t == 0), stop=(t == DT - 1)),
                            wait=[("ssq", P.val("ssq"))],
                            inc=[("squ", 1)] if c == 3 else ())
                P.emit("v", lambda e: e.tensor_scalar(
                    mv[:, :], stat_ps[:, :], 1.0 / D, None, ALU.mult),
                    wait=["sxu", "squ"], inc=[("lnv", 1)])
                P.emit("v", lambda e: e.tensor_scalar(
                    vv[:, :], stat_ps2[:, :], 1.0 / D, None, ALU.mult),
                    wait=["sxu", "squ"], inc=[("lnv", 1)])
                P.emit("v", lambda e: e.scalar_tensor_tensor(
                    sq_sb[0:1, 0:S], mv[0:1, :], -1.0, mv[0:1, :], ALU.mult, ALU.mult),
                    wait=["lnv"], inc=[("lnv", 1)])
                P.emit("v", lambda e: e.tensor_add(vv[:, :], vv[:, :], sq_sb[0:1, 0:S]),
                       wait=["lnv"], inc=[("lnv", 1)])
                P.emit("v", lambda e: e.memset(sq_sb[0:1, 0:8], 0.0),
                       inc=[("lnv", 1)])
                P.emit("s", lambda e: e.activation(
                    vv[:, :], vv[:, :], AF.Sqrt, bias=eps_sb[0:1, 0:1]),
                    wait=["lnv", "ldv"], inc=())
                P.emit("s", lambda e: e.activation(
                    sq_sb[0:1, 0:8], sq_sb[0:1, 0:8], AF.Square),
                    inc=[("lnv", 1)])
                P.emit("v", lambda e: e.reciprocal(vv[:, :], vv[:, :]),
                       wait=["lnv"], inc=())
                P.emit("v", lambda e: e.memset(sq_sb[0:1, 0:8], 0.0),
                       inc=[("rstd", 1)])
                P.flush("phC1a")
            with ExitStack() as c1b:
                rep_ps = c1b.enter_context(nc.psum_tensor("C_rep_ps", [128, 512], F32))
                rep_ps2 = c1b.enter_context(nc.psum_tensor("C_rep_ps2", [128, 512], F32))
                emit_rep(P, rep_ps, rep_ps2, mv, vv, m_rep, r_rep, ones_f32r, "C")
                for t in range(DT):
                    P.emit("y", lambda e, t=t, sl=t % 2: e.dma_start(
                        xr[sl][:, :], xin[t * 128:(t + 1) * 128, :]),
                        wait=[P.lag("xsub", 1)], inc=[("ldx", 16)])
                    P.emit("v", lambda e, t=t, sl=t % 2: e.tensor_sub(
                        sq_sb[:, :], xr[sl][:, :], m_rep[:, :]),
                        wait=["mrv", "rrv", "ldx",
                              ("xlnd", P.val("xlnd"))],
                        inc=[("xsub", 1)])
                    P.emit("v", lambda e, t=t: e.tensor_mul(
                        x_ln[t][:, :], sq_sb[:, :], r_rep[:, :]),
                        wait=[("xsub", P.val("xsub"))], inc=[("xlnd", 1)])
                P.flush("phC1b")

        with ExitStack() as c2:
            C = c2.enter_context
            wbuf = [C(nc.sbuf_tensor(f"C_wbuf{i}", [128, D], BF16)) for i in range(2)]
            vT_t = C(nc.sbuf_tensor("C_vT_t", [128, S], BF16))
            pp = [C(nc.psum_tensor(f"C_pp{i}", [128, 512], F32)) for i in range(4)]
            tp = C(nc.psum_tensor("C_tp", [128, 128], BF16))
            P.gate(["s", "v", "g"], ["ldm", "bk8"])

            def proj_dmajor(wdram, bias_sb, scale, out_tiles, ppbase):
                for ob in range(DT):
                    slot = ob % 2
                    P.emit("g", lambda e, ob=ob, slot=slot, w=wdram: e.dma_start(
                        wbuf[slot][:, :], w[ob * 128:(ob + 1) * 128, :]),
                        wait=[P.lag("pev", 4)], inc=[("wld", 16)])
                    for c in range(4):
                        cs = slice(c * 512, (c + 1) * 512)
                        ps = pp[ppbase + (c % 2)]
                        for t in range(DT):
                            P.emit("t", lambda e, slot=slot, t=t, cs=cs, ps=ps: e.matmul(
                                ps[:, :], wbuf[slot][:, t * 128:(t + 1) * 128],
                                x_ln[t][:, cs], start=(t == 0), stop=(t == DT - 1)),
                                wait=[("wld", P.val("wld")), P.lag("pev", 1),
                                      ("xlnd", DT)],
                                inc=[("pmm", 1)] if t == DT - 1 else ())
                        P.emit("s", lambda e, ob=ob, cs=cs, ps=ps, b=bias_sb, sc=scale,
                               ot=out_tiles: e.activation(
                            ot[ob][:, cs], ps[:, :], AF.Identity,
                            bias=b[:, ob:ob + 1], scale=sc),
                            wait=[("pmm", P.val("pmm"))], inc=[("pev", 1)])

            proj_dmajor(wq, bq_sb, 1.0, qT, 0)
            proj_dmajor(wk, bk_sb, 0.125, kT, 0)

            for j in range(ST):
                P.emit("v", lambda e, j=j: e.memset(v_aug[j][:, :], 1.0),
                       inc=[("vamem", 1)])
            for ob in range(DT):
                slot = ob % 2
                P.emit("g", lambda e, ob=ob, slot=slot: e.dma_start(
                    wbuf[slot][:, :], wv[ob * 128:(ob + 1) * 128, :]),
                    wait=[P.lag("pev", 4)], inc=[("wld", 16)])
                for c in range(4):
                    cs = slice(c * 512, (c + 1) * 512)
                    ps = pp[2 + (c % 2)]
                    for t in range(DT):
                        P.emit("t", lambda e, slot=slot, t=t, cs=cs, ps=ps: e.matmul(
                            ps[:, :], wbuf[slot][:, t * 128:(t + 1) * 128],
                            x_ln[t][:, cs], start=(t == 0), stop=(t == DT - 1)),
                            wait=[("wld", P.val("wld")), P.lag("pev", 1)],
                            inc=[("vmm", 1)] if t == DT - 1 else ())
                    P.emit("s", lambda e, ob=ob, cs=cs, ps=ps: e.activation(
                        vT_t[:, cs], ps[:, :], AF.Identity, bias=bv_sb[:, ob:ob + 1]),
                        wait=[("vmm", P.val("vmm")), ("vtr", P.val("vtr"))],
                        inc=[("pev", 1)])
                for j in range(ST):
                    P.emit("t", lambda e, j=j: e.transpose(
                        tp[:, :], vT_t[:, j * 128:(j + 1) * 128], id_sb[:, :]),
                        wait=[("pev", P.val("pev")), ("vtrc", P.val("vtrc"))],
                        inc=[("vtr", 1)])
                    P.emit("v", lambda e, ob=ob, j=j: e.tensor_copy(
                        v_aug[j][:, (2 * ob) * 65:(2 * ob) * 65 + 64], tp[:, 0:64]),
                        wait=[("vtr", P.val("vtr")), "vamem"], inc=[("vtrc", 1)])
                    P.emit("v", lambda e, ob=ob, j=j: e.tensor_copy(
                        v_aug[j][:, (2 * ob + 1) * 65:(2 * ob + 1) * 65 + 64],
                        tp[:, 64:128]),
                        wait=[("vtr", P.val("vtr")), "vamem"], inc=[("vtrc", 1)])
            P.flush("phC2")

        cx.close()
        attnT = [E(nc.sbuf_tensor(f"C_attnT{t}", [128, S], BF16)) for t in range(DT)]

        with ExitStack() as c3:
            C = c3.enter_context
            Em = [C(nc.sbuf_tensor(f"C_Em{i}", [128, 1024], BF16)) for i in range(2)]
            qz = [C(nc.sbuf_tensor(f"C_qz{i}", [128, 256], BF16)) for i in range(2)]
            rcp = C(nc.sbuf_tensor("C_rcp", [1, 256], BF16))
            rep64 = C(nc.sbuf_tensor("C_rep64", [64, 256], BF16))
            avcp = [C(nc.sbuf_tensor(f"C_avcp{i}", [65, 256], F32)) for i in range(2)]
            cstl = C(nc.sbuf_tensor("C_stl", [1, 8], F32))
            ps_S = [C(nc.psum_tensor(f"C_ps_S{i}", [128, 1024], F32)) for i in range(2)]
            ps_AV = [C(nc.psum_tensor(f"C_ps_AV{i}", [65, 256], F32)) for i in range(2)]
            ps_bc = C(nc.psum_tensor("C_ps_bc", [64, 256], F32))

            P.emit("v", lambda e: e.memset(cstl[:, :], 0.0), inc=[("qzi", 1)])
            P.emit("v", lambda e: e.memset(qz[0][:, :], 0.0), inc=[("qzi", 1)])
            P.emit("v", lambda e: e.memset(qz[1][:, :], 0.0), inc=[("qzi", 1)])

            NIC = 8
            for ic in range(NIC):
                cs = slice(ic * 256, (ic + 1) * 256)
                jlist = [jj for jj in range(2 * ic - 1, 2 * ic + 3) if 0 <= jj < ST]
                rel0 = jlist[0] - 2 * ic
                nj = len(jlist)
                for h in range(H):
                    par = h % 2
                    ht = h // 2
                    half = (ic * H + h) % 2
                    sb = ps_S[half]
                    eh = "exh" + str(half)
                    P.emit("v", lambda e, par=par, ht=ht, cs=cs: e.tensor_copy(
                        qz[par][par * 64:par * 64 + 64, :],
                        qT[ht][par * 64:par * 64 + 64, cs]),
                        wait=["qzi", ("qzu" + str(par), P.val("qzu" + str(par)))],
                        inc=())
                    P.emit("v", lambda e: e.memset(cstl[:, :], 0.0),
                           inc=[("qzc" + str(par), 1)])
                    for idx, jj in enumerate(jlist):
                        P.emit("t", lambda e, ht=ht, jj=jj, idx=idx, sb=sb, par=par,
                               last=(idx == nj - 1): e.matmul(
                            sb[:, idx * 256:(idx + 1) * 256],
                            kT[ht][:, jj * 128:(jj + 1) * 128],
                            qz[par][:, :], start=True, stop=True),
                            wait=[("qzc" + str(par), P.val("qzc" + str(par))),
                                  (eh, P.val(eh))],
                            inc=[("qzu" + str(par), 1)] if idx == nj - 1 else ())
                    P.emit("s", lambda e, sb=sb, nj=nj, half=half: e.activation(
                        Em[half][:, 0:nj * 256], sb[:, 0:nj * 256], AF.Exp),
                        wait=[("qzu" + str(par), P.val("qzu" + str(par))),
                              P.lag("emv", 1), P.lag("ava", 1)],
                        inc=[(eh, 1)])
                    P.emit("v", lambda e, nj=nj, rel0=rel0, half=half: e.tensor_mul(
                        Em[half][:, 0:nj * 256], Em[half][:, 0:nj * 256],
                        band_sb[:, (rel0 + 1) * 256:(rel0 + 1 + nj) * 256]),
                        wait=[(eh, P.val(eh))], inc=())
                    P.emit("v", lambda e: e.memset(cstl[:, :], 0.0),
                           inc=[("emv", 1)])
                    for idx, jj in enumerate(jlist):
                        P.emit("t", lambda e, h=h, jj=jj, idx=idx, half=half,
                               pav=ps_AV[h % 2],
                               st=(idx == 0), sp=(idx == nj - 1): e.matmul(
                            pav[:, :], v_aug[jj][:, h * 65:h * 65 + 65],
                            Em[half][:, idx * 256:(idx + 1) * 256],
                            start=st, stop=sp),
                            wait=[("emv", P.val("emv")), P.lag("avn", 1)],
                            inc=[("ava", 1)] if idx == nj - 1 else ())
                    P.emit("v", lambda e, sl=h % 2: e.tensor_copy(
                        avcp[sl][:, :], ps_AV[sl][:, :]),
                        wait=[("ava", P.val("ava")), P.lag("anm", 1)], inc=[("avn", 1)])
                    P.emit("v", lambda e: e.memset(cstl[:, :], 0.0),
                        inc=[("avn", 1)])
                    P.emit("v", lambda e, sl=h % 2: e.reciprocal(
                        rcp[:, :], avcp[sl][64:65, :]),
                        wait=[("avn", P.val("avn")), ("bcm", P.val("bcm"))],
                        inc=())
                    P.emit("v", lambda e: e.memset(cstl[:, :], 0.0),
                        inc=[("rcv", 1)])
                    P.emit("t", lambda e: e.matmul(
                        ps_bc[:, :], ones_bf[:, 0:64], rcp[:, :], start=True, stop=True),
                        wait=[("rcv", P.val("rcv")), P.lag("rpv", 0)], inc=[("bcm", 1)])
                    P.emit("v", lambda e: e.tensor_copy(rep64[:, :], ps_bc[:, :]),
                           wait=[("bcm", P.val("bcm"))], inc=[("rpv", 1)])
                    P.emit("v", lambda e, par=par, ht=ht, cs=cs, sl=h % 2: e.tensor_mul(
                        attnT[ht][par * 64:par * 64 + 64, cs],
                        avcp[sl][0:64, :], rep64[:, :]),
                        wait=[("rpv", P.val("rpv"))], inc=[("anm", 1)])
            P.flush("phC3")

        with ExitStack() as c4:
            C = c4.enter_context
            wbuf2 = [C(nc.sbuf_tensor(f"C_wbuf2_{i}", [128, D], BF16)) for i in range(2)]
            xc = [C(nc.sbuf_tensor(f"C_xc{i}", [128, 512], F32)) for i in range(4)]
            outst = [C(nc.sbuf_tensor(f"C_outst{i}", [128, 512], F32)) for i in range(2)]
            ppo = [C(nc.psum_tensor(f"C_ppo{i}", [128, 512], F32)) for i in range(2)]
            for ob in range(DT):
                slot = ob % 2
                P.emit("g", lambda e, ob=ob, slot=slot: e.dma_start(
                    wbuf2[slot][:, :], wo[ob * 128:(ob + 1) * 128, :]),
                    wait=[P.lag("oev", 4)], inc=[("owl", 16)])
                for c in range(4):
                    cs = slice(c * 512, (c + 1) * 512)
                    P.emit("y", lambda e, ob=ob, cs=cs, xs=c % 4: e.dma_start(
                        xc[xs][:, :], xin[ob * 128:(ob + 1) * 128, cs]),
                        wait=[P.lag("oev", 2)], inc=[("lxc", 16)])
                    for t in range(DT):
                        P.emit("t", lambda e, slot=slot, t=t, cs=cs, c=c: e.matmul(
                            ppo[c % 2][:, :], wbuf2[slot][:, t * 128:(t + 1) * 128],
                            attnT[t][:, cs], start=(t == 0), stop=(t == DT - 1)),
                            wait=[("owl", P.val("owl")), P.lag("oev", 1)],
                            inc=[("omm", 1)] if t == DT - 1 else ())
                    P.emit("v", lambda e, ob=ob, c=c: e.scalar_tensor_tensor(
                        outst[c % 2][:, :], ppo[c % 2][:, :], bo_sb[:, ob:ob + 1],
                        xc[c % 4][:, :], ALU.add, ALU.add),
                        wait=[("omm", P.val("omm")), ("lxc", P.val("lxc")),
                              P.lag("ost", 16)],
                        inc=[("oev", 1)])
                    P.emit("y", lambda e, ob=ob, cs=cs, c=c: e.dma_start(
                        xoT[ob * 128:(ob + 1) * 128, cs], outst[c % 2][:, :]),
                        wait=[("oev", P.val("oev"))], inc=[("ost", 16)])
            P.emit("y", None, wait=[("ost", P.val("ost"))])
            P.flush("phC4")


def prep_l1_attn(ins):
    w = {}
    w["wq"] = host_blocks(_c(ins["L1_attn_Wq"].T))
    w["wk"] = host_blocks(_c(ins["L1_attn_Wk"].T))
    w["wv"] = host_blocks(_c(ins["L1_attn_Wv"].T))
    w["wo"] = host_blocks(_c(ins["L1_attn_Wo"].T))
    w["bqv"] = host_vec(ins["L1_attn_bq"])
    w["bkv"] = host_vec(ins["L1_attn_bk"])
    w["bvv"] = host_vec(ins["L1_attn_bv"])
    w["bov"] = host_vec(ins["L1_attn_bo"])
    return w





NE = 8
DH2 = 2048
DH2T = DH2 // 128


def emit_D(nc, P, ctx, xin, xout, debug=False):
    wr = nc.dram_tensor("D_wr", [D, NE], F32, kind="ExternalInput").ap()
    brv = nc.dram_tensor("D_brv", [1, NE], F32, kind="ExternalInput").ap()
    w1e = nc.dram_tensor("D_w1e", [NE * 128, DH2T * D], F32, kind="ExternalInput").ap()
    w2e = nc.dram_tensor("D_w2e", [NE * 128, DT * DH2], F32, kind="ExternalInput").ap()
    b1v = nc.dram_tensor("D_b1v", [128, NE * DH2T], F32, kind="ExternalInput").ap()
    b2v = nc.dram_tensor("D_b2v", [NE, D], F32, kind="ExternalInput").ap()
    xoT = xout
    if debug:
        dbg_mk = nc.dram_tensor("D_dbg_mk", [NE, 512], F32, kind="ExternalOutput").ap()
        dbg_mk4 = nc.dram_tensor("D_dbg_mk4", [4 * NE, 512], F32, kind="ExternalOutput").ap()
        dbg_mk5 = nc.dram_tensor("D_dbg_mk5", [4 * NE, 512], F32, kind="ExternalOutput").ap()
        dbg_y0 = nc.dram_tensor("D_dbg_y0", [D, 512], F32, kind="ExternalOutput").ap()
        dbg_h = nc.dram_tensor("D_dbg_h", [DH2, 512], F32, kind="ExternalOutput").ap()
        dbg_ya = nc.dram_tensor("D_dbg_ya", [D, 512], F32, kind="ExternalOutput").ap()
        dbg_xl = nc.dram_tensor("D_dbg_xl", [D, 512], F32, kind="ExternalOutput").ap()
        dbg_lg = nc.dram_tensor("D_dbg_lg", [128, NE], F32, kind="ExternalOutput").ap()
        dbg_mx = nc.dram_tensor("D_dbg_mx", [128, 1], F32, kind="ExternalOutput").ap()
        dbg_ms = nc.dram_tensor("D_dbg_ms", [128, NE], F32, kind="ExternalOutput").ap()
        dbg_mb = nc.dram_tensor("D_dbg_mb", [128, NE], F32, kind="ExternalOutput").ap()
        dbg_r0 = nc.dram_tensor("D_dbg_r0", [128, 4 * NE + 4], F32, kind="ExternalOutput").ap()
        dbg_st = nc.dram_tensor("D_dbg_st", [4, S], F32, kind="ExternalOutput").ap()
        dbg_x0 = nc.dram_tensor("D_dbg_x0", [D, 512], F32, kind="ExternalOutput").ap()
        dbg_mr = nc.dram_tensor("D_dbg_mr", [128, 512], F32, kind="ExternalOutput").ap()

    id128 = nc.inline_tensor(np.eye(128, dtype=BF), "D_id128")
    e8 = np.zeros((NE, NE * 128), dtype=BF)
    for i in range(NE):
        e8[i, i * 128:(i + 1) * 128] = 1.0
    e8sel = nc.inline_tensor(e8, "D_e8sel")

    with ExitStack() as bctx:
        E = bctx.enter_context

        wr_sb = E(nc.sbuf_tensor("D_wr_sb", [128, DT * NE], F32))
        br_sb = E(nc.sbuf_tensor("D_br_sb", [1, NE], F32))
        brb = E(nc.sbuf_tensor("D_brb", [128, NE], F32))
        b1_sb = E(nc.sbuf_tensor("D_b1_sb", [128, NE * DH2T], F32))
        b2_sb = E(nc.sbuf_tensor("D_b2_sb", [NE, D], BF16))
        id_sb = E(nc.sbuf_tensor("D_id_sb", [128, 128], BF16))
        e8_sb = E(nc.sbuf_tensor("D_e8_sb", [NE, NE * 128], BF16))
        ones_f32 = E(nc.sbuf_tensor("D_ones_f32", [128, 1], F32))
        ones_f32r = E(nc.sbuf_tensor("D_ones_f32r", [1, 128], F32))
        eps_sb = E(nc.sbuf_tensor("D_eps_sb", [1, 1], F32))
        m_rep = E(nc.sbuf_tensor("D_m_rep", [128, S], F32))
        r_rep = E(nc.sbuf_tensor("D_r_rep", [128, S], F32))

        for t in range(DT):
            P.emit("y", lambda e, t=t: e.dma_start(
                wr_sb[:, t * NE:(t + 1) * NE], wr[t * 128:(t + 1) * 128, :]),
                inc=[("ldm", 16)])
        P.emit("y", lambda e: e.dma_start(br_sb[:, :], brv), inc=[("ldm", 16)])
        P.emit("y", lambda e: e.dma_start(b1_sb[:, :], b1v), inc=[("ldm", 16)])
        P.emit("g", lambda e: e.dma_start(b2_sb[:, :], b2v), inc=[("ldm", 16)])
        P.emit("g", lambda e: e.dma_start(id_sb[:, :], id128.ap()), inc=[("ldm", 16)])
        P.emit("g", lambda e: e.dma_start(e8_sb[:, :], e8sel.ap()), inc=[("ldm", 16)])
        LDMD = 13 * 16
        P.emit("v", lambda e: e.memset(ones_f32[:, :], 1.0), inc=[("ldv", 1)])
        P.emit("v", lambda e: e.memset(ones_f32r[:, :], 1.0), inc=[("ldv", 1)])
        P.emit("v", lambda e: e.memset(eps_sb[:, :], EPS), inc=[("ldv", 1)])

        with ExitStack() as c1:
            C = c1.enter_context
            xr = [C(nc.sbuf_tensor(f"D_xr{i}", [128, S], F32)) for i in range(2)]
            sq_sb = C(nc.sbuf_tensor("D_sq", [128, S], F32))
            mv = C(nc.sbuf_tensor("D_mv", [1, S], F32))
            vv = C(nc.sbuf_tensor("D_vv", [1, S], F32))
            with ExitStack() as c1s:
                stat_ps = c1s.enter_context(nc.psum_tensor("D_stat_ps", [1, S], F32))
                stat_ps2 = c1s.enter_context(nc.psum_tensor("D_stat_ps2", [1, S], F32))
                P.gate(["t"], ["ldv"])
                for t in range(DT):
                    P.emit("y", lambda e, t=t, sl=t % 2: e.dma_start(
                        xr[sl][:, :], xin[t * 128:(t + 1) * 128, :]),
                        wait=[P.lag("sxu", 0), P.lag("ssq", 0)], inc=[("ldx", 16)])
                    P.emit("s", lambda e, sl=t % 2: e.activation(
                        sq_sb[:, :], xr[sl][:, :], AF.Square),
                        wait=["ldx", ("squ", P.val("squ"))],
                        inc=[("ssq", 1)])
                    for c in range(4):
                        P.emit("t", lambda e, t=t, c=c, sl=t % 2: e.matmul(
                            stat_ps[:, c * 512:(c + 1) * 512], ones_f32[:, :],
                            xr[sl][:, c * 512:(c + 1) * 512],
                            start=(t == 0), stop=(t == DT - 1)),
                            wait=["ldx"],
                            inc=[("sxu", 1)] if c == 3 else ())
                    for c in range(4):
                        P.emit("t", lambda e, t=t, c=c: e.matmul(
                            stat_ps2[:, c * 512:(c + 1) * 512], ones_f32[:, :],
                            sq_sb[:, c * 512:(c + 1) * 512],
                            start=(t == 0), stop=(t == DT - 1)),
                            wait=[("ssq", P.val("ssq"))],
                            inc=[("squ", 1)] if c == 3 else ())
                P.emit("v", lambda e: e.tensor_scalar(
                    mv[:, :], stat_ps[:, :], 1.0 / D, None, ALU.mult),
                    wait=["sxu", "squ"], inc=[("lnv", 1)])
                P.emit("v", lambda e: e.tensor_scalar(
                    vv[:, :], stat_ps2[:, :], 1.0 / D, None, ALU.mult),
                    wait=["sxu", "squ"], inc=[("lnv", 1)])
                P.emit("v", lambda e: e.scalar_tensor_tensor(
                    sq_sb[0:1, 0:S], mv[0:1, :], -1.0, mv[0:1, :], ALU.mult, ALU.mult),
                    wait=["lnv"], inc=[("lnv", 1)])
                P.emit("v", lambda e: e.tensor_add(vv[:, :], vv[:, :], sq_sb[0:1, 0:S]),
                       wait=["lnv"], inc=[("lnv", 1)])
                P.emit("v", lambda e: e.memset(sq_sb[0:1, 0:8], 0.0),
                       inc=[("lnv", 1)])
                P.emit("s", lambda e: e.activation(
                    vv[:, :], vv[:, :], AF.Sqrt, bias=eps_sb[0:1, 0:1]),
                    wait=["lnv", "ldv"], inc=())
                P.emit("s", lambda e: e.activation(
                    sq_sb[0:1, 0:8], sq_sb[0:1, 0:8], AF.Square),
                    inc=[("lnv", 1)])
                P.emit("v", lambda e: e.reciprocal(vv[:, :], vv[:, :]),
                       wait=["lnv"], inc=())
                P.emit("v", lambda e: e.memset(sq_sb[0:1, 0:8], 0.0),
                       inc=[("rstd", 1)])
                P.flush("phD1a")
            with ExitStack() as c1b:
                rep_ps = c1b.enter_context(nc.psum_tensor("D_rep_ps", [128, 512], F32))
                rep_ps2 = c1b.enter_context(nc.psum_tensor("D_rep_ps2", [128, 512], F32))
                emit_rep(P, rep_ps, rep_ps2, mv, vv, m_rep, r_rep, ones_f32r, "D")
                P.emit("t", lambda e: e.matmul(
                    rep_ps[:, 0:NE], ones_f32r[0:1, :], br_sb[0:1, :],
                    start=True, stop=True),
                    wait=["ldm", "mrv"], inc=[("brt", 1)])
                P.emit("v", lambda e: e.tensor_copy(brb[:, :], rep_ps[:, 0:NE]),
                       wait=["brt"], inc=[("brv2", 1)])
                P.flush("phD1b")

        with ExitStack() as c2:
            C = c2.enter_context
            x4c = [C(nc.sbuf_tensor(f"D_x4c{t}", [128, 512], F32)) for t in range(DT)]
            xln = [C(nc.sbuf_tensor(f"D_xln{t}", [128, 512], BF16)) for t in range(DT)]
            xlf = [C(nc.sbuf_tensor(f"D_xlf{t}", [128, 512], F32)) for t in range(DT)]
            scr = C(nc.sbuf_tensor("D_scr", [128, 512], F32))
            hT = [C(nc.sbuf_tensor(f"D_hT{i}", [128, 512], BF16)) for i in range(DH2T)]
            rl = [C(nc.sbuf_tensor(f"D_rl{i}", [128, 512], BF16)) for i in range(2)]
            w1sb = C(nc.sbuf_tensor("D_w1sb", [128, DH2T * D], BF16))
            w2sb = C(nc.sbuf_tensor("D_w2sb", [128, DT * DH2], BF16))
            y_acc = [C(nc.sbuf_tensor(f"D_yacc{t}", [128, 512], F32)) for t in range(DT)]
            mrep = [C(nc.sbuf_tensor(f"D_mrep{i}", [128, 512], BF16)) for i in range(2)]
            maskpT = C(nc.sbuf_tensor("D_maskpT", [NE, 512], BF16))
            lgt = C(nc.sbuf_tensor("D_lgt", [128, NE], F32))
            mrow = C(nc.sbuf_tensor("D_mrow", [128, 1], F32))
            nmrow = C(nc.sbuf_tensor("D_nmrow", [128, 1], F32))
            msk = C(nc.sbuf_tensor("D_msk", [128, NE], F32))
            exr = C(nc.sbuf_tensor("D_exr", [128, NE], F32))
            smr = C(nc.sbuf_tensor("D_smr", [128, 1], F32))
            mpr = C(nc.sbuf_tensor("D_mpr", [128, 1], F32))
            mkb = C(nc.sbuf_tensor("D_mkb", [128, NE], BF16))
            stl_v = C(nc.sbuf_tensor("D_stl_v", [128, 8], F32))
            stl_s = C(nc.sbuf_tensor("D_stl_s", [128, 8], F32))
            outst = [C(nc.sbuf_tensor(f"D_outst{i}", [128, 512], F32)) for i in range(2)]
            pr = C(nc.psum_tensor("D_pr", [128, NE], F32))
            tp8 = C(nc.psum_tensor("D_tp8", [NE, 128], BF16))
            pm = C(nc.psum_tensor("D_pm", [128, 512], F32))
            ph = [C(nc.psum_tensor(f"D_ph{i}", [128, 512], F32)) for i in range(2)]
            py = [C(nc.psum_tensor(f"D_py{i}", [128, 512], F32)) for i in range(2)]

            P.gate(["s", "v", "g", "t"], ["ldm", "rstd", "brv2"])
            if debug:
                P.emit("g", lambda e: e.dma_start(dbg_st[0:1, :], m_rep[0:1, :]),
                       wait=["mrv"], inc=[("dbg1", 16)])
                P.emit("g", lambda e: e.dma_start(dbg_st[1:2, :], r_rep[0:1, :]),
                       wait=["rrv"], inc=[("dbg1", 16)])
                P.emit("g", lambda e: e.dma_start(dbg_st[2:3, :], m_rep[64:65, :]),
                       wait=["mrv"], inc=[("dbg1", 16)])
                P.emit("g", lambda e: e.dma_start(dbg_st[3:4, :], r_rep[64:65, :]),
                       wait=["rrv"], inc=[("dbg1", 16)])

            for ch in range(4):
                cs = slice(ch * 512, (ch + 1) * 512)
                for t in range(DT):
                    P.emit("y", lambda e, t=t, cs=cs: e.dma_start(
                        x4c[t][:, :], xin[t * 128:(t + 1) * 128, cs]),
                        wait=[P.lag("dov", 0)], inc=[("lxc", 16)])
                    P.emit("v", lambda e, t=t, cs=cs: e.tensor_sub(
                        scr[:, :], x4c[t][:, :], m_rep[:, cs]),
                        wait=[("lxc", P.val("lxc")), "mrv", "rrv",
                              ("dxf", P.val("dxf")), ("dhm", P.val("dhm")),
                              ("drm", P.val("drm")), ("dbg1", P.val("dbg1"))],
                        inc=[("dxs", 1)])
                    P.emit("v", lambda e, t=t, cs=cs: e.tensor_mul(
                        xlf[t][:, :], scr[:, :], r_rep[:, cs]),
                        wait=[("dxs", P.val("dxs"))], inc=[("dxf", 1)])
                    P.emit("v", lambda e, t=t: e.tensor_copy(
                        xln[t][:, :], xlf[t][:, :]),
                        wait=[("dxf", P.val("dxf")), ("dxl", P.val("dxl"))],
                        inc=[("dxl", 1)])
                for tt in range(4):
                    ts_ = slice(tt * 128, (tt + 1) * 128)
                    for ct in range(DT):
                        P.emit("t", lambda e, ct=ct, ts_=ts_: e.matmul(
                            pr[:, :], xlf[ct][:, ts_], wr_sb[:, ct * NE:(ct + 1) * NE],
                            start=(ct == 0), stop=(ct == DT - 1)),
                            wait=[("dxf", P.val("dxf")), ("dlg", P.val("dlg"))],
                            inc=[("drm", 1)] if ct == DT - 1 else ())
                    P.emit("v", lambda e: e.tensor_add(lgt[:, :], brb[:, :], pr[:, :]),
                           wait=[("drm", P.val("drm")), ("dtse", P.val("dtse")),
                                 ("dbg", P.val("dbg")), ("dbg0", P.val("dbg0"))],
                           inc=[("dlg", 1)])
                    P.emit("v", lambda e: e.reduce_max(
                        mrow[:, :], lgt[:, :], axis=mybir.AxisListType.X),
                        wait=[("dlg", P.val("dlg"))], inc=[("dmx", 1)])
                    P.emit("v", lambda e: e.tensor_scalar(
                        nmrow[:, :], mrow[:, :], -1.0, None, ALU.mult),
                        wait=[("dmx", P.val("dmx"))], inc=())
                    P.emit("v", lambda e: e.memset(stl_v[:, :], 0.0),
                           wait=(), inc=[("dnm", 1)])
                    P.emit("g", lambda e: e.tensor_scalar(
                        msk[:, :], lgt[:, :], mrow[:, 0:1], None, ALU.is_ge),
                        wait=[("dnm", P.val("dnm"))], inc=[("dms", 1)])
                    P.emit("s", lambda e: e.activation(
                        exr[:, :], lgt[:, :], AF.Exp, bias=nmrow[:, 0:1]),
                        wait=[("dms", P.val("dms"))], inc=())
                    P.emit("s", lambda e: e.activation(
                        stl_s[:, :], exr[:, :], AF.Square),
                        wait=(), inc=[("dex", 1)])
                    P.emit("v", lambda e: e.reduce_sum(
                        smr[:, :], exr[:, :], axis=mybir.AxisListType.X),
                        wait=[("dex", P.val("dex"))], inc=[("dsm", 1)])
                    P.emit("v", lambda e: e.reciprocal(mpr[:, :], smr[:, :]),
                           wait=[("dsm", P.val("dsm"))], inc=())
                    P.emit("v", lambda e: e.memset(stl_v[:, :], 0.0),
                           wait=(), inc=[("dmp", 1)])
                    P.emit("g", lambda e: e.tensor_scalar(
                        mkb[:, :], msk[:, :], mpr[:, 0:1], None, ALU.mult),
                        wait=[("dmp", P.val("dmp"))], inc=[("dmk", 1)])
                    if debug and ch == 0 and tt == 0:
                        P.emit("g", lambda e: e.dma_start(dbg_r0[:, 0:NE], lgt[:, :]),
                               wait=[("dlg", P.val("dlg"))], inc=[("dbg0", 16)])
                        P.emit("g", lambda e: e.dma_start(dbg_r0[:, NE:NE + 1], mrow[:, :], allow_slow_non_contiguous=True),
                               wait=[("dnm", P.val("dnm"))], inc=[("dbg0", 16)])
                        P.emit("g", lambda e: e.dma_start(dbg_r0[:, NE + 1:2 * NE + 1], msk[:, :]),
                               wait=[("dms", P.val("dms"))], inc=[("dbg0", 16)])
                        P.emit("g", lambda e: e.dma_start(dbg_r0[:, 2 * NE + 1:3 * NE + 1], exr[:, :]),
                               wait=[("dex", P.val("dex"))], inc=[("dbg0", 16)])
                        P.emit("g", lambda e: e.dma_start(dbg_r0[:, 3 * NE + 1:3 * NE + 2], smr[:, :], allow_slow_non_contiguous=True),
                               wait=[("dsm", P.val("dsm"))], inc=[("dbg0", 16)])
                        P.emit("g", lambda e: e.dma_start(dbg_r0[:, 3 * NE + 2:3 * NE + 3], mpr[:, :], allow_slow_non_contiguous=True),
                               wait=[("dmp", P.val("dmp"))], inc=[("dbg0", 16)])
                        P.emit("g", lambda e: e.dma_start(dbg_r0[:, 3 * NE + 3:4 * NE + 3], mkb[:, :]),
                               wait=[("dmk", P.val("dmk"))], inc=[("dbg0", 16)])
                    if debug and ch == 0:
                        pass
                    if debug and ch == 3 and tt == 0:
                        P.emit("g", lambda e: e.dma_start(dbg_lg, lgt[:, :]),
                               wait=[("dlg", P.val("dlg"))], inc=[("dbg", 16)])
                        P.emit("g", lambda e: e.dma_start(dbg_mx, mrow[:, :]),
                               wait=[("dmx", P.val("dmx"))], inc=[("dbg", 16)])
                        P.emit("g", lambda e: e.dma_start(dbg_ms, msk[:, :]),
                               wait=[("dms", P.val("dms"))], inc=[("dbg", 16)])
                        P.emit("g", lambda e: e.dma_start(dbg_mb, mkb[:, :]),
                               wait=[("dmk", P.val("dmk"))], inc=[("dbg", 16)])
                    P.emit("t", lambda e: e.transpose(
                        tp8[:, :], mkb[:, 0:NE], id_sb[:, :]),
                        wait=[("dmk", P.val("dmk")), ("dtse", P.val("dtse"))],
                        inc=[("dtp", 1)])
                    P.emit("v", lambda e, ts_=ts_: e.tensor_copy(
                        maskpT[:, ts_], tp8[:, :]),
                        wait=[("dtp", P.val("dtp")), ("dmr", P.val("dmr")),
                              ("db2", P.val("db2")), ("dbgm", P.val("dbgm"))],
                        inc=())
                    P.emit("v", lambda e: e.memset(stl_v[0:1, 0:8], 0.0),
                        inc=[("dtse", 1)])
                for do in range(DT):
                    P.emit("t", lambda e, do=do: e.matmul(
                        py[do % 2][:, :], b2_sb[:, do * 128:(do + 1) * 128],
                        maskpT[:, :], start=True, stop=True),
                        wait=[("dtse", P.val("dtse")), P.lag("dyv", 1)],
                        inc=[("db2", 1)])
                    P.emit("v", lambda e, do=do: e.tensor_copy(
                        y_acc[do][:, :], py[do % 2][:, :]),
                        wait=[("db2", P.val("db2")), ("dov", P.val("dov"))],
                        inc=[("dyv", 1)])
                if debug:
                    P.emit("g", lambda e, ch=ch: e.dma_start(
                        dbg_mk4[ch * NE:(ch + 1) * NE, :], maskpT[:, :]),
                        wait=[("dtse", P.val("dtse"))], inc=[("dbgm", 16)])
                if debug and ch == 0:
                    for t in range(DT):
                        P.emit("g", lambda e, t=t: e.dma_start(
                            dbg_x0[t * 128:(t + 1) * 128, :], xlf[t][:, :]),
                            wait=[("dxf", P.val("dxf"))], inc=[("dbg1", 16)])
                if debug and ch == 3:
                    P.emit("g", lambda e: e.dma_start(dbg_mk, maskpT[:, :]),
                           wait=[("dtse", P.val("dtse"))], inc=[("dbg", 16)])
                    for t in range(DT):
                        P.emit("g", lambda e, t=t: e.dma_start(
                            dbg_xl[t * 128:(t + 1) * 128, :], xlf[t][:, :]),
                            wait=[("dxf", P.val("dxf"))], inc=[("dbg", 16)])
                    for do in range(DT):
                        P.emit("g", lambda e, do=do: e.dma_start(
                            dbg_y0[do * 128:(do + 1) * 128, :], y_acc[do][:, :]),
                            wait=[("dyv", P.val("dyv"))], inc=[("dbg", 16)])
                for ex in range(NE):
                    for part in range(4):
                        P.emit("g", lambda e, ex=ex, part=part: e.dma_start(
                            w1sb[:, part * 4096:(part + 1) * 4096],
                            w1e[ex * 128:(ex + 1) * 128,
                                part * 4096:(part + 1) * 4096]),
                            wait=[("dhm", P.val("dhm"))], inc=[("dw1", 16)])
                    for part in range(4):
                        P.emit("g", lambda e, ex=ex, part=part: e.dma_start(
                            w2sb[:, part * 4096:(part + 1) * 4096],
                            w2e[ex * 128:(ex + 1) * 128,
                                part * 4096:(part + 1) * 4096]),
                            wait=[("dym", P.val("dym"))], inc=[("dw2", 16)])
                    P.emit("t", lambda e, ex=ex: e.matmul(
                        pm[:, :], e8_sb[:, ex * 128:(ex + 1) * 128], maskpT[:, :],
                        start=True, stop=True),
                        wait=[("dtse", P.val("dtse")), P.lag("dmrv", 0)],
                        inc=[("dmr", 1)] + ([] if ex < NE - 1 else []))
                    P.emit("v", lambda e, sl=ex % 2: e.tensor_copy(
                        mrep[sl][:, :], pm[:, :]),
                        wait=[("dmr", P.val("dmr")), P.lag("dhtv", DH2T)],
                        inc=[("dmrv", 1)])
                    for hb in range(DH2T):
                        for t in range(DT):
                            P.emit("t", lambda e, hb=hb, t=t: e.matmul(
                                ph[hb % 2][:, :],
                                w1sb[:, hb * D + t * 128:hb * D + (t + 1) * 128],
                                xln[t][:, :], start=(t == 0), stop=(t == DT - 1)),
                                wait=[("dw1", P.val("dw1")), P.lag("dhrl", 1),
                                      ("dxl", P.val("dxl"))],
                                inc=[("dhm", 1)] if t == DT - 1 else ())
                        P.emit("s", lambda e, ex=ex, hb=hb, sl=hb % 2: e.activation(
                            rl[sl][:, :], ph[hb % 2][:, :], AF.Relu,
                            bias=b1_sb[:, ex * DH2T + hb:ex * DH2T + hb + 1]),
                            wait=[("dhm", P.val("dhm")), P.lag("dhtv", 1)],
                            inc=[("dhrl", 1)])
                        P.emit("v", lambda e, hb=hb, sl=hb % 2, msl=ex % 2: e.tensor_mul(
                            hT[hb][:, :], rl[sl][:, :], mrep[msl][:, :]),
                            wait=[("dhrl", P.val("dhrl")), ("dmrv", P.val("dmrv")),
                                  ("dym", P.val("dym"))],
                            inc=[("dhtv", 1)])
                    if debug and ch == 3 and ex == 7:
                        for hb in range(DH2T):
                            P.emit("g", lambda e, hb=hb: e.dma_start(
                                dbg_h[hb * 128:(hb + 1) * 128, :], hT[hb][:, :]),
                                wait=[("dhtv", P.val("dhtv"))], inc=[("dbg", 16)])
                        P.emit("g", lambda e: e.dma_start(dbg_mr, mrep[0][:, :]),
                               wait=[("dmrv", P.val("dmrv"))], inc=[("dbg", 16)])
                    for do in range(DT):
                        for hb in range(DH2T):
                            P.emit("t", lambda e, do=do, hb=hb: e.matmul(
                                py[do % 2][:, :],
                                w2sb[:, do * DH2 + hb * 128:do * DH2 + (hb + 1) * 128],
                                hT[hb][:, :], start=(hb == 0), stop=(hb == DH2T - 1)),
                                wait=[("dw2", P.val("dw2")), ("dhtv", P.val("dhtv")),
                                      P.lag("dyv", 1)],
                                inc=[("dym", 1)] if hb == DH2T - 1 else ())
                        P.emit("v", lambda e, do=do: e.tensor_add(
                            y_acc[do][:, :], y_acc[do][:, :], py[do % 2][:, :]),
                            wait=[("dym", P.val("dym")), ("dbg", P.val("dbg"))],
                            inc=[("dyv", 1)])
                if debug:
                    P.emit("g", lambda e, ch=ch: e.dma_start(
                        dbg_mk5[ch * NE:(ch + 1) * NE, :], maskpT[:, :]),
                        wait=[("dyv", P.val("dyv"))], inc=[("dbgm", 16)])
                if debug and ch == 3:
                    for do in range(DT):
                        P.emit("g", lambda e, do=do: e.dma_start(
                            dbg_ya[do * 128:(do + 1) * 128, :], y_acc[do][:, :]),
                            wait=[("dyv", P.val("dyv"))], inc=[("dbg", 16)])
                for do in range(DT):
                    P.emit("v", lambda e, do=do, sl=do % 2: e.tensor_add(
                        outst[sl][:, :], y_acc[do][:, :], x4c[do][:, :]),
                        wait=[("dyv", P.val("dyv")), ("lxc", P.val("lxc")),
                              P.lag("dst", 16)],
                        inc=[("dov", 1)])
                    P.emit("y", lambda e, do=do, cs=cs, sl=do % 2: e.dma_start(
                        xoT[do * 128:(do + 1) * 128, cs], outst[sl][:, :]),
                        wait=[("dov", P.val("dov"))], inc=[("dst", 16)])
            P.emit("y", None, wait=[("dst", P.val("dst"))])
            if debug:
                P.emit("g", None, wait=[("dbg", P.val("dbg"))])
            P.flush("phD2")


def prep_l1_ffn(ins):
    w = {}
    w["wr"] = _c(ins["L1_ffn_Wr"])
    w["brv"] = _c(ins["L1_ffn_br"].reshape(1, NE))
    w["w1e"] = _c(np.concatenate([host_pwide(host_blocks(_c(ins["L1_ffn_W1"][e])))
                                  for e in range(NE)], axis=0))
    w["w2e"] = _c(np.concatenate([host_pwide(host_blocks(_c(ins["L1_ffn_W2"][e])))
                                  for e in range(NE)], axis=0))
    w["b1v"] = _c(np.concatenate([host_vec(ins["L1_ffn_b1"][e]) for e in range(NE)],
                                 axis=1))
    w["b2v"] = _c(ins["L1_ffn_b2"])
    return w



def prep_l0_attn(ins):
    w = {}
    w["wq"] = host_blocks(_c(ins["L0_attn_Wq"].T))
    w["wk"] = host_blocks(_c(ins["L0_attn_Wk"].T))
    w["wv"] = host_blocks(_c(ins["L0_attn_Wv"].T))
    w["wo"] = host_blocks(_c(ins["L0_attn_Wo"].T))
    w["bqv"] = host_vec(ins["L0_attn_bq"])
    w["bkv"] = host_vec(ins["L0_attn_bk"])
    w["bvv"] = host_vec(ins["L0_attn_bv"])
    w["bov"] = host_vec(ins["L0_attn_bo"])
    w["ri"] = _c(ins["L0_attn_rel_W"][:, :D].T)
    w["rj"] = _c(ins["L0_attn_rel_W"][:, D:].T)
    w["relb"] = _c(ins["L0_attn_rel_b"].reshape(3, 1))
    w["lbT"] = _c(ins["L0_attn_logical_bias"].T)
    w["gw"] = _c(ins["L0_attn_gate_W"].T)
    w["gbv"] = _c(ins["L0_attn_gate_b"].reshape(H, 1))
    return w


def build_fused(debug=False, scr_debug=False):
    nc = bass.Bass("TRN2", target_bir_lowering=False)
    xT = nc.dram_tensor("xT", [D, S], F32, kind="ExternalInput").ap()
    xoT = nc.dram_tensor("xoT", [D, S], F32, kind="ExternalOutput").ap()
    scrkind = "ExternalOutput" if scr_debug else "Internal"
    x2scr = nc.dram_tensor("x2scr", [D, S], F32, kind=scrkind).ap()
    x3scr = nc.dram_tensor("x3scr", [D, S], F32, kind=scrkind).ap()
    x4scr = nc.dram_tensor("x4scr", [D, S], F32, kind=scrkind).ap()
    with ExitStack() as ctx:
        P = Plan(nc, ctx)
        emit_A(nc, P, ctx, xT, x2scr, debug=debug)
        emit_B(nc, P, ctx, x2scr, x3scr, debug=debug)
        emit_C(nc, P, ctx, x3scr, x4scr, debug=debug)
        emit_D(nc, P, ctx, x4scr, xoT, debug=debug)
    return nc


def kernel(**inputs):
    ins = {k: np.asarray(v, dtype=np.float32) for k, v in inputs.items()}
    x = ins["L0_attn_hidden_states"]
    w = {}
    for pfx, prep in (("A", prep_l0_attn), ("B", prep_l0_ffn),
                      ("C", prep_l1_attn), ("D", prep_l1_ffn)):
        for k, v in prep(ins).items():
            w[f"{pfx}_{k}"] = v
    nc = build_fused()
    maps = [dict(w, xT=_c(x[c].T)) for c in range(NCORE)]
    res = run_bass_kernel_spmd(nc, maps, core_ids=list(range(NCORE)))
    out = np.stack([res.results[c]["xoT"].T for c in range(NCORE)])
    return np.ascontiguousarray(out.astype(np.float32))
